# version 01
# baseline (speedup 1.0000x reference)
import numpy as np
import ml_dtypes
from contextlib import ExitStack

import concourse.bass as bass
import concourse.mybir as mybir
from concourse.bass_utils import run_bass_kernel_spmd

F32 = mybir.dt.float32
BF16 = mybir.dt.bfloat16
AX = mybir.AxisListType.X
AF = mybir.ActivationFunctionType
OP = mybir.AluOpType
BF16NP = ml_dtypes.bfloat16

B, T, D = 4, 4096, 4096
NCORES = 8
NH, NKV, HD = 32, 8, 128
KVD = NKV * HD
NT = T // 128
NQT = 16
RANK = 32
NLORA = 16
EPS = 1e-5
NEGBIG = -30000.0
ENGINES = ("tensor", "vector", "scalar", "sync", "gpsimd")


class Ctr:
    def __init__(self, sem, name):
        self.sem = sem
        self.name = name
        self.v = 0


class Prog:

    def __init__(self):
        self.ops = []

    def op(self, engine, fn, waits=(), inc=None, k=1):
        w = [(c, v) for (c, v) in waits if c is not None and v and v > 0]
        self.ops.append((engine, fn, w, inc, k))
        if inc is not None:
            inc.v += k
            return inc.v
        return None

    def emit(self, engine_name, eng):
        last_wait = {}
        for e, fn, waits, inc, k in self.ops:
            if e != engine_name:
                continue
            for ctr, val in waits:
                if last_wait.get(id(ctr), 0) < val:
                    eng.wait_ge(ctr.sem, val)
                    last_wait[id(ctr)] = val
            if fn is None:
                continue
            ins = fn(eng)
            if inc is not None:
                ins.then_inc(inc.sem, k)


def slot_rounds(j):
    L = 256 * (j + 1)
    rounds = []
    off = 0
    while off < L:
        w = min(1024, L - off)
        chunks = []
        o = off
        while o < off + w:
            cw = min(512, off + w - o)
            chunks.append((o, cw))
            o += cw
        rounds.append((off, w, chunks))
        off += w
    return L, rounds


def build_program(phases=("P0", "P1", "P2", "P3", "P5"), debug=False):
    nc = bass.Bass("TRN2", target_bir_lowering=False)

    x = nc.dram_tensor("x", [T, D], F32, kind="ExternalInput").ap()
    x_own = nc.dram_tensor("x_own", [NQT * 128, D], F32, kind="ExternalInput").ap()
    wq = nc.dram_tensor("wq", [D, D], F32, kind="ExternalInput").ap()
    wk = nc.dram_tensor("wk", [D, KVD], F32, kind="ExternalInput").ap()
    wv = nc.dram_tensor("wv", [D, KVD], F32, kind="ExternalInput").ap()
    wo = nc.dram_tensor("wo", [D, D], F32, kind="ExternalInput").ap()
    wkern = nc.dram_tensor("wkern", [D, D], F32, kind="ExternalInput").ap()
    bias_fold = nc.dram_tensor("bias_fold", [128, 32], F32, kind="ExternalInput").ap()
    gate2 = nc.dram_tensor("gate2", [1, 2 * NLORA], F32, kind="ExternalInput").ap()
    loraA_dev = nc.dram_tensor("loraA_dev", [NLORA, 128, 32 * RANK], F32, kind="ExternalInput").ap()
    As_dev = nc.dram_tensor("As_dev", [128, 32 * RANK], F32, kind="ExternalInput").ap()
    loraB_dev = nc.dram_tensor("loraB_dev", [NLORA, 128, 32 * RANK], F32, kind="ExternalInput").ap()
    Bs_fold = nc.dram_tensor("Bs_fold", [128, 32 * RANK], F32, kind="ExternalInput").ap()
    ident_in = nc.dram_tensor("ident_in", [128, 128], BF16, kind="ExternalInput").ap()
    mask_in = nc.dram_tensor("mask_in", [128, 256], BF16, kind="ExternalInput").ap()
    ccq = nc.dram_tensor("ccq", [NQT * 128, 128], F32, kind="ExternalInput")
    ssq = nc.dram_tensor("ssq", [NQT * 128, 128], F32, kind="ExternalInput")
    cck = nc.dram_tensor("cck", [T, 128], F32, kind="ExternalInput")
    ssk = nc.dram_tensor("ssk", [T, 128], F32, kind="ExternalInput")

    sk = "ExternalOutput" if debug else "Internal"
    xn_s = nc.dram_tensor("xn_s", [T, D], BF16, kind=sk).ap()
    vn_s = nc.dram_tensor("vn_s", [T, KVD], BF16, kind=sk).ap()
    qT_s = nc.dram_tensor("qT_s", [NQT, 8, 128, 512], BF16, kind=sk).ap()
    g_s = nc.dram_tensor("g_s", [NQT * 128, D], F32, kind=sk).ap()
    ctxT_s = nc.dram_tensor("ctxT_s", [NQT, 8, 128, 512], BF16, kind=sk).ap()
    kT_dbg = nc.dram_tensor("kT_dbg", [128, NKV * T], BF16,
                            kind="ExternalOutput" if debug else "Internal").ap()
    astk_s = nc.dram_tensor("astk_s", [2, 128, 2048], BF16, kind=sk).ap()
    axt_dbg = nc.dram_tensor("axt_dbg", [2, 65, 1024], BF16,
                             kind="ExternalOutput" if debug else "Internal").ap()
    bsb_dbg = nc.dram_tensor("bsb_dbg", [2, 128, D], BF16,
                             kind="ExternalOutput" if debug else "Internal").ap()
    wbf_dbg = nc.dram_tensor("wbf_dbg", [2, 2, 128, 32 * 512], BF16,
                             kind="ExternalOutput" if debug else "Internal").ap()
    xnt_dbg = nc.dram_tensor("xnt_dbg", [2, 128, 32 * 1024], BF16,
                             kind="ExternalOutput" if debug else "Internal").ap()
    bstk_s_h = nc.dram_tensor("bstk_s", [2 * 65 * D], BF16, kind=sk)
    bstk_s = bstk_s_h.ap()
    y = nc.dram_tensor("y", [NQT * 128, D], F32, kind="ExternalOutput").ap()

    es = ExitStack()
    sem_handles = []

    def SEM(name):
        h = es.enter_context(nc.semaphore(name))
        sem_handles.append(h)
        return h

    P = Prog()
    cW = Ctr(SEM("sW"), "W")
    cWb = Ctr(SEM("sWb"), "Wb")
    cW2 = Ctr(SEM("sW2"), "W2")
    cX2 = Ctr(SEM("sX2"), "X2")
    cXb = Ctr(SEM("sXb"), "Xb")
    cQa = Ctr(SEM("sQa"), "Qa")
    cQb = Ctr(SEM("sQb"), "Qb")
    cY2 = Ctr(SEM("sY2"), "Y2")
    cYb = Ctr(SEM("sYb"), "Yb")
    cZb = Ctr(SEM("sZb"), "Zb")
    cX = Ctr(SEM("sX"), "X")
    cY = Ctr(SEM("sY"), "Y")
    cT = Ctr(SEM("sT"), "T")
    cT2 = Ctr(SEM("sT2"), "T2")
    cV = Ctr(SEM("sV"), "V")
    cV2 = Ctr(SEM("sV2"), "V2")
    cS = Ctr(SEM("sS"), "S")
    cS2 = Ctr(SEM("sS2"), "S2")
    cB = Ctr(SEM("sB"), "B")

    ident = es.enter_context(nc.sbuf_tensor("identity", [128, 128], BF16))
    ones_bf = es.enter_context(nc.sbuf_tensor("ones_bf", [1, 128], BF16))
    eps_sb = es.enter_context(nc.sbuf_tensor("eps_sb", [128, 1], F32))

    def barrier():
        P.op("sync", None, waits=[(cW, cW.v), (cWb, cWb.v), (cW2, cW2.v),
                                  (cY, cY.v), (cYb, cYb.v), (cY2, cY2.v),
                                  (cZb, cZb.v)])
        P.op("gpsimd", None, waits=[(cX, cX.v), (cXb, cXb.v), (cX2, cX2.v),
                                    (cQa, cQa.v), (cQb, cQb.v)])
        for name in ENGINES:
            P.op(name, lambda eng: eng.nop(), inc=cB)
        tgt = cB.v
        for name in ENGINES:
            P.op(name, None, waits=[(cB, tgt)])

    ev_ident = P.op("sync", lambda e: e.dma_start(ident[:, :], ident_in), inc=cW2, k=16)
    P.op("vector", lambda e: e.memset(eps_sb[:, :], EPS), inc=cV)
    ev_ones = P.op("vector", lambda e: e.memset(ones_bf[:, :], 1.0), inc=cV)

    def make_ln_bufs(ph_sb, tag):
        return dict(
            stats=[ph_sb(f"{tag}st{i}", [128, 48], F32) for i in range(2)],
            mv=[ph_sb(f"{tag}mv{i}", [128, 2], F32) for i in range(2)],
            lnv=[ph_sb(f"{tag}ln{i}", [128, 1], F32) for i in range(2)],
            rstd=[ph_sb(f"{tag}rs{i}", [128, 1], F32) for i in range(2)],
            nms=[ph_sb(f"{tag}nm{i}", [128, 1], F32) for i in range(2)],
        )

    def emit_ln(lb, par, x_buf, xn_buf, w_first, w_norm):
        evb = None
        for r in range(8):
            w = list(w_first) if r == 0 else []
            evb = P.op("vector", lambda e, r=r: e.bn_stats(
                lb["stats"][par][:, r * 6:(r + 1) * 6],
                x_buf[:, r * 512:(r + 1) * 512]), waits=w, inc=cV)
        ev_ag = P.op("vector", lambda e: e.bn_aggr(
            lb["mv"][par][:, :], lb["stats"][par][:, :]), waits=[(cV, evb)], inc=cV)
        ev_ln = P.op("scalar", lambda e: e.activation(
            lb["lnv"][par][:, :], lb["mv"][par][:, 1:2], AF.Ln, bias=eps_sb[:, :1],
            scale=1.0), waits=[(cV, ev_ag)], inc=cS2)
        ev_rs = P.op("scalar", lambda e: e.activation(
            lb["rstd"][par][:, :], lb["lnv"][par][:, :], AF.Exp, scale=-0.5),
            waits=[(cS2, ev_ln)], inc=cS2)
        ev_nm = P.op("vector", lambda e: e.scalar_tensor_tensor(
            lb["nms"][par][:, :], lb["mv"][par][:, :1], -1.0, lb["rstd"][par][:, :1],
            op0=OP.mult, op1=OP.mult), waits=[(cS2, ev_rs)], inc=cV)
        return P.op("vector", lambda e: e.tensor_scalar(
            xn_buf[:, :], x_buf[:, :], lb["rstd"][par][:, :1], lb["nms"][par][:, :1],
            op0=OP.mult, op1=OP.add), waits=[(cV, ev_nm)] + list(w_norm), inc=cV)

    if "P0" in phases:
        with ExitStack() as ph:
            sb = lambda n, s, d: ph.enter_context(nc.sbuf_tensor(n, s, d))
            ps = lambda n, s: ph.enter_context(nc.psum_tensor(n, s, F32))
            wv_bf = sb("wv_bf", [128, 32 * KVD], BF16)
            wst = [sb(f"p0w{i}", [128, KVD], F32) for i in range(2)]
            x_f32 = [sb(f"p0x{i}", [128, D], F32) for i in range(2)]
            xn_nat = [sb(f"p0n{i}", [128, D], BF16) for i in range(2)]
            xnT_t = [sb(f"p0t{i}", [128, D], BF16) for i in range(2)]
            v_bf = [sb(f"p0v{i}", [128, KVD], BF16) for i in range(2)]
            lb = make_ln_bufs(sb, "p0")
            ps_t = [ps(f"p0pt{i}", [128, 512]) for i in range(2)]
            ps_v = [ps(f"p0pv{i}", [128, KVD]) for i in range(2)]

            ev_wvc = [0] * 32
            for wt in range(32):
                wp = cW if wt % 2 == 0 else cWb
                w = [(cS, ev_wvc[wt - 2])] if wt >= 2 else []
                ev = P.op("sync", lambda e, wt=wt: e.dma_start(
                    wst[wt % 2][:, :], wv[wt * 128:(wt + 1) * 128, :]),
                    waits=w, inc=wp, k=16)
                ev_wvc[wt] = P.op("scalar", lambda e, wt=wt: e.copy(
                    wv_bf[:, wt * KVD:(wt + 1) * KVD], wst[wt % 2][:, :]),
                    waits=[(wp, ev)], inc=cS)

            ev_norm = [0] * NT
            ev_xnw = [0] * NT
            ev_tr = [[0] * 8 for _ in range(NT)]
            ev_evT = [[0] * 8 for _ in range(NT)]
            ev_vmm = [0] * NT
            ev_evV = [0] * NT
            ev_vw = [0] * NT
            for t in range(NT):
                xp = cX if t % 2 == 0 else cXb
                yp = cY if t % 2 == 0 else cYb
                w = [(cV, ev_norm[t - 2])] if t >= 2 else []
                ev_x = P.op("gpsimd", lambda e, t=t: e.dma_start(
                    x_f32[t % 2][:, :], x[t * 128:(t + 1) * 128, :]),
                    waits=w, inc=xp, k=16)
                w_norm = []
                if t >= 2:
                    w_norm = [(yp, ev_xnw[t - 2]), (cT, ev_tr[t - 2][7])]
                ev_norm[t] = emit_ln(lb, t % 2, x_f32[t % 2], xn_nat[t % 2],
                                     [(xp, ev_x)], w_norm)
                ev_xnw[t] = P.op("sync", lambda e, t=t: e.dma_start(
                    xn_s[t * 128:(t + 1) * 128, :], xn_nat[t % 2][:, :]),
                    waits=[(cV, ev_norm[t])], inc=yp, k=16)
                for g in range(8):
                    for q in range(4):
                        dt = g * 4 + q
                        w = []
                        if g == 0 and q == 0:
                            w = [(cV, ev_norm[t]), (cW2, ev_ident)]
                        if q == 0 and t * 8 + g >= 2:
                            gi = t * 8 + g - 2
                            w.append((cS2, ev_evT[gi // 8][gi % 8]))
                        r = P.op("tensor", lambda e, t=t, dt=dt, g=g, q=q: e.matmul(
                            ps_t[g % 2][:, q * 128:(q + 1) * 128],
                            xn_nat[t % 2][:, dt * 128:(dt + 1) * 128], ident[:, :],
                            start=True, stop=True),
                            waits=w, inc=cT if q == 3 else None)
                        if q == 3:
                            ev_tr[t][g] = r
                    w = [(cT, ev_tr[t][g])]
                    if g == 0 and t >= 2:
                        w.append((cT2, ev_vmm[t - 2]))
                    ev_evT[t][g] = P.op("scalar", lambda e, t=t, g=g: e.copy(
                        xnT_t[t % 2][:, g * 512:(g + 1) * 512], ps_t[g % 2][:, :]),
                        waits=w, inc=cS2)
                r = None
                for dt in range(32):
                    for hf in range(2):
                        w = [(cS2, ev_evT[t][dt // 4]), (cS, ev_wvc[dt])]
                        if dt == 0 and hf == 0 and t >= 2:
                            w.append((cS2, ev_evV[t - 2]))
                        last = (dt == 31 and hf == 1)
                        r = P.op("tensor", lambda e, t=t, dt=dt, hf=hf: e.matmul(
                            ps_v[t % 2][:, hf * 512:(hf + 1) * 512],
                            xnT_t[t % 2][:, dt * 128:(dt + 1) * 128],
                            wv_bf[:, dt * KVD + hf * 512: dt * KVD + (hf + 1) * 512],
                            start=(dt == 0), stop=(dt == 31)),
                            waits=w, inc=cT2 if last else None)
                ev_vmm[t] = r
                zp = cY2 if t % 2 == 0 else cZb
                w = [(cT2, ev_vmm[t])]
                if t >= 2:
                    w.append((zp, ev_vw[t - 2]))
                ev_evV[t] = P.op("scalar", lambda e, t=t: e.copy(
                    v_bf[t % 2][:, :], ps_v[t % 2][:, :]), waits=w, inc=cS2)
                ev_vw[t] = P.op("sync", lambda e, t=t: e.dma_start(
                    vn_s[t * 128:(t + 1) * 128, :], v_bf[t % 2][:, :]),
                    waits=[(cS2, ev_evV[t])], inc=zp, k=16)
        barrier()

    if "P1" in phases:
        with ExitStack() as ph:
            sb = lambda n, s, d: ph.enter_context(nc.sbuf_tensor(n, s, d))
            ps = lambda n, s: ph.enter_context(nc.psum_tensor(n, s, F32))
            A_stack = sb("Astk", [128, 32 * 64], BF16)
            B_stack = sb("Bstk", [128, D], BF16)
            AxT = sb("AxT", [128, 1024], BF16)
            gate_f = sb("gate_f", [1, 2 * NLORA], F32)
            gate_bf = sb("gate_bf", [1, 2 * NLORA], BF16)
            gbc = sb("gbc", [128, NLORA], F32)
            ccq_sb = sb("ccq_sb", [128, 8 * 128], F32)
            ssq_sb = sb("ssq_sb", [128, 8 * 128], F32)
            lb1 = make_ln_bufs(sb, "p1")
            wst4 = [sb(f"w4_{i}", [128, 512], F32) for i in range(2)]
            t1b = sb("t1b", [128, 512], F32)
            t2b = sb("t2b", [128, 512], F32)
            qr_nat = [sb(f"qrn{i}", [128, 512], BF16) for i in range(2)]
            qstage = [sb(f"qst{i}", [128, 512], BF16) for i in range(2)]
            gstage = [sb(f"gst{i}", [128, 512], F32) for i in range(2)]
            sge = [sb(f"sge{i}", [128, 512], F32) for i in range(2)]
            wdum = sb("wdum", [128, 8], F32)
            ps_qg = [ps(f"psqg{i}", [128, 512]) for i in range(2)]
            ps_t1 = [ps(f"pst1{i}", [128, 512]) for i in range(2)]
            ps_ax = ps("ps_ax", [64, 512])
            ps_sm = ps("ps_sm", [128, NLORA])

            with ExitStack() as tmp:
                tsb = lambda n, s, d: tmp.enter_context(nc.sbuf_tensor(n, s, d))
                acc_a = [tsb(f"acca{i}", [128, 32 * RANK], F32) for i in range(2)]
                acc_b = [tsb(f"accb{i}", [128, 32 * RANK], F32) for i in range(2)]
                At_st = [tsb(f"Atst{i}", [128, 32 * RANK], F32) for i in range(2)]
                As_st = tsb("Asst", [128, 32 * RANK], F32)
                ab_bf = tsb("ab_bf", [128, 2048], BF16)
                fb_bf = tsb("fb_bf", [128, 32 * RANK], BF16)
                bi_st = tsb("bi_st", [128, 32], F32)
                bi_bf = tsb("bi_bf", [128, 32], BF16)

                evg = P.op("sync", lambda e: e.dma_start(gate_f[:, :], gate2), inc=cW2, k=16)
                evA = P.op("sync", lambda e: e.dma_start(As_st[:, :], As_dev), inc=cW2, k=16)
                evBs = P.op("sync", lambda e: e.dma_start(At_st[0][:, :], Bs_fold),
                            waits=[], inc=cW2, k=16)
                evbi0 = P.op("sync", lambda e: e.dma_start(bi_st[:, :], bias_fold),
                             inc=cW2, k=16)
                ev_gbf = P.op("scalar", lambda e: e.copy(gate_bf[:, :], gate_f[:, :]),
                              waits=[(cW2, evbi0)], inc=cS)
                ev_bsc = P.op("scalar", lambda e: e.copy(fb_bf[:, :], At_st[0][:, :]),
                              waits=[(cW2, evbi0)], inc=cS)
                for c in range(2):
                    P.op("sync", lambda e, c=c: e.dma_start(
                        bass.AP(bstk_s_h, c * 65 * D + 32 * D, [[1024, 128], [1, 1024]]),
                        fb_bf[:, :]), waits=[(cS, ev_bsc)], inc=cW2, k=16)
                ev_bic = P.op("scalar", lambda e: e.copy(bi_bf[:, :], bi_st[:, :]),
                              waits=[(cW2, evbi0)], inc=cS)
                for c in range(2):
                    P.op("sync", lambda e, c=c: e.dma_start(
                        bass.AP(bstk_s_h, c * 65 * D + 64 * D, [[32, 128], [1, 32]]),
                        bi_bf[:, :]), waits=[(cS, ev_bic)], inc=cW2, k=16)

                at_free = [(cS, ev_bsc), None]
                ev_fb_free = cW2.v
                ev_ab_free = 0
                ev_gbc_prev = 0
                ev_lastac = 0
                for c in range(2):
                    w = [(cS, ev_gbf), (cV, ev_ones), (cW, ev_ident)]
                    if ev_gbc_prev:
                        w.append((cV, ev_gbc_prev))
                    evm = P.op("tensor", lambda e, c=c: e.matmul(
                        ps_sm[:, :], ones_bf[:1, :], gate_bf[:1, c * NLORA:(c + 1) * NLORA],
                        start=True, stop=True), waits=w, inc=cT)
                    w = [(cT, evm)]
                    if ev_lastac:
                        w.append((cV, ev_lastac))
                    ev_gbc = P.op("vector", lambda e: e.tensor_copy(
                        gbc[:, :], ps_sm[:, :]), waits=w, inc=cV)
                    ev_gbc_prev = ev_gbc
                    for mat, dev, accs in (("A", loraA_dev, acc_a),
                                           ("B", loraB_dev, acc_b)):
                        ev_dma = [0] * NLORA
                        ev_ac = [0] * NLORA
                        for n in range(NLORA):
                            xp = cX if n % 2 == 0 else cXb
                            w = [at_free[n % 2]] if at_free[n % 2] else []
                            ev_dma[n] = P.op("gpsimd", lambda e, dev=dev, n=n: e.dma_start(
                                At_st[n % 2][:, :], dev[n, :, :]), waits=w, inc=xp, k=16)
                            if n == 0:
                                ev_ac[0] = P.op("vector", lambda e, accs=accs: e.tensor_scalar(
                                    accs[1][:, :], At_st[0][:, :], gbc[:, 0:1], None,
                                    op0=OP.mult),
                                    waits=[(xp, ev_dma[0]), (cV, ev_gbc)], inc=cV)
                            else:
                                ev_ac[n] = P.op("vector", lambda e, accs=accs, n=n:
                                                e.scalar_tensor_tensor(
                                    accs[(n + 1) % 2][:, :], At_st[n % 2][:, :],
                                    gbc[:, n:n + 1], accs[n % 2][:, :],
                                    op0=OP.mult, op1=OP.add),
                                    waits=[(xp, ev_dma[n]), (cV, ev_ac[n - 1])], inc=cV)
                            at_free[n % 2] = (cV, ev_ac[n])
                            ev_lastac = ev_ac[n]
                        if mat == "A":
                            w = [(cV, ev_ac[NLORA - 1])]
                            if ev_ab_free:
                                w.append((cW2, ev_ab_free))
                            e1 = P.op("vector", lambda e: e.tensor_copy(
                                bass.AP(ab_bf, 0, [[2048, 128], [64, 32], [1, 32]]),
                                bass.AP(acc_a[0], 0, [[1024, 128], [32, 32], [1, 32]])),
                                waits=w, inc=cV)
                            e2 = P.op("vector", lambda e: e.tensor_copy(
                                bass.AP(ab_bf, 32, [[2048, 128], [64, 32], [1, 32]]),
                                bass.AP(As_st, 0, [[1024, 128], [32, 32], [1, 32]])),
                                waits=[(cW2, evA), (cV, e1)], inc=cV)
                            ev_ab_free = P.op("sync", lambda e, c=c: e.dma_start(
                                astk_s[c, :, :], ab_bf[:, :]),
                                waits=[(cV, e2)], inc=cW2, k=16)
                        else:
                            e1 = P.op("vector", lambda e: e.tensor_copy(
                                fb_bf[:, :], acc_b[0][:, :]),
                                waits=[(cV, ev_ac[NLORA - 1]), (cW2, ev_fb_free)], inc=cV)
                            ev_fb_free = P.op("sync", lambda e, c=c: e.dma_start(
                                bass.AP(bstk_s_h, c * 65 * D, [[1024, 128], [1, 1024]]),
                                fb_bf[:, :]), waits=[(cV, e1)], inc=cW2, k=16)
                ev_stk_w = cW2.v

            xnT_sb = sb("xnT_sb", [128, 32 * 1024], BF16)

            for h in range(2):
                hscope = ExitStack()
                xo_f32 = [hscope.enter_context(nc.sbuf_tensor(f"p1x{i}_{h}", [128, D], F32))
                          for i in range(2)]
                natq = [hscope.enter_context(nc.sbuf_tensor(f"natq{i}_{h}", [128, D], BF16))
                        for i in range(2)]
                ev_Ald = P.op("gpsimd", lambda e, h=h: e.dma_start(
                    A_stack[:, :], astk_s[h, :, :]), waits=[(cW2, ev_stk_w)], inc=cX2, k=16)
                ev_bz = P.op("vector", lambda e: e.memset(B_stack[64:128, :], 0.0), inc=cV)
                ev_Bld = P.op("gpsimd", lambda e, h=h: e.dma_start(
                    B_stack[0:65, :], bass.AP(bstk_s_h, h * 65 * D, [[D, 65], [1, D]])),
                    waits=[(cW2, ev_stk_w), (cV, ev_bz)], inc=cX2, k=16)
                P.op("vector", lambda e: e.memset(AxT[64:128, :], 0.0), inc=cV)
                ev_ones_ax = P.op("vector", lambda e: e.memset(AxT[64:65, :], 1.0), inc=cV)
                ev_cc = P.op("gpsimd", lambda e, h=h: e.dma_start(
                    ccq_sb[:, :], bass.AP(ccq, h * 1024 * 128,
                                          [[128, 128], [128 * 128, 8], [1, 128]])),
                    inc=cX2, k=16)
                ev_ss = P.op("gpsimd", lambda e, h=h: e.dma_start(
                    ssq_sb[:, :], bass.AP(ssq, h * 1024 * 128,
                                          [[128, 128], [128 * 128, 8], [1, 128]])),
                    inc=cX2, k=16)
                ev_tr = [[0] * 8 for _ in range(8)]
                ev_ev = [[0] * 8 for _ in range(8)]
                ev_nrm = [0] * 8
                for jl in range(8):
                    j = h * 8 + jl
                    xp = cX if jl % 2 == 0 else cXb
                    w = [(cV, ev_nrm[jl - 2])] if jl >= 2 else []
                    ev_x = P.op("gpsimd", lambda e, j=j, jl=jl: e.dma_start(
                        xo_f32[jl % 2][:, :], x_own[j * 128:(j + 1) * 128, :]),
                        waits=w, inc=xp, k=16)
                    w_norm = [(cT, ev_tr[jl - 2][7])] if jl >= 2 else []
                    ev_nrm[jl] = emit_ln(lb1, jl % 2, xo_f32[jl % 2], natq[jl % 2],
                                         [(xp, ev_x)], w_norm)
                    for g in range(8):
                        for q in range(4):
                            dt = g * 4 + q
                            w = []
                            if g == 0 and q == 0:
                                w = [(cV, ev_nrm[jl])]
                            if q == 0 and jl * 8 + g >= 2:
                                gi = jl * 8 + g - 2
                                w.append((cS2, ev_ev[gi // 8][gi % 8]))
                            r = P.op("tensor", lambda e, jl=jl, dt=dt, g=g, q=q: e.matmul(
                                ps_t1[g % 2][:, q * 128:(q + 1) * 128],
                                natq[jl % 2][:, dt * 128:(dt + 1) * 128], ident[:, :],
                                start=True, stop=True),
                                waits=w, inc=cT if q == 3 else None)
                            if q == 3:
                                ev_tr[jl][g] = r
                        ev_ev[jl][g] = P.op("scalar", lambda e, jl=jl, g=g: e.copy(
                            bass.AP(xnT_sb, (g * 4) * 1024 + jl * 128,
                                    [[32768, 128], [1024, 4], [1, 128]]),
                            ps_t1[g % 2][:, :]), waits=[(cT, ev_tr[jl][g])], inc=cS2)
                ev_xnT = ev_ev[7][7]
                hscope.close()
                wscope = ExitStack()
                wbf = [wscope.enter_context(nc.sbuf_tensor(f"wbf{i}_{h}", [128, 32 * 512], BF16))
                       for i in range(2)]

                ev_axev = [0, 0]
                for tb in range(2):
                    r = None
                    for dt in range(32):
                        w = []
                        if dt == 0:
                            w = [(cS2, ev_xnT), (cX2, ev_ss)]
                            if tb == 1:
                                w = [(cS2, ev_axev[0])]
                        r = P.op("tensor", lambda e, tb=tb, dt=dt: e.matmul(
                            ps_ax[:, :], A_stack[:, dt * 64:(dt + 1) * 64],
                            xnT_sb[:, dt * 1024 + tb * 512: dt * 1024 + (tb + 1) * 512],
                            start=(dt == 0), stop=(dt == 31)),
                            waits=w, inc=cT if dt == 31 else None)
                    ev_axev[tb] = P.op("scalar", lambda e, tb=tb: e.copy(
                        AxT[0:64, tb * 512:(tb + 1) * 512], ps_ax[:, :]),
                        waits=[(cT, r)], inc=cS2)
                if debug:
                    P.op("sync", lambda e, h=h: e.dma_start(
                        axt_dbg[h, :, :], AxT[0:65, :]),
                        waits=[(cS2, ev_axev[1]), (cV, ev_ones_ax)], inc=cY, k=16)

                ev_wc = {}
                ev_mm = {}
                ev_rope2 = {}
                ev_rope3 = {}
                ev_qtr = {}
                ev_qev = {}
                ev_qw = {}
                ev_sil = {}
                ev_sg3 = {}
                ev_gw = {}
                ev_r3_last = 0
                for wb in range(16):
                    src = wq if wb < 8 else wkern
                    cb = wb % 8
                    for dt in range(32):
                        i = wb * 32 + dt
                        wp = cW if i % 2 == 0 else cWb
                        w = [(cS, ev_wc[(i - 2) // 32, (i - 2) % 32])] if i >= 2 else []
                        evl = P.op("sync", lambda e, src=src, cb=cb, dt=dt, i=i: e.dma_start(
                            wst4[i % 2][:, :],
                            src[dt * 128:(dt + 1) * 128, cb * 512:(cb + 1) * 512]),
                            waits=w, inc=wp, k=16)
                        w = [(wp, evl)]
                        if wb >= 2 and dt == 0:
                            w.append((cT, ev_mm[wb - 2, 7]))
                        ev_wc[wb, dt] = P.op("scalar", lambda e, wb=wb, dt=dt, i=i: e.copy(
                            wbf[wb % 2][:, dt * 512:(dt + 1) * 512], wst4[i % 2][:, :]),
                            waits=w, inc=cS)
                    ev_wc[wb, 32] = P.op("scalar", lambda e, wb=wb: e.copy(
                        wdum[:, :], wst4[(wb * 32 + 31) % 2][:, 0:8]),
                        waits=[(cS, ev_wc[wb, 31])], inc=cS)
                    for jl in range(8):
                        j = h * 8 + jl
                        ui = wb * 8 + jl
                        r = None
                        for dt in range(32):
                            w = [(cS, ev_wc[wb, 32])]
                            if dt == 0:
                                if ui >= 2:
                                    pw, pj = (ui - 2) // 8, (ui - 2) % 8
                                    if pw < 8:
                                        w.append((cV, ev_rope2[pw, pj]))
                                    else:
                                        w.append((cV, ev_sil[pw, pj]))
                                if wb == 8:
                                    w += [(cS2, ev_axev[1]), (cX2, ev_ss),
                                          (cV, ev_ones_ax)]
                            last = (wb < 8 and dt == 31)
                            r = P.op("tensor", lambda e, wb=wb, jl=jl, dt=dt, last=last: e.matmul(
                                ps_qg[jl % 2][:, :],
                                xnT_sb[:, dt * 1024 + jl * 128: dt * 1024 + jl * 128 + 128],
                                wbf[wb % 2][:, dt * 512:(dt + 1) * 512],
                                start=(dt == 0), stop=last),
                                waits=w, inc=cT if last else None)
                        if wb >= 8:
                            r = P.op("tensor", lambda e, jl=jl, cb=cb: e.matmul(
                                ps_qg[jl % 2][:, :], AxT[:, jl * 128:(jl + 1) * 128],
                                B_stack[:, cb * 512:(cb + 1) * 512],
                                start=False, stop=True), inc=cT)
                        ev_mm[wb, jl] = r
                        if wb < 8:
                            w = [(cT, ev_mm[wb, jl]), (cX2, ev_ss)]
                            if ev_r3_last:
                                w.append((cV, ev_r3_last))
                            e1 = P.op("vector", lambda e, wb=wb, jl=jl: e.tensor_tensor(
                                t1b[:, :], ps_qg[jl % 2][:, :],
                                bass.AP(ccq_sb, jl * 128, [[1024, 128], [0, 4], [1, 128]]),
                                op=OP.mult), waits=w, inc=cV)
                            ev_rope2[wb, jl] = P.op("vector", lambda e, wb=wb, jl=jl: e.tensor_tensor(
                                t2b[:, :],
                                bass.AP(ps_qg[jl % 2], 64,
                                        [[512, 128], [128, 4], [-64, 2], [1, 64]]),
                                bass.AP(ssq_sb, jl * 128, [[1024, 128], [0, 4], [1, 128]]),
                                op=OP.mult), waits=[(cX2, ev_ss)], inc=cV)
                            w = [(cV, ev_rope2[wb, jl])]
                            if ui >= 2:
                                pw, pj = (ui - 2) // 8, (ui - 2) % 8
                                w.append((cT, ev_qtr[pw, pj]))
                            ev_rope3[wb, jl] = P.op("vector", lambda e, jl=jl: e.tensor_add(
                                qr_nat[jl % 2][:, :], t1b[:, :], t2b[:, :]),
                                waits=w, inc=cV)
                            ev_r3_last = ev_rope3[wb, jl]
                            r = None
                            for hh in range(4):
                                w = [(cV, ev_rope3[wb, jl])] if hh == 0 else []
                                if hh == 0 and ui >= 2:
                                    pw, pj = (ui - 2) // 8, (ui - 2) % 8
                                    w.append((cS2, ev_qev[pw, pj]))
                                elif hh == 0:
                                    w.append((cS2, ev_ev[7][6 + jl]))
                                r = P.op("tensor", lambda e, jl=jl, hh=hh: e.matmul(
                                    ps_t1[jl % 2][:, hh * 128:(hh + 1) * 128],
                                    qr_nat[jl % 2][:, hh * 128:(hh + 1) * 128], ident[:, :],
                                    start=True, stop=True),
                                    waits=w, inc=cT if hh == 3 else None)
                            ev_qtr[wb, jl] = r
                            w = [(cT, ev_qtr[wb, jl])]
                            if ui >= 2:
                                pw, pj = (ui - 2) // 8, (ui - 2) % 8
                                w.append((cY if pj % 2 == 0 else cYb, ev_qw[pw, pj]))
                            ev_qev[wb, jl] = P.op("scalar", lambda e, jl=jl: e.copy(
                                qstage[jl % 2][:, :], ps_t1[jl % 2][:, :]), waits=w, inc=cS2)
                            ev_qw[wb, jl] = P.op("sync", lambda e, j=j, wb=wb, jl=jl: e.dma_start(
                                qT_s[j, wb, :, :], qstage[jl % 2][:, :]),
                                waits=[(cS2, ev_qev[wb, jl])],
                                inc=(cY if jl % 2 == 0 else cYb), k=16)
                        else:
                            w = [(cT, ev_mm[wb, jl])]
                            if ui >= 2:
                                pw, pj = (ui - 2) // 8, (ui - 2) % 8
                                if pw >= 8:
                                    w.append((cV, ev_sg3[pw, pj]))
                            ev_se = P.op("scalar", lambda e, jl=jl: e.activation(
                                sge[jl % 2][:, :], ps_qg[jl % 2][:, :], AF.Exp,
                                scale=-1.0), waits=w, inc=cS2)
                            e1 = P.op("vector", lambda e, jl=jl: e.tensor_scalar(
                                sge[jl % 2][:, :], sge[jl % 2][:, :], 1.0, None,
                                op0=OP.add), waits=[(cS2, ev_se)], inc=cV)
                            e2 = P.op("vector", lambda e, jl=jl: e.reciprocal(
                                sge[jl % 2][:, :], sge[jl % 2][:, :]),
                                waits=[(cV, e1)], inc=cV)
                            w = [(cV, e2)]
                            if ui >= 2:
                                pw, pj = (ui - 2) // 8, (ui - 2) % 8
                                if pw >= 8:
                                    w.append((cY2 if pj % 2 == 0 else cZb, ev_gw[pw, pj]))
                            ev_sg3[wb, jl] = P.op("vector", lambda e, jl=jl: e.tensor_tensor(
                                gstage[jl % 2][:, :], ps_qg[jl % 2][:, :],
                                sge[jl % 2][:, :], op=OP.mult),
                                waits=w, inc=cV)
                            ev_sil[wb, jl] = ev_sg3[wb, jl]
                            ev_gw[wb, jl] = P.op("sync", lambda e, j=j, cb=cb, jl=jl: e.dma_start(
                                g_s[j * 128:(j + 1) * 128, cb * 512:(cb + 1) * 512],
                                gstage[jl % 2][:, :]),
                                waits=[(cV, ev_sil[wb, jl])],
                                inc=(cY2 if jl % 2 == 0 else cZb), k=16)
                if debug:
                    P.op("sync", lambda e, h=h: e.dma_start(
                        xnt_dbg[h, :, :], xnT_sb[:, :]),
                        waits=[(cT, ev_mm[15, 7])], inc=cY, k=16)
                    P.op("sync", lambda e, h=h: e.dma_start(
                        bsb_dbg[h, :, :], B_stack[:, :]),
                        waits=[(cT, ev_mm[15, 7])], inc=cY, k=16)
                    for par in range(2):
                        P.op("sync", lambda e, h=h, par=par: e.dma_start(
                            wbf_dbg[h, par, :, :], wbf[par][:, :]),
                            waits=[(cT, ev_mm[15, 7])], inc=cY, k=16)
                wscope.close()
                barrier()

    KT_res = None
    kt_open = [False]
    if "P2" in phases:
        kt_scope = ExitStack()
        kt_open[0] = True
        KT_res = kt_scope.enter_context(nc.sbuf_tensor("KT_res", [128, NKV * T], BF16))
        with ExitStack() as ph:
            sb = lambda n, s, d: ph.enter_context(nc.sbuf_tensor(n, s, d))
            ps = lambda n, s: ph.enter_context(nc.psum_tensor(n, s, F32))
            wk_bf = sb("wk_bf", [128, 32 * KVD], BF16)
            wst2 = [sb(f"p2w{i}", [128, KVD], F32) for i in range(2)]
            natk = [sb(f"p2n{i}", [128, D], BF16) for i in range(2)]
            xnT_t2 = [sb(f"p2t{i}", [128, D], BF16) for i in range(2)]
            cck_t = [sb(f"cck{i}", [128, 128], F32) for i in range(2)]
            ssk_t = [sb(f"ssk{i}", [128, 128], F32) for i in range(2)]
            kr_nat = [sb(f"p2k{i}", [128, KVD], BF16) for i in range(2)]
            kt1 = [sb(f"p2r{i}", [128, KVD], F32) for i in range(2)]
            kt2 = [sb(f"p2s{i}", [128, KVD], F32) for i in range(2)]
            ps_t2 = [ps(f"p2pt{i}", [128, 512]) for i in range(2)]
            ps_k = [ps(f"p2pk{i}", [128, KVD]) for i in range(2)]

            ev_wkc = [0] * 32
            for wt in range(32):
                wp = cW if wt % 2 == 0 else cWb
                w = [(cS, ev_wkc[wt - 2])] if wt >= 2 else []
                ev = P.op("sync", lambda e, wt=wt: e.dma_start(
                    wst2[wt % 2][:, :], wk[wt * 128:(wt + 1) * 128, :]),
                    waits=w, inc=wp, k=16)
                ev_wkc[wt] = P.op("scalar", lambda e, wt=wt: e.copy(
                    wk_bf[:, wt * KVD:(wt + 1) * KVD], wst2[wt % 2][:, :]),
                    waits=[(wp, ev)], inc=cS)

            ev_tr = [[0] * 8 for _ in range(NT)]
            ev_ev = [[0] * 8 for _ in range(NT)]
            ev_kmm = [0] * NT
            ev_rp = [0] * NT
            ev_ktr = [[0] * 2 for _ in range(NT)]
            ev_kev = [[0] * 2 for _ in range(NT)]
            for kt in range(NT):
                xp = cX if kt % 2 == 0 else cXb
                w = [(cT, ev_tr[kt - 2][7])] if kt >= 2 else []
                ev_x = P.op("gpsimd", lambda e, kt=kt: e.dma_start(
                    natk[kt % 2][:, :], xn_s[kt * 128:(kt + 1) * 128, :]),
                    waits=w, inc=xp, k=16)
                for g in range(8):
                    for q in range(4):
                        dt = g * 4 + q
                        w = [(xp, ev_x)] if (g == 0 and q == 0) else []
                        if q == 0:
                            if g >= 2:
                                w.append((cS2, ev_ev[kt][g - 2]))
                            elif kt >= 1:
                                w.append((cS2, ev_kev[kt - 1][g]))
                        r = P.op("tensor", lambda e, kt=kt, dt=dt, g=g, q=q: e.matmul(
                            ps_t2[g % 2][:, q * 128:(q + 1) * 128],
                            natk[kt % 2][:, dt * 128:(dt + 1) * 128], ident[:, :],
                            start=True, stop=True),
                            waits=w, inc=cT if q == 3 else None)
                        if q == 3:
                            ev_tr[kt][g] = r
                    ev_ev[kt][g] = P.op("scalar", lambda e, kt=kt, g=g: e.copy(
                        xnT_t2[kt % 2][:, g * 512:(g + 1) * 512], ps_t2[g % 2][:, :]),
                        waits=[(cT, ev_tr[kt][g])], inc=cS2)
                r = None
                for dt in range(32):
                    for hf in range(2):
                        w = [(cS2, ev_ev[kt][dt // 4]), (cS, ev_wkc[dt])]
                        if dt == 0 and hf == 0 and kt >= 2:
                            w.append((cV, ev_rp[kt - 2]))
                        last = (dt == 31 and hf == 1)
                        r = P.op("tensor", lambda e, kt=kt, dt=dt, hf=hf: e.matmul(
                            ps_k[kt % 2][:, hf * 512:(hf + 1) * 512],
                            xnT_t2[kt % 2][:, dt * 128:(dt + 1) * 128],
                            wk_bf[:, dt * KVD + hf * 512: dt * KVD + (hf + 1) * 512],
                            start=(dt == 0), stop=(dt == 31)),
                            waits=w, inc=cT2 if last else None)
                ev_kmm[kt] = r
                qp = cQa if kt % 2 == 0 else cQb
                wtb = [(cV, ev_rp[kt - 2])] if kt >= 2 else []
                P.op("gpsimd", lambda e, kt=kt: e.dma_start(
                    cck_t[kt % 2][:, :], bass.AP(cck, kt * 128 * 128,
                                                 [[128, 128], [1, 128]])),
                    waits=wtb, inc=qp, k=16)
                ev_tbl = P.op("gpsimd", lambda e, kt=kt: e.dma_start(
                    ssk_t[kt % 2][:, :], bass.AP(ssk, kt * 128 * 128,
                                                 [[128, 128], [1, 128]])),
                    inc=qp, k=16)
                w = [(cT2, ev_kmm[kt]), (qp, ev_tbl)]
                if kt >= 2:
                    w.append((cV, ev_rp[kt - 2]))
                e1 = P.op("vector", lambda e, kt=kt: e.tensor_tensor(
                    kt1[kt % 2][:, :], ps_k[kt % 2][:, :],
                    bass.AP(cck_t[kt % 2], 0, [[128, 128], [0, 8], [1, 128]]),
                    op=OP.mult), waits=w, inc=cV)
                e2 = P.op("vector", lambda e, kt=kt: e.tensor_tensor(
                    kt2[kt % 2][:, :],
                    bass.AP(ps_k[kt % 2], 64, [[1024, 128], [128, 8], [-64, 2], [1, 64]]),
                    bass.AP(ssk_t[kt % 2], 0, [[128, 128], [0, 8], [1, 128]]),
                    op=OP.mult), inc=cV)
                w = [(cV, e2)]
                if kt >= 2:
                    w.append((cT, ev_ktr[kt - 2][1]))
                ev_rp[kt] = P.op("vector", lambda e, kt=kt: e.tensor_add(
                    kr_nat[kt % 2][:, :], kt1[kt % 2][:, :], kt2[kt % 2][:, :]),
                    waits=w, inc=cV)
                for g in range(2):
                    for q in range(4):
                        hh = g * 4 + q
                        w = [(cV, ev_rp[kt])] if (g == 0 and q == 0) else []
                        if q == 0:
                            w.append((cS2, ev_ev[kt][6 + g]))
                        r = P.op("tensor", lambda e, kt=kt, hh=hh, g=g, q=q: e.matmul(
                            ps_t2[g % 2][:, q * 128:(q + 1) * 128],
                            kr_nat[kt % 2][:, hh * 128:(hh + 1) * 128], ident[:, :],
                            start=True, stop=True),
                            waits=w, inc=cT if q == 3 else None)
                        if q == 3:
                            ev_ktr[kt][g] = r
                    ev_kev[kt][g] = P.op("scalar", lambda e, kt=kt, g=g: e.copy(
                        bass.AP(KT_res, (g * 4) * T + kt * 128,
                                [[NKV * T, 128], [T, 4], [1, 128]]),
                        ps_t2[g % 2][:, :]), waits=[(cT, ev_ktr[kt][g])], inc=cS2)
        if debug:
            evd = P.op("sync", lambda e: e.dma_start(kT_dbg[:, :], KT_res[:, :]),
                       waits=[(cS2, cS2.v)], inc=cY, k=16)
        barrier()

    if "P3" in phases:
        with ExitStack() as ph:
            sb = lambda n, s, d: ph.enter_context(nc.sbuf_tensor(n, s, d))
            ps = lambda n, s: ph.enter_context(nc.psum_tensor(n, s, F32))
            V_res = sb("V_res", [128, NT * KVD], BF16)
            mask_sb = sb("mask_sb", [128, 256], BF16)
            qbuf = [sb(f"qb{i}", [128, 8 * 512], BF16) for i in range(2)]
            probs_f = [sb(f"pf{i}", [128, 4096], F32) for i in range(2)]
            probs = [sb(f"pb{i}", [128, 4096], BF16) for i in range(2)]
            pTsb = [sb(f"pT{i}", [128, 512], BF16) for i in range(2)]
            cstage = [sb(f"cst{i}", [128, 512], BF16) for i in range(2)]
            lparts = [sb(f"lp{i}", [128, 4], F32) for i in range(2)]
            lsum = [sb(f"ls{i}", [128, 1], F32) for i in range(2)]
            rl = [sb(f"rl{i}", [128, 1], F32) for i in range(2)]
            negm = [sb(f"ng{i}", [128, 1], F32) for i in range(2)]
            ps_S = [ps(f"psS{i}", [128, 1024]) for i in range(2)]
            ps_T3 = [ps(f"psT3{i}", [128, 512]) for i in range(2)]
            ps_C = ps("psC", [128, 256])

            ev_msk = P.op("gpsimd", lambda e: e.dma_start(mask_sb[:, :], mask_in),
                          inc=cX2, k=16)
            ev_vres = 0
            for kt in range(NT):
                ev_vres = P.op("gpsimd", lambda e, kt=kt: e.dma_start(
                    V_res[:, kt * KVD:(kt + 1) * KVD], vn_s[kt * 128:(kt + 1) * 128, :]),
                    inc=cX2, k=16)
            ev_qld = {}
            cQ = [cQa, cQb]
            for wb in range(8):
                ev_qld[0] = P.op("gpsimd", lambda e, wb=wb: e.dma_start(
                    qbuf[0][:, wb * 512:(wb + 1) * 512], qT_s[0, wb, :, :]),
                    inc=cQ[0], k=16)

            NU = NQT * NH
            bank_user = [0, 0]
            ev_chunk = {}
            ev_max = {}
            ev_exp = {}
            ev_explast = {}
            ev_scale = {}
            ev_ttail = {}
            ev_pev = {}
            ev_cgrp = {}
            ev_ctx = {}
            ev_cev = {}
            ev_lsum = {}
            ev_cw = {}
            ev_sc_last_q = [0, 0]

            psT_free = [None, None]
            pT_free = [0, 0]

            def emit_tc(u):
                j, hh = u // NH, u % NH
                L, rounds = slot_rounds(j)
                nkt = L // 128
                hkv = hh // 4
                ectr = cS2 if u % 2 == 0 else cV2
                eeng = "scalar" if u % 2 == 0 else "vector"
                r = None
                for g in range((nkt + 3) // 4):
                    b = g % 2
                    qn = min(4, nkt - g * 4)
                    for q in range(qn):
                        kt = g * 4 + q
                        w = [(cV, ev_scale[u])] if kt == 0 else []
                        if q == 0 and psT_free[b] is not None:
                            w.append(psT_free[b])
                        r = P.op("tensor", lambda e, u=u, kt=kt, b=b: e.matmul(
                            ps_T3[b][:, (kt % 4) * 128:(kt % 4 + 1) * 128],
                            probs[u % 2][:, kt * 128:(kt + 1) * 128], ident[:, :],
                            start=True, stop=True),
                            waits=w, inc=cT2 if q == qn - 1 else None)
                    ev_ttail[u] = r
                    w = [(cT2, r)]
                    if pT_free[b]:
                        w.append((cT2, pT_free[b]))
                    def _evac(e, b=b, qn=qn):
                        if hasattr(e, "tensor_copy"):
                            return e.tensor_copy(pTsb[b][:, :qn * 128],
                                                 ps_T3[b][:, :qn * 128])
                        return e.copy(pTsb[b][:, :qn * 128], ps_T3[b][:, :qn * 128])
                    ev_pev[(u, g)] = (ectr, P.op(eeng, _evac, waits=w, inc=ectr))
                    psT_free[b] = ev_pev[(u, g)]
                    r2 = None
                    for q in range(qn):
                        kt = g * 4 + q
                        w = [ev_pev[(u, g)]]
                        if kt == 0:
                            w.append((cX2, ev_vres))
                            if u >= 2:
                                w.append((cV, ev_cev[u - 2]))
                        r2 = P.op("tensor", lambda e, u=u, kt=kt, hkv=hkv, b=b, nkt=nkt: e.matmul(
                            ps_C[:, (u % 2) * 128:(u % 2) * 128 + 128],
                            V_res[:, kt * KVD + hkv * 128: kt * KVD + hkv * 128 + 128],
                            pTsb[b][:, (kt % 4) * 128:(kt % 4 + 1) * 128],
                            start=(kt == 0), stop=(kt == nkt - 1)),
                            waits=w, inc=cT2 if q == qn - 1 else None)
                    ev_cgrp[(u, g)] = r2
                    pT_free[b] = r2
                ev_ctx[u] = r2
                grp = hh // 4
                w = [(cT2, ev_ctx[u])]
                pkey = (j, grp - 2) if grp >= 2 else (j - 1, 8 + grp - 2)
                if pkey in ev_cw:
                    w.append((cY if pkey[1] % 2 == 0 else cYb, ev_cw[pkey]))
                ev_cev[u] = P.op("vector", lambda e, u=u, hh=hh, grp=grp: e.tensor_copy(
                    cstage[grp % 2][:, (hh % 4) * 128:(hh % 4 + 1) * 128],
                    ps_C[:, (u % 2) * 128:(u % 2) * 128 + 128]),
                    waits=w, inc=cV)
                if hh % 4 == 3:
                    ev_cw[(j, grp)] = P.op("sync", lambda e, j=j, grp=grp: e.dma_start(
                        ctxT_s[j, grp, :, :], cstage[grp % 2][:, :]),
                        waits=[(cV, ev_cev[u])],
                        inc=(cY if grp % 2 == 0 else cYb), k=16)

            for u in range(NU):
                j, hh = u // NH, u % NH
                L, rounds = slot_rounds(j)
                nr = len(rounds)
                hkv = hh // 4
                if hh == 16 and j + 1 < NQT:
                    for wb in range(8):
                        w = [(cT, ev_sc_last_q[(j + 1) % 2])] if wb == 0 else []
                        ev_qld[j + 1] = P.op("gpsimd", lambda e, j=j, wb=wb: e.dma_start(
                            qbuf[(j + 1) % 2][:, wb * 512:(wb + 1) * 512],
                            qT_s[j + 1, wb, :, :]), waits=w, inc=cQ[(j + 1) % 2], k=16)
                def emit_round(r):
                    r_off, r_w, chunks = rounds[r]
                    bank = r % 2
                    last = None
                    for ci, (c_off, c_w) in enumerate(chunks):
                        w = []
                        if ci == 0:
                            if bank_user[bank]:
                                w.append((cS2, bank_user[bank]))
                            w.append((cQ[j % 2], ev_qld[j]))
                        has_mask = (r == nr - 1 and ci == len(chunks) - 1)
                        r1 = P.op("tensor", lambda e, u=u, r=r, c_off=c_off, c_w=c_w,
                                  bank=bank, r_off=r_off, hkv=hkv, hh=hh, j=j: e.matmul(
                            ps_S[bank][:, c_off - r_off: c_off - r_off + c_w],
                            qbuf[j % 2][:, (hh // 4) * 512 + (hh % 4) * 128:
                                        (hh // 4) * 512 + (hh % 4) * 128 + 128],
                            KT_res[:, hkv * T + c_off: hkv * T + c_off + c_w],
                            start=True, stop=not has_mask),
                            waits=w, inc=None if has_mask else cT)
                        if has_mask:
                            r1 = P.op("tensor", lambda e, u=u, bank=bank, L=L, r_off=r_off: e.matmul(
                                ps_S[bank][:, L - 256 - r_off: L - r_off],
                                ident[:, :], mask_sb[:, :],
                                start=False, stop=True),
                                waits=[(cX2, ev_vres)], inc=cT)
                        last = r1
                    ev_chunk[(u, r)] = last
                    return last

                for r in range(min(2, nr)):
                    emit_round(r)
                w = [(cT, ev_chunk[(u, 0)])]
                if u >= 2:
                    w.append((cS2, ev_explast[u - 2]))
                ev_max[u] = P.op("vector", lambda e, u=u, L=L: e.reduce_max(
                    negm[u % 2][:, :], ps_S[0][:, 0:min(L, 512)], axis=AX, negate=True),
                    waits=w, inc=cV)
                def emit_exp(r):
                    r_off, r_w, chunks = rounds[r]
                    bank = r % 2
                    w = [(cT, ev_chunk[(u, r)]), (cV, ev_max[u])]
                    if u >= 2 and r == 0:
                        w.append((cV, ev_scale.get(u - 2, 0)))
                        w.append((cV, ev_lsum.get(u - 2, 0)))
                    ev_exp[(u, r)] = P.op("scalar", lambda e, u=u, r=r, r_off=r_off,
                                          r_w=r_w, bank=bank: e.activation(
                        probs_f[u % 2][:, r_off:r_off + r_w], ps_S[bank][:, 0:r_w],
                        AF.Exp, bias=negm[u % 2][:, :1],
                        accum_out=lparts[u % 2][:, r:r + 1]),
                        waits=w, inc=cS2)
                    bank_user[bank] = ev_exp[(u, r)]

                for r in range(min(2, nr)):
                    emit_exp(r)
                if u >= 1:
                    emit_tc(u - 1)
                for r in range(2, nr):
                    emit_round(r)
                    emit_exp(r)
                ev_explast[u] = ev_exp[(u, nr - 1)]
                ev_sc_last_q[j % 2] = cT.v
                ev_lsum[u] = P.op("vector", lambda e, u=u, nr=nr: e.reduce_sum(
                    lsum[u % 2][:, :], lparts[u % 2][:, 0:nr], axis=AX),
                    waits=[(cS2, ev_explast[u])], inc=cV)
                ev_rl = P.op("vector", lambda e, u=u: e.reciprocal(
                    rl[u % 2][:, :], lsum[u % 2][:, :]),
                    waits=[(cV, ev_lsum[u])], inc=cV)
                w = [(cV, ev_rl)]
                if u >= 2:
                    w.append((cT2, ev_ttail.get(u - 2, 0)))
                ev_scale[u] = P.op("vector", lambda e, u=u, L=L: e.tensor_scalar(
                    probs[u % 2][:, 0:L], probs_f[u % 2][:, 0:L], rl[u % 2][:, :1], None,
                    op0=OP.mult), waits=w, inc=cV)
            emit_tc(NU - 1)
        barrier()
        if kt_open[0]:
            kt_scope.close()
            kt_open[0] = False

    if "P5" in phases:
        with ExitStack() as ph:
            sb = lambda n, s, d: ph.enter_context(nc.sbuf_tensor(n, s, d))
            ps = lambda n, s: ph.enter_context(nc.psum_tensor(n, s, F32))
            ctx_sb = sb("ctx_sb", [128, 8 * 8 * 512], BF16)
            wst45 = [sb(f"o4_{i}", [128, 512], F32) for i in range(2)]
            wdum5 = sb("wdum5", [128, 8], F32)
            wobf = [sb(f"wobf{i}", [128, 32 * 512], BF16) for i in range(2)]
            gbuf = [sb(f"gb{i}", [128, 512], F32) for i in range(2)]
            xbuf = [sb(f"xb{i}", [128, 512], F32) for i in range(2)]
            tbuf = [sb(f"tb{i}", [128, 512], F32) for i in range(2)]
            obuf = [sb(f"ob{i}", [128, 512], F32) for i in range(2)]
            ps_s = [ps(f"pss{i}", [128, 512]) for i in range(2)]

            for h in range(2):
                ev_ctxl = 0
                for jl in range(8):
                    for wb in range(8):
                        j = h * 8 + jl
                        ev_ctxl = P.op("gpsimd", lambda e, j=j, jl=jl, wb=wb: e.dma_start(
                            ctx_sb[:, (jl * 8 + wb) * 512:(jl * 8 + wb + 1) * 512],
                            ctxT_s[j, wb, :, :]), inc=cX2, k=16)
                ev_wc = {}
                ev_mm = {}
                ev_t1 = {}
                ev_out = {}
                ev_yw = {}
                ev_gx = {}
                for wb in range(8):
                    for dt in range(32):
                        i = wb * 32 + dt
                        wp = cW if i % 2 == 0 else cWb
                        w = [(cS, ev_wc[(i - 2) // 32, (i - 2) % 32])] if i >= 2 else []
                        evl = P.op("sync", lambda e, wb=wb, dt=dt, i=i: e.dma_start(
                            wst45[i % 2][:, :],
                            wo[dt * 128:(dt + 1) * 128, wb * 512:(wb + 1) * 512]),
                            waits=w, inc=wp, k=16)
                        w = [(wp, evl)]
                        if wb >= 2 and dt == 0:
                            w.append((cT, ev_mm[wb - 2, 7]))
                        ev_wc[wb, dt] = P.op("scalar", lambda e, wb=wb, dt=dt, i=i: e.copy(
                            wobf[wb % 2][:, dt * 512:(dt + 1) * 512], wst45[i % 2][:, :]),
                            waits=w, inc=cS)
                    ev_wc[wb, 32] = P.op("scalar", lambda e, wb=wb: e.copy(
                        wdum5[:, :], wst45[(wb * 32 + 31) % 2][:, 0:8]),
                        waits=[(cS, ev_wc[wb, 31])], inc=cS)
                    for jl in range(8):
                        j = h * 8 + jl
                        ui = wb * 8 + jl
                        w = []
                        if ui >= 2:
                            pw, pj = (ui - 2) // 8, (ui - 2) % 8
                            w.append((cV, ev_t1[pw, pj]))
                        ev_g = P.op("gpsimd", lambda e, j=j, wb=wb, jl=jl: e.dma_start(
                            gbuf[jl % 2][:, :],
                            g_s[j * 128:(j + 1) * 128, wb * 512:(wb + 1) * 512]),
                            waits=w, inc=(cX if jl % 2 == 0 else cXb), k=16)
                        w2 = []
                        if ui >= 2:
                            pw, pj = (ui - 2) // 8, (ui - 2) % 8
                            w2.append((cV, ev_out[pw, pj]))
                        ev_xl = P.op("gpsimd", lambda e, j=j, wb=wb, jl=jl: e.dma_start(
                            xbuf[jl % 2][:, :],
                            x_own[j * 128:(j + 1) * 128, wb * 512:(wb + 1) * 512]),
                            waits=w2, inc=(cQa if jl % 2 == 0 else cQb), k=16)
                        ev_gx[wb, jl] = (ev_g, ev_xl)
                        r = None
                        for dt in range(32):
                            w = [(cS, ev_wc[wb, 32])]
                            if dt == 0:
                                w.append((cX2, ev_ctxl))
                                if ui >= 2:
                                    pw, pj = (ui - 2) // 8, (ui - 2) % 8
                                    w.append((cV, ev_t1[pw, pj]))
                            r = P.op("tensor", lambda e, wb=wb, jl=jl, dt=dt: e.matmul(
                                ps_s[jl % 2][:, :],
                                ctx_sb[:, (jl * 8 + dt // 4) * 512 + (dt % 4) * 128:
                                       (jl * 8 + dt // 4) * 512 + (dt % 4) * 128 + 128],
                                wobf[wb % 2][:, dt * 512:(dt + 1) * 512],
                                start=(dt == 0), stop=(dt == 31)),
                                waits=w, inc=cT if dt == 31 else None)
                        ev_mm[wb, jl] = r
                        w = [(cT, r), (cX if jl % 2 == 0 else cXb, ev_g)]
                        if ui >= 2:
                            pw, pj = (ui - 2) // 8, (ui - 2) % 8
                            w.append((cV, ev_out[pw, pj]))
                        ev_t1[wb, jl] = P.op("vector", lambda e, jl=jl: e.tensor_tensor(
                            tbuf[jl % 2][:, :], ps_s[jl % 2][:, :], gbuf[jl % 2][:, :],
                            op=OP.mult), waits=w, inc=cV)
                        w = [(cV, ev_t1[wb, jl]),
                             (cQa if jl % 2 == 0 else cQb, ev_xl)]
                        if ui >= 2:
                            pw, pj = (ui - 2) // 8, (ui - 2) % 8
                            w.append((cY if pj % 2 == 0 else cYb, ev_yw[pw, pj]))
                        ev_out[wb, jl] = P.op("vector", lambda e, jl=jl: e.tensor_add(
                            obuf[jl % 2][:, :], tbuf[jl % 2][:, :], xbuf[jl % 2][:, :]),
                            waits=w, inc=cV)
                        ev_yw[wb, jl] = P.op("sync", lambda e, j=j, wb=wb, jl=jl: e.dma_start(
                            y[j * 128:(j + 1) * 128, wb * 512:(wb + 1) * 512],
                            obuf[jl % 2][:, :]),
                            waits=[(cV, ev_out[wb, jl])],
                            inc=(cY if jl % 2 == 0 else cYb), k=16)
                barrier()

    if KT_res is not None and kt_open[0]:
        kt_scope.close()
        kt_open[0] = False

    P.op("sync", None, waits=[(cY, cY.v), (cYb, cYb.v), (cY2, cY2.v),
                              (cZb, cZb.v), (cW2, cW2.v)])
    barrier()

    nums = sorted(h.num for h in sem_handles)
    assert nums == list(range(nums[0], nums[0] + len(nums))), nums
    srange = range(nums[0], nums[-1] + 1)
    nc.gpsimd.dma_reset(srange)
    nc.gpsimd.sem_clear(srange)
    nc.all_engine_barrier()

    with nc.Block() as block:
        @block.tensor
        def _(eng):
            P.emit("tensor", eng)

        @block.vector
        def _(eng):
            P.emit("vector", eng)

        @block.scalar
        def _(eng):
            P.emit("scalar", eng)

        @block.sync
        def _(eng):
            P.emit("sync", eng)

        @block.gpsimd
        def _(eng):
            P.emit("gpsimd", eng)

    es.close()
    return nc


def _rope_tables():
    inv = 1.0 / (500000.0 ** (np.arange(0, HD, 2, dtype=np.float64) / HD))
    t = np.arange(T, dtype=np.float64)
    fr = np.outer(t, inv)
    cos = np.cos(fr).astype(np.float32)
    sin = np.sin(fr).astype(np.float32)
    cc = np.concatenate([cos, cos], axis=1)
    ss = np.concatenate([-sin, sin], axis=1)
    return cc, ss


def _mask_for_parity(p):
    m = np.zeros((128, 256), np.float32)
    qi, ci = np.meshgrid(np.arange(128), np.arange(256), indexing="ij")
    if p == 1:
        m[(ci >= 128) & (ci - 128 > qi)] = NEGBIG
    else:
        m[(ci < 128) & (ci > qi)] = NEGBIG
        m[ci >= 128] = NEGBIG
    return m.astype(BF16NP)


def prep_in_maps(L0_attn_x, L0_attn_Wq, L0_attn_Wk, L0_attn_Wv, L0_attn_Wo,
                 L0_ffn_gate, L0_ffn_kernel, L0_ffn_bias, L0_ffn_lora_A,
                 L0_ffn_lora_B, L0_ffn_lora_A_shared, L0_ffn_lora_B_shared):
    xin = np.ascontiguousarray(np.asarray(L0_attn_x, dtype=np.float32))
    wq = np.ascontiguousarray(np.asarray(L0_attn_Wq, dtype=np.float32))
    wk = np.ascontiguousarray(np.asarray(L0_attn_Wk, dtype=np.float32))
    wv = np.ascontiguousarray(np.asarray(L0_attn_Wv, dtype=np.float32))
    wo = np.ascontiguousarray(np.asarray(L0_attn_Wo, dtype=np.float32))
    gate = np.asarray(L0_ffn_gate, dtype=np.float32)
    wkern = np.ascontiguousarray(np.asarray(L0_ffn_kernel, dtype=np.float32))
    bias = np.asarray(L0_ffn_bias, dtype=np.float32).reshape(1, D)
    lora_A = np.asarray(L0_ffn_lora_A, dtype=np.float32)
    lora_B = np.asarray(L0_ffn_lora_B, dtype=np.float32)
    As = np.asarray(L0_ffn_lora_A_shared, dtype=np.float32)
    Bs = np.asarray(L0_ffn_lora_B_shared, dtype=np.float32)

    loraA_dev = np.ascontiguousarray(
        lora_A.reshape(NLORA, RANK, 32, 128).transpose(0, 3, 2, 1).reshape(NLORA, 128, 32 * RANK))
    As_dev = np.ascontiguousarray(As.reshape(RANK, 32, 128).transpose(2, 1, 0).reshape(128, 32 * RANK))
    loraB_dev = np.ascontiguousarray(
        lora_B.transpose(0, 2, 1).reshape(NLORA, 128, 32 * RANK))
    Bs_fold = np.ascontiguousarray(Bs.T.reshape(128, 32 * RANK))
    bias_f = np.ascontiguousarray(bias.reshape(128, 32))
    identity = np.eye(128, dtype=np.float32).astype(BF16NP)
    cc, ss = _rope_tables()
    scale = 1.0 / np.sqrt(HD)
    in_maps = []
    for c in range(NCORES):
        b, p = c // 2, c % 2
        tiles = [2 * j + p for j in range(NQT)]
        rows = np.concatenate([np.arange(t * 128, t * 128 + 128) for t in tiles])
        x_b = np.ascontiguousarray(xin[b])
        in_maps.append({
            "x": x_b,
            "x_own": np.ascontiguousarray(x_b[rows]),
            "wq": wq, "wk": wk, "wv": wv, "wo": wo, "wkern": wkern,
            "bias_fold": bias_f,
            "gate2": np.ascontiguousarray(gate[[2 * b, 2 * b + 1]].reshape(1, 2 * NLORA)),
            "loraA_dev": loraA_dev, "As_dev": As_dev,
            "loraB_dev": loraB_dev, "Bs_fold": Bs_fold,
            "ident_in": identity,
            "mask_in": _mask_for_parity(p),
            "ccq": np.ascontiguousarray(cc[rows] * scale),
            "ssq": np.ascontiguousarray(ss[rows] * scale),
            "cck": cc, "ssk": ss,
        })
    return in_maps


def assemble_out(results):
    out = np.empty((B, T, D), np.float32)
    for c in range(NCORES):
        b, p = c // 2, c % 2
        yc = np.asarray(results[c]["y"])
        for j in range(NQT):
            t = 2 * j + p
            out[b, t * 128:(t + 1) * 128, :] = yc[j * 128:(j + 1) * 128, :]
    return out


def kernel(**inputs):
    in_maps = prep_in_maps(**inputs)
    nc = build_program()
    res = run_bass_kernel_spmd(nc, in_maps, core_ids=list(range(NCORES)))
    return assemble_out(res.results)


# version 02
# speedup vs baseline: 1.0518x; 1.0518x over previous
import numpy as np
import ml_dtypes
from contextlib import ExitStack

import concourse.bass as bass
import concourse.mybir as mybir
from concourse.bass_utils import run_bass_kernel_spmd

F32 = mybir.dt.float32
BF16 = mybir.dt.bfloat16
AX = mybir.AxisListType.X
AF = mybir.ActivationFunctionType
OP = mybir.AluOpType
BF16NP = ml_dtypes.bfloat16

B, T, D = 4, 4096, 4096
NCORES = 8
NH, NKV, HD = 32, 8, 128
KVD = NKV * HD
NT = T // 128
NQT = 16
RANK = 32
NLORA = 16
EPS = 1e-5
NEGBIG = -30000.0
ENGINES = ("tensor", "vector", "scalar", "sync", "gpsimd")


class Ctr:
    def __init__(self, sem, name):
        self.sem = sem
        self.name = name
        self.v = 0


class Prog:

    def __init__(self):
        self.ops = []

    def op(self, engine, fn, waits=(), inc=None, k=1):
        w = [(c, v) for (c, v) in waits if c is not None and v and v > 0]
        self.ops.append((engine, fn, w, inc, k))
        if inc is not None:
            inc.v += k
            return inc.v
        return None

    def emit(self, engine_name, eng):
        last_wait = {}
        for e, fn, waits, inc, k in self.ops:
            if e != engine_name:
                continue
            for ctr, val in waits:
                if last_wait.get(id(ctr), 0) < val:
                    eng.wait_ge(ctr.sem, val)
                    last_wait[id(ctr)] = val
            if fn is None:
                continue
            ins = fn(eng)
            if inc is not None:
                ins.then_inc(inc.sem, k)


def slot_rounds(j):
    L = 256 * (j + 1)
    rounds = []
    off = 0
    while off < L:
        w = min(1024, L - off)
        chunks = []
        o = off
        while o < off + w:
            cw = min(512, off + w - o)
            chunks.append((o, cw))
            o += cw
        rounds.append((off, w, chunks))
        off += w
    return L, rounds


def build_program(phases=("P0", "P1", "P2", "P3", "P5"), debug=False):
    nc = bass.Bass("TRN2", target_bir_lowering=False)

    x = nc.dram_tensor("x", [T, D], F32, kind="ExternalInput").ap()
    x_own = nc.dram_tensor("x_own", [NQT * 128, D], F32, kind="ExternalInput").ap()
    wq = nc.dram_tensor("wq", [D, D], F32, kind="ExternalInput").ap()
    wk = nc.dram_tensor("wk", [D, KVD], F32, kind="ExternalInput").ap()
    wv = nc.dram_tensor("wv", [D, KVD], F32, kind="ExternalInput").ap()
    wo = nc.dram_tensor("wo", [D, D], F32, kind="ExternalInput").ap()
    wkern = nc.dram_tensor("wkern", [D, D], F32, kind="ExternalInput").ap()
    bias_fold = nc.dram_tensor("bias_fold", [128, 32], F32, kind="ExternalInput").ap()
    gate2 = nc.dram_tensor("gate2", [1, 2 * NLORA], F32, kind="ExternalInput").ap()
    loraA_dev = nc.dram_tensor("loraA_dev", [NLORA, 128, 32 * RANK], F32, kind="ExternalInput").ap()
    As_dev = nc.dram_tensor("As_dev", [128, 32 * RANK], F32, kind="ExternalInput").ap()
    loraB_dev = nc.dram_tensor("loraB_dev", [NLORA, 128, 32 * RANK], F32, kind="ExternalInput").ap()
    Bs_fold = nc.dram_tensor("Bs_fold", [128, 32 * RANK], F32, kind="ExternalInput").ap()
    ident_in = nc.dram_tensor("ident_in", [128, 128], BF16, kind="ExternalInput").ap()
    mask_in = nc.dram_tensor("mask_in", [128, 256], BF16, kind="ExternalInput").ap()
    ccq = nc.dram_tensor("ccq", [NQT * 128, 128], F32, kind="ExternalInput")
    ssq = nc.dram_tensor("ssq", [NQT * 128, 128], F32, kind="ExternalInput")
    cck = nc.dram_tensor("cck", [T, 128], F32, kind="ExternalInput")
    ssk = nc.dram_tensor("ssk", [T, 128], F32, kind="ExternalInput")

    sk = "ExternalOutput" if debug else "Internal"
    xn_s = nc.dram_tensor("xn_s", [T, D], BF16, kind=sk).ap()
    vn_s = nc.dram_tensor("vn_s", [T, KVD], BF16, kind=sk).ap()
    qT_s = nc.dram_tensor("qT_s", [NQT, 8, 128, 512], BF16, kind=sk).ap()
    g_s = nc.dram_tensor("g_s", [NQT * 128, D], F32, kind=sk).ap()
    ctxT_s = nc.dram_tensor("ctxT_s", [NQT, 8, 128, 512], BF16, kind=sk).ap()
    kT_dbg = nc.dram_tensor("kT_dbg", [128, NKV * T], BF16,
                            kind="ExternalOutput" if debug else "Internal").ap()
    astk_s = nc.dram_tensor("astk_s", [2, 128, 2048], BF16, kind=sk).ap()
    axt_dbg = nc.dram_tensor("axt_dbg", [2, 65, 1024], BF16,
                             kind="ExternalOutput" if debug else "Internal").ap()
    bsb_dbg = nc.dram_tensor("bsb_dbg", [2, 128, D], BF16,
                             kind="ExternalOutput" if debug else "Internal").ap()
    wbf_dbg = nc.dram_tensor("wbf_dbg", [2, 2, 128, 32 * 512], BF16,
                             kind="ExternalOutput" if debug else "Internal").ap()
    xnt_dbg = nc.dram_tensor("xnt_dbg", [2, 128, 32 * 1024], BF16,
                             kind="ExternalOutput" if debug else "Internal").ap()
    bstk_s_h = nc.dram_tensor("bstk_s", [2 * 65 * D], BF16, kind=sk)
    bstk_s = bstk_s_h.ap()
    y = nc.dram_tensor("y", [NQT * 128, D], F32, kind="ExternalOutput").ap()

    es = ExitStack()
    sem_handles = []

    def SEM(name):
        h = es.enter_context(nc.semaphore(name))
        sem_handles.append(h)
        return h

    P = Prog()
    cW = Ctr(SEM("sW"), "W")
    cWb = Ctr(SEM("sWb"), "Wb")
    cW2 = Ctr(SEM("sW2"), "W2")
    cX2 = Ctr(SEM("sX2"), "X2")
    cXb = Ctr(SEM("sXb"), "Xb")
    cQa = Ctr(SEM("sQa"), "Qa")
    cQb = Ctr(SEM("sQb"), "Qb")
    cY2 = Ctr(SEM("sY2"), "Y2")
    cYb = Ctr(SEM("sYb"), "Yb")
    cZb = Ctr(SEM("sZb"), "Zb")
    cX = Ctr(SEM("sX"), "X")
    cY = Ctr(SEM("sY"), "Y")
    cT = Ctr(SEM("sT"), "T")
    cT2 = Ctr(SEM("sT2"), "T2")
    cV = Ctr(SEM("sV"), "V")
    cV2 = Ctr(SEM("sV2"), "V2")
    cS = Ctr(SEM("sS"), "S")
    cS2 = Ctr(SEM("sS2"), "S2")
    cB = Ctr(SEM("sB"), "B")

    ident = es.enter_context(nc.sbuf_tensor("identity", [128, 128], BF16))
    ones_bf = es.enter_context(nc.sbuf_tensor("ones_bf", [1, 128], BF16))
    eps_sb = es.enter_context(nc.sbuf_tensor("eps_sb", [128, 1], F32))

    def barrier():
        allw = [(cW, cW.v), (cWb, cWb.v), (cW2, cW2.v), (cY, cY.v),
                (cYb, cYb.v), (cY2, cY2.v), (cZb, cZb.v), (cX, cX.v),
                (cXb, cXb.v), (cX2, cX2.v), (cQa, cQa.v), (cQb, cQb.v)]
        P.op("sync", None, waits=list(allw))
        P.op("gpsimd", None, waits=list(allw))
        for name in ENGINES:
            P.op(name, lambda eng: eng.nop(), inc=cB)
        tgt = cB.v
        for name in ENGINES:
            P.op(name, None, waits=[(cB, tgt)])

    ev_ident = P.op("sync", lambda e: e.dma_start(ident[:, :], ident_in), inc=cW2, k=16)
    P.op("vector", lambda e: e.memset(eps_sb[:, :], EPS), inc=cV)
    ev_ones = P.op("vector", lambda e: e.memset(ones_bf[:, :], 1.0), inc=cV)

    def make_ln_bufs(ph_sb, tag):
        return dict(
            stats=[ph_sb(f"{tag}st{i}", [128, 48], F32) for i in range(2)],
            mv=[ph_sb(f"{tag}mv{i}", [128, 2], F32) for i in range(2)],
            lnv=[ph_sb(f"{tag}ln{i}", [128, 1], F32) for i in range(2)],
            rstd=[ph_sb(f"{tag}rs{i}", [128, 1], F32) for i in range(2)],
            nms=[ph_sb(f"{tag}nm{i}", [128, 1], F32) for i in range(2)],
        )

    def emit_ln(lb, par, x_buf, xn_buf, w_first, w_norm):
        evb = None
        for r in range(8):
            w = list(w_first) if r == 0 else []
            evb = P.op("vector", lambda e, r=r: e.bn_stats(
                lb["stats"][par][:, r * 6:(r + 1) * 6],
                x_buf[:, r * 512:(r + 1) * 512]), waits=w, inc=cV)
        ev_ag = P.op("vector", lambda e: e.bn_aggr(
            lb["mv"][par][:, :], lb["stats"][par][:, :]), waits=[(cV, evb)], inc=cV)
        ev_ln = P.op("scalar", lambda e: e.activation(
            lb["lnv"][par][:, :], lb["mv"][par][:, 1:2], AF.Ln, bias=eps_sb[:, :1],
            scale=1.0), waits=[(cV, ev_ag)], inc=cS2)
        ev_rs = P.op("scalar", lambda e: e.activation(
            lb["rstd"][par][:, :], lb["lnv"][par][:, :], AF.Exp, scale=-0.5),
            waits=[(cS2, ev_ln)], inc=cS2)
        ev_nm = P.op("vector", lambda e: e.scalar_tensor_tensor(
            lb["nms"][par][:, :], lb["mv"][par][:, :1], -1.0, lb["rstd"][par][:, :1],
            op0=OP.mult, op1=OP.mult), waits=[(cS2, ev_rs)], inc=cV)
        return P.op("vector", lambda e: e.tensor_scalar(
            xn_buf[:, :], x_buf[:, :], lb["rstd"][par][:, :1], lb["nms"][par][:, :1],
            op0=OP.mult, op1=OP.add), waits=[(cV, ev_nm)] + list(w_norm), inc=cV)

    if "P0" in phases:
        with ExitStack() as ph:
            sb = lambda n, s, d: ph.enter_context(nc.sbuf_tensor(n, s, d))
            ps = lambda n, s: ph.enter_context(nc.psum_tensor(n, s, F32))
            wv_bf = sb("wv_bf", [128, 32 * KVD], BF16)
            wst = [sb(f"p0w{i}", [128, KVD], F32) for i in range(2)]
            x_f32 = [sb(f"p0x{i}", [128, D], F32) for i in range(2)]
            xn_nat = [sb(f"p0n{i}", [128, D], BF16) for i in range(2)]
            xnT_t = [sb(f"p0t{i}", [128, D], BF16) for i in range(2)]
            v_bf = [sb(f"p0v{i}", [128, KVD], BF16) for i in range(2)]
            lb = make_ln_bufs(sb, "p0")
            ps_t = [ps(f"p0pt{i}", [128, 512]) for i in range(2)]
            ps_v = [ps(f"p0pv{i}", [128, KVD]) for i in range(2)]

            ev_wvc = [0] * 32
            for wt in range(32):
                wp = cW if wt % 2 == 0 else cWb
                w = [(cS, ev_wvc[wt - 2])] if wt >= 2 else []
                ev = P.op("sync", lambda e, wt=wt: e.dma_start(
                    wst[wt % 2][:, :], wv[wt * 128:(wt + 1) * 128, :]),
                    waits=w, inc=wp, k=16)
                ev_wvc[wt] = P.op("scalar", lambda e, wt=wt: e.copy(
                    wv_bf[:, wt * KVD:(wt + 1) * KVD], wst[wt % 2][:, :]),
                    waits=[(wp, ev)], inc=cS)

            ev_norm = [0] * NT
            ev_xnw = [0] * NT
            ev_tr = [[0] * 8 for _ in range(NT)]
            ev_evT = [[0] * 8 for _ in range(NT)]
            ev_vmm = [0] * NT
            ev_evV = [0] * NT
            ev_vw = [0] * NT
            for t in range(NT):
                xp = cX if t % 2 == 0 else cXb
                yp = cY if t % 2 == 0 else cYb
                w = [(cV, ev_norm[t - 2])] if t >= 2 else []
                ev_x = P.op("gpsimd", lambda e, t=t: e.dma_start(
                    x_f32[t % 2][:, :], x[t * 128:(t + 1) * 128, :]),
                    waits=w, inc=xp, k=16)
                w_norm = []
                if t >= 2:
                    w_norm = [(yp, ev_xnw[t - 2]), (cT, ev_tr[t - 2][7])]
                ev_norm[t] = emit_ln(lb, t % 2, x_f32[t % 2], xn_nat[t % 2],
                                     [(xp, ev_x)], w_norm)
                ev_xnw[t] = P.op("sync", lambda e, t=t: e.dma_start(
                    xn_s[t * 128:(t + 1) * 128, :], xn_nat[t % 2][:, :]),
                    waits=[(cV, ev_norm[t])], inc=yp, k=16)
                for g in range(8):
                    for q in range(4):
                        dt = g * 4 + q
                        w = []
                        if g == 0 and q == 0:
                            w = [(cV, ev_norm[t]), (cW2, ev_ident)]
                        if q == 0 and t * 8 + g >= 2:
                            gi = t * 8 + g - 2
                            w.append((cS2, ev_evT[gi // 8][gi % 8]))
                        r = P.op("tensor", lambda e, t=t, dt=dt, g=g, q=q: e.matmul(
                            ps_t[g % 2][:, q * 128:(q + 1) * 128],
                            xn_nat[t % 2][:, dt * 128:(dt + 1) * 128], ident[:, :],
                            start=True, stop=True),
                            waits=w, inc=cT if q == 3 else None)
                        if q == 3:
                            ev_tr[t][g] = r
                    w = [(cT, ev_tr[t][g])]
                    if g == 0 and t >= 2:
                        w.append((cT2, ev_vmm[t - 2]))
                    ev_evT[t][g] = P.op("scalar", lambda e, t=t, g=g: e.copy(
                        xnT_t[t % 2][:, g * 512:(g + 1) * 512], ps_t[g % 2][:, :]),
                        waits=w, inc=cS2)
                r = None
                for dt in range(32):
                    for hf in range(2):
                        w = [(cS2, ev_evT[t][dt // 4]), (cS, ev_wvc[dt])]
                        if dt == 0 and hf == 0 and t >= 2:
                            w.append((cS2, ev_evV[t - 2]))
                        last = (dt == 31 and hf == 1)
                        r = P.op("tensor", lambda e, t=t, dt=dt, hf=hf: e.matmul(
                            ps_v[t % 2][:, hf * 512:(hf + 1) * 512],
                            xnT_t[t % 2][:, dt * 128:(dt + 1) * 128],
                            wv_bf[:, dt * KVD + hf * 512: dt * KVD + (hf + 1) * 512],
                            start=(dt == 0), stop=(dt == 31)),
                            waits=w, inc=cT2 if last else None)
                ev_vmm[t] = r
                zp = cY2 if t % 2 == 0 else cZb
                w = [(cT2, ev_vmm[t])]
                if t >= 2:
                    w.append((zp, ev_vw[t - 2]))
                ev_evV[t] = P.op("scalar", lambda e, t=t: e.copy(
                    v_bf[t % 2][:, :], ps_v[t % 2][:, :]), waits=w, inc=cS2)
                ev_vw[t] = P.op("sync", lambda e, t=t: e.dma_start(
                    vn_s[t * 128:(t + 1) * 128, :], v_bf[t % 2][:, :]),
                    waits=[(cS2, ev_evV[t])], inc=zp, k=16)
        barrier()

    if "P1" in phases:
        with ExitStack() as ph:
            sb = lambda n, s, d: ph.enter_context(nc.sbuf_tensor(n, s, d))
            ps = lambda n, s: ph.enter_context(nc.psum_tensor(n, s, F32))
            A_stack = sb("Astk", [128, 32 * 64], BF16)
            B_stack = sb("Bstk", [128, D], BF16)
            AxT = sb("AxT", [128, 1024], BF16)
            gate_f = sb("gate_f", [1, 2 * NLORA], F32)
            gate_bf = sb("gate_bf", [1, 2 * NLORA], BF16)
            gbc = sb("gbc", [128, NLORA], F32)
            ccq_sb = sb("ccq_sb", [128, 8 * 128], F32)
            ssq_sb = sb("ssq_sb", [128, 8 * 128], F32)
            lb1 = make_ln_bufs(sb, "p1")
            wst4 = [sb(f"w4_{i}", [128, 512], F32) for i in range(2)]
            t1b = sb("t1b", [128, 512], F32)
            t2b = sb("t2b", [128, 512], F32)
            qr_nat = [sb(f"qrn{i}", [128, 512], BF16) for i in range(2)]
            qstage = [sb(f"qst{i}", [128, 512], BF16) for i in range(2)]
            gstage = [sb(f"gst{i}", [128, 512], F32) for i in range(2)]
            sge = [sb(f"sge{i}", [128, 512], F32) for i in range(2)]
            wdum = sb("wdum", [128, 8], F32)
            ps_qg = [ps(f"psqg{i}", [128, 512]) for i in range(2)]
            ps_t1 = [ps(f"pst1{i}", [128, 512]) for i in range(2)]
            ps_ax = ps("ps_ax", [64, 512])
            ps_sm = ps("ps_sm", [128, NLORA])

            with ExitStack() as tmp:
                tsb = lambda n, s, d: tmp.enter_context(nc.sbuf_tensor(n, s, d))
                acc_a = [tsb(f"acca{i}", [128, 32 * RANK], F32) for i in range(2)]
                acc_b = [tsb(f"accb{i}", [128, 32 * RANK], F32) for i in range(2)]
                At_st = [tsb(f"Atst{i}", [128, 32 * RANK], F32) for i in range(2)]
                As_st = tsb("Asst", [128, 32 * RANK], F32)
                ab_bf = tsb("ab_bf", [128, 2048], BF16)
                fb_bf = tsb("fb_bf", [128, 32 * RANK], BF16)
                bi_st = tsb("bi_st", [128, 32], F32)
                bi_bf = tsb("bi_bf", [128, 32], BF16)

                evg = P.op("sync", lambda e: e.dma_start(gate_f[:, :], gate2), inc=cW2, k=16)
                evA = P.op("sync", lambda e: e.dma_start(As_st[:, :], As_dev), inc=cW2, k=16)
                evBs = P.op("sync", lambda e: e.dma_start(At_st[0][:, :], Bs_fold),
                            waits=[], inc=cW2, k=16)
                evbi0 = P.op("sync", lambda e: e.dma_start(bi_st[:, :], bias_fold),
                             inc=cW2, k=16)
                ev_gbf = P.op("scalar", lambda e: e.copy(gate_bf[:, :], gate_f[:, :]),
                              waits=[(cW2, evbi0)], inc=cS)
                ev_bsc = P.op("scalar", lambda e: e.copy(fb_bf[:, :], At_st[0][:, :]),
                              waits=[(cW2, evbi0)], inc=cS)
                for c in range(2):
                    P.op("sync", lambda e, c=c: e.dma_start(
                        bass.AP(bstk_s_h, c * 65 * D + 32 * D, [[1024, 128], [1, 1024]]),
                        fb_bf[:, :]), waits=[(cS, ev_bsc)], inc=cW2, k=16)
                ev_bic = P.op("scalar", lambda e: e.copy(bi_bf[:, :], bi_st[:, :]),
                              waits=[(cW2, evbi0)], inc=cS)
                for c in range(2):
                    P.op("sync", lambda e, c=c: e.dma_start(
                        bass.AP(bstk_s_h, c * 65 * D + 64 * D, [[32, 128], [1, 32]]),
                        bi_bf[:, :]), waits=[(cS, ev_bic)], inc=cW2, k=16)

                at_free = [(cS, ev_bsc), None]
                ev_fb_free = cW2.v
                ev_ab_free = 0
                ev_gbc_prev = 0
                ev_lastac = 0
                for c in range(2):
                    w = [(cS, ev_gbf), (cV, ev_ones), (cW, ev_ident)]
                    if ev_gbc_prev:
                        w.append((cV, ev_gbc_prev))
                    evm = P.op("tensor", lambda e, c=c: e.matmul(
                        ps_sm[:, :], ones_bf[:1, :], gate_bf[:1, c * NLORA:(c + 1) * NLORA],
                        start=True, stop=True), waits=w, inc=cT)
                    w = [(cT, evm)]
                    if ev_lastac:
                        w.append((cV, ev_lastac))
                    ev_gbc = P.op("vector", lambda e: e.tensor_copy(
                        gbc[:, :], ps_sm[:, :]), waits=w, inc=cV)
                    ev_gbc_prev = ev_gbc
                    for mat, dev, accs in (("A", loraA_dev, acc_a),
                                           ("B", loraB_dev, acc_b)):
                        ev_dma = [0] * NLORA
                        ev_ac = [0] * NLORA
                        for n in range(NLORA):
                            xp = cX if n % 2 == 0 else cXb
                            w = [at_free[n % 2]] if at_free[n % 2] else []
                            ev_dma[n] = P.op("gpsimd", lambda e, dev=dev, n=n: e.dma_start(
                                At_st[n % 2][:, :], dev[n, :, :]), waits=w, inc=xp, k=16)
                            if n == 0:
                                ev_ac[0] = P.op("vector", lambda e, accs=accs: e.tensor_scalar(
                                    accs[1][:, :], At_st[0][:, :], gbc[:, 0:1], None,
                                    op0=OP.mult),
                                    waits=[(xp, ev_dma[0]), (cV, ev_gbc)], inc=cV)
                            else:
                                ev_ac[n] = P.op("vector", lambda e, accs=accs, n=n:
                                                e.scalar_tensor_tensor(
                                    accs[(n + 1) % 2][:, :], At_st[n % 2][:, :],
                                    gbc[:, n:n + 1], accs[n % 2][:, :],
                                    op0=OP.mult, op1=OP.add),
                                    waits=[(xp, ev_dma[n]), (cV, ev_ac[n - 1])], inc=cV)
                            at_free[n % 2] = (cV, ev_ac[n])
                            ev_lastac = ev_ac[n]
                        if mat == "A":
                            w = [(cV, ev_ac[NLORA - 1])]
                            if ev_ab_free:
                                w.append((cW2, ev_ab_free))
                            e1 = P.op("vector", lambda e: e.tensor_copy(
                                bass.AP(ab_bf, 0, [[2048, 128], [64, 32], [1, 32]]),
                                bass.AP(acc_a[0], 0, [[1024, 128], [32, 32], [1, 32]])),
                                waits=w, inc=cV)
                            e2 = P.op("vector", lambda e: e.tensor_copy(
                                bass.AP(ab_bf, 32, [[2048, 128], [64, 32], [1, 32]]),
                                bass.AP(As_st, 0, [[1024, 128], [32, 32], [1, 32]])),
                                waits=[(cW2, evA), (cV, e1)], inc=cV)
                            ev_ab_free = P.op("sync", lambda e, c=c: e.dma_start(
                                astk_s[c, :, :], ab_bf[:, :]),
                                waits=[(cV, e2)], inc=cW2, k=16)
                        else:
                            e1 = P.op("vector", lambda e: e.tensor_copy(
                                fb_bf[:, :], acc_b[0][:, :]),
                                waits=[(cV, ev_ac[NLORA - 1]), (cW2, ev_fb_free)], inc=cV)
                            ev_fb_free = P.op("sync", lambda e, c=c: e.dma_start(
                                bass.AP(bstk_s_h, c * 65 * D, [[1024, 128], [1, 1024]]),
                                fb_bf[:, :]), waits=[(cV, e1)], inc=cW2, k=16)
                ev_stk_w = cW2.v

            xnT_sb = sb("xnT_sb", [128, 32 * 1024], BF16)

            for h in range(2):
                hscope = ExitStack()
                xo_f32 = [hscope.enter_context(nc.sbuf_tensor(f"p1x{i}_{h}", [128, D], F32))
                          for i in range(2)]
                natq = [hscope.enter_context(nc.sbuf_tensor(f"natq{i}_{h}", [128, D], BF16))
                        for i in range(2)]
                ev_Ald = P.op("gpsimd", lambda e, h=h: e.dma_start(
                    A_stack[:, :], astk_s[h, :, :]), waits=[(cW2, ev_stk_w)], inc=cX2, k=16)
                ev_bz = P.op("vector", lambda e: e.memset(B_stack[64:128, :], 0.0), inc=cV)
                ev_Bld = P.op("gpsimd", lambda e, h=h: e.dma_start(
                    B_stack[0:65, :], bass.AP(bstk_s_h, h * 65 * D, [[D, 65], [1, D]])),
                    waits=[(cW2, ev_stk_w), (cV, ev_bz)], inc=cX2, k=16)
                P.op("vector", lambda e: e.memset(AxT[64:128, :], 0.0), inc=cV)
                ev_ones_ax = P.op("vector", lambda e: e.memset(AxT[64:65, :], 1.0), inc=cV)
                ev_cc = P.op("gpsimd", lambda e, h=h: e.dma_start(
                    ccq_sb[:, :], bass.AP(ccq, h * 1024 * 128,
                                          [[128, 128], [128 * 128, 8], [1, 128]])),
                    inc=cX2, k=16)
                ev_ss = P.op("gpsimd", lambda e, h=h: e.dma_start(
                    ssq_sb[:, :], bass.AP(ssq, h * 1024 * 128,
                                          [[128, 128], [128 * 128, 8], [1, 128]])),
                    inc=cX2, k=16)
                ev_tr = [[0] * 8 for _ in range(8)]
                ev_ev = [[0] * 8 for _ in range(8)]
                ev_nrm = [0] * 8
                for jl in range(8):
                    j = h * 8 + jl
                    xp = cX if jl % 2 == 0 else cXb
                    w = [(cV, ev_nrm[jl - 2])] if jl >= 2 else []
                    ev_x = P.op("gpsimd", lambda e, j=j, jl=jl: e.dma_start(
                        xo_f32[jl % 2][:, :], x_own[j * 128:(j + 1) * 128, :]),
                        waits=w, inc=xp, k=16)
                    w_norm = [(cT, ev_tr[jl - 2][7])] if jl >= 2 else []
                    ev_nrm[jl] = emit_ln(lb1, jl % 2, xo_f32[jl % 2], natq[jl % 2],
                                         [(xp, ev_x)], w_norm)
                    for g in range(8):
                        for q in range(4):
                            dt = g * 4 + q
                            w = []
                            if g == 0 and q == 0:
                                w = [(cV, ev_nrm[jl])]
                            if q == 0 and jl * 8 + g >= 2:
                                gi = jl * 8 + g - 2
                                w.append((cS2, ev_ev[gi // 8][gi % 8]))
                            r = P.op("tensor", lambda e, jl=jl, dt=dt, g=g, q=q: e.matmul(
                                ps_t1[g % 2][:, q * 128:(q + 1) * 128],
                                natq[jl % 2][:, dt * 128:(dt + 1) * 128], ident[:, :],
                                start=True, stop=True),
                                waits=w, inc=cT if q == 3 else None)
                            if q == 3:
                                ev_tr[jl][g] = r
                        ev_ev[jl][g] = P.op("scalar", lambda e, jl=jl, g=g: e.copy(
                            bass.AP(xnT_sb, (g * 4) * 1024 + jl * 128,
                                    [[32768, 128], [1024, 4], [1, 128]]),
                            ps_t1[g % 2][:, :]), waits=[(cT, ev_tr[jl][g])], inc=cS2)
                ev_xnT = ev_ev[7][7]
                hscope.close()
                wscope = ExitStack()
                wbf = [wscope.enter_context(nc.sbuf_tensor(f"wbf{i}_{h}", [128, 32 * 512], BF16))
                       for i in range(2)]

                ev_axev = [0, 0]
                for tb in range(2):
                    r = None
                    for dt in range(32):
                        w = []
                        if dt == 0:
                            w = [(cS2, ev_xnT), (cX2, ev_ss)]
                            if tb == 1:
                                w = [(cS2, ev_axev[0])]
                        r = P.op("tensor", lambda e, tb=tb, dt=dt: e.matmul(
                            ps_ax[:, :], A_stack[:, dt * 64:(dt + 1) * 64],
                            xnT_sb[:, dt * 1024 + tb * 512: dt * 1024 + (tb + 1) * 512],
                            start=(dt == 0), stop=(dt == 31)),
                            waits=w, inc=cT if dt == 31 else None)
                    ev_axev[tb] = P.op("scalar", lambda e, tb=tb: e.copy(
                        AxT[0:64, tb * 512:(tb + 1) * 512], ps_ax[:, :]),
                        waits=[(cT, r)], inc=cS2)
                if debug:
                    P.op("sync", lambda e, h=h: e.dma_start(
                        axt_dbg[h, :, :], AxT[0:65, :]),
                        waits=[(cS2, ev_axev[1]), (cV, ev_ones_ax)], inc=cY, k=16)

                ev_wc = {}
                ev_mm = {}
                ev_rope2 = {}
                ev_rope3 = {}
                ev_qtr = {}
                ev_qev = {}
                ev_qw = {}
                ev_sil = {}
                ev_sg3 = {}
                ev_gw = {}
                ev_r3_last = 0
                for wb in range(16):
                    src = wq if wb < 8 else wkern
                    cb = wb % 8
                    for dt in range(32):
                        i = wb * 32 + dt
                        wp = cW if i % 2 == 0 else cWb
                        w = [(cS, ev_wc[(i - 2) // 32, (i - 2) % 32])] if i >= 2 else []
                        evl = P.op("sync", lambda e, src=src, cb=cb, dt=dt, i=i: e.dma_start(
                            wst4[i % 2][:, :],
                            src[dt * 128:(dt + 1) * 128, cb * 512:(cb + 1) * 512]),
                            waits=w, inc=wp, k=16)
                        w = [(wp, evl)]
                        if wb >= 2 and dt == 0:
                            w.append((cT, ev_mm[wb - 2, 7]))
                        ev_wc[wb, dt] = P.op("scalar", lambda e, wb=wb, dt=dt, i=i: e.copy(
                            wbf[wb % 2][:, dt * 512:(dt + 1) * 512], wst4[i % 2][:, :]),
                            waits=w, inc=cS)
                    ev_wc[wb, 32] = P.op("scalar", lambda e, wb=wb: e.copy(
                        wdum[:, :], wst4[(wb * 32 + 31) % 2][:, 0:8]),
                        waits=[(cS, ev_wc[wb, 31])], inc=cS)
                    for jl in range(8):
                        j = h * 8 + jl
                        ui = wb * 8 + jl
                        r = None
                        for dt in range(32):
                            w = [(cS, ev_wc[wb, 32])]
                            if dt == 0:
                                if ui >= 2:
                                    pw, pj = (ui - 2) // 8, (ui - 2) % 8
                                    if pw < 8:
                                        w.append((cV, ev_rope2[pw, pj]))
                                    else:
                                        w.append((cV, ev_sil[pw, pj]))
                                if wb == 8:
                                    w += [(cS2, ev_axev[1]), (cX2, ev_ss),
                                          (cV, ev_ones_ax)]
                            last = (wb < 8 and dt == 31)
                            r = P.op("tensor", lambda e, wb=wb, jl=jl, dt=dt, last=last: e.matmul(
                                ps_qg[jl % 2][:, :],
                                xnT_sb[:, dt * 1024 + jl * 128: dt * 1024 + jl * 128 + 128],
                                wbf[wb % 2][:, dt * 512:(dt + 1) * 512],
                                start=(dt == 0), stop=last),
                                waits=w, inc=cT if last else None)
                        if wb >= 8:
                            r = P.op("tensor", lambda e, jl=jl, cb=cb: e.matmul(
                                ps_qg[jl % 2][:, :], AxT[:, jl * 128:(jl + 1) * 128],
                                B_stack[:, cb * 512:(cb + 1) * 512],
                                start=False, stop=True), inc=cT)
                        ev_mm[wb, jl] = r
                        if wb < 8:
                            w = [(cT, ev_mm[wb, jl]), (cX2, ev_ss)]
                            if ev_r3_last:
                                w.append((cV, ev_r3_last))
                            e1 = P.op("vector", lambda e, wb=wb, jl=jl: e.tensor_tensor(
                                t1b[:, :], ps_qg[jl % 2][:, :],
                                bass.AP(ccq_sb, jl * 128, [[1024, 128], [0, 4], [1, 128]]),
                                op=OP.mult), waits=w, inc=cV)
                            ev_rope2[wb, jl] = P.op("vector", lambda e, wb=wb, jl=jl: e.tensor_tensor(
                                t2b[:, :],
                                bass.AP(ps_qg[jl % 2], 64,
                                        [[512, 128], [128, 4], [-64, 2], [1, 64]]),
                                bass.AP(ssq_sb, jl * 128, [[1024, 128], [0, 4], [1, 128]]),
                                op=OP.mult), waits=[(cX2, ev_ss)], inc=cV)
                            w = [(cV, ev_rope2[wb, jl])]
                            if ui >= 2:
                                pw, pj = (ui - 2) // 8, (ui - 2) % 8
                                w.append((cT, ev_qtr[pw, pj]))
                            ev_rope3[wb, jl] = P.op("vector", lambda e, jl=jl: e.tensor_add(
                                qr_nat[jl % 2][:, :], t1b[:, :], t2b[:, :]),
                                waits=w, inc=cV)
                            ev_r3_last = ev_rope3[wb, jl]
                            r = None
                            for hh in range(4):
                                w = [(cV, ev_rope3[wb, jl])] if hh == 0 else []
                                if hh == 0 and ui >= 2:
                                    pw, pj = (ui - 2) // 8, (ui - 2) % 8
                                    w.append((cS2, ev_qev[pw, pj]))
                                elif hh == 0:
                                    w.append((cS2, ev_ev[7][6 + jl]))
                                r = P.op("tensor", lambda e, jl=jl, hh=hh: e.matmul(
                                    ps_t1[jl % 2][:, hh * 128:(hh + 1) * 128],
                                    qr_nat[jl % 2][:, hh * 128:(hh + 1) * 128], ident[:, :],
                                    start=True, stop=True),
                                    waits=w, inc=cT if hh == 3 else None)
                            ev_qtr[wb, jl] = r
                            w = [(cT, ev_qtr[wb, jl])]
                            if ui >= 2:
                                pw, pj = (ui - 2) // 8, (ui - 2) % 8
                                w.append((cY if pj % 2 == 0 else cYb, ev_qw[pw, pj]))
                            ev_qev[wb, jl] = P.op("scalar", lambda e, jl=jl: e.copy(
                                qstage[jl % 2][:, :], ps_t1[jl % 2][:, :]), waits=w, inc=cS2)
                            ev_qw[wb, jl] = P.op("gpsimd", lambda e, j=j, wb=wb, jl=jl: e.dma_start(
                                qT_s[j, wb, :, :], qstage[jl % 2][:, :]),
                                waits=[(cS2, ev_qev[wb, jl])],
                                inc=(cY if jl % 2 == 0 else cYb), k=16)
                        else:
                            w = [(cT, ev_mm[wb, jl])]
                            if ui >= 2:
                                pw, pj = (ui - 2) // 8, (ui - 2) % 8
                                if pw >= 8:
                                    w.append((cV, ev_sg3[pw, pj]))
                            ev_se = P.op("scalar", lambda e, jl=jl: e.activation(
                                sge[jl % 2][:, :], ps_qg[jl % 2][:, :], AF.Exp,
                                scale=-1.0), waits=w, inc=cS2)
                            e1 = P.op("vector", lambda e, jl=jl: e.tensor_scalar(
                                sge[jl % 2][:, :], sge[jl % 2][:, :], 1.0, None,
                                op0=OP.add), waits=[(cS2, ev_se)], inc=cV)
                            e2 = P.op("vector", lambda e, jl=jl: e.reciprocal(
                                sge[jl % 2][:, :], sge[jl % 2][:, :]),
                                waits=[(cV, e1)], inc=cV)
                            w = [(cV, e2)]
                            if ui >= 2:
                                pw, pj = (ui - 2) // 8, (ui - 2) % 8
                                if pw >= 8:
                                    w.append((cY2 if pj % 2 == 0 else cZb, ev_gw[pw, pj]))
                            ev_sg3[wb, jl] = P.op("vector", lambda e, jl=jl: e.tensor_tensor(
                                gstage[jl % 2][:, :], ps_qg[jl % 2][:, :],
                                sge[jl % 2][:, :], op=OP.mult),
                                waits=w, inc=cV)
                            ev_sil[wb, jl] = ev_sg3[wb, jl]
                            ev_gw[wb, jl] = P.op("gpsimd", lambda e, j=j, cb=cb, jl=jl: e.dma_start(
                                g_s[j * 128:(j + 1) * 128, cb * 512:(cb + 1) * 512],
                                gstage[jl % 2][:, :]),
                                waits=[(cV, ev_sil[wb, jl])],
                                inc=(cY2 if jl % 2 == 0 else cZb), k=16)
                if debug:
                    P.op("sync", lambda e, h=h: e.dma_start(
                        xnt_dbg[h, :, :], xnT_sb[:, :]),
                        waits=[(cT, ev_mm[15, 7])], inc=cY, k=16)
                    P.op("sync", lambda e, h=h: e.dma_start(
                        bsb_dbg[h, :, :], B_stack[:, :]),
                        waits=[(cT, ev_mm[15, 7])], inc=cY, k=16)
                    for par in range(2):
                        P.op("sync", lambda e, h=h, par=par: e.dma_start(
                            wbf_dbg[h, par, :, :], wbf[par][:, :]),
                            waits=[(cT, ev_mm[15, 7])], inc=cY, k=16)
                wscope.close()
                barrier()

    KT_res = None
    kt_open = [False]
    if "P2" in phases:
        kt_scope = ExitStack()
        kt_open[0] = True
        KT_res = kt_scope.enter_context(nc.sbuf_tensor("KT_res", [128, NKV * T], BF16))
        with ExitStack() as ph:
            sb = lambda n, s, d: ph.enter_context(nc.sbuf_tensor(n, s, d))
            ps = lambda n, s: ph.enter_context(nc.psum_tensor(n, s, F32))
            wk_bf = sb("wk_bf", [128, 32 * KVD], BF16)
            wst2 = [sb(f"p2w{i}", [128, KVD], F32) for i in range(2)]
            natk = [sb(f"p2n{i}", [128, D], BF16) for i in range(2)]
            xnT_t2 = [sb(f"p2t{i}", [128, D], BF16) for i in range(2)]
            cck_t = [sb(f"cck{i}", [128, 128], F32) for i in range(2)]
            ssk_t = [sb(f"ssk{i}", [128, 128], F32) for i in range(2)]
            kr_nat = [sb(f"p2k{i}", [128, KVD], BF16) for i in range(2)]
            kt1 = [sb(f"p2r{i}", [128, KVD], F32) for i in range(2)]
            kt2 = [sb(f"p2s{i}", [128, KVD], F32) for i in range(2)]
            ps_t2 = [ps(f"p2pt{i}", [128, 512]) for i in range(2)]
            ps_k = [ps(f"p2pk{i}", [128, KVD]) for i in range(2)]

            ev_wkc = [0] * 32
            for wt in range(32):
                wp = cW if wt % 2 == 0 else cWb
                w = [(cS, ev_wkc[wt - 2])] if wt >= 2 else []
                ev = P.op("sync", lambda e, wt=wt: e.dma_start(
                    wst2[wt % 2][:, :], wk[wt * 128:(wt + 1) * 128, :]),
                    waits=w, inc=wp, k=16)
                ev_wkc[wt] = P.op("scalar", lambda e, wt=wt: e.copy(
                    wk_bf[:, wt * KVD:(wt + 1) * KVD], wst2[wt % 2][:, :]),
                    waits=[(wp, ev)], inc=cS)

            ev_tr = [[0] * 8 for _ in range(NT)]
            ev_ev = [[0] * 8 for _ in range(NT)]
            ev_kmm = [0] * NT
            ev_rp = [0] * NT
            ev_ktr = [[0] * 2 for _ in range(NT)]
            ev_kev = [[0] * 2 for _ in range(NT)]
            for kt in range(NT):
                xp = cX if kt % 2 == 0 else cXb
                w = [(cT, ev_tr[kt - 2][7])] if kt >= 2 else []
                ev_x = P.op("gpsimd", lambda e, kt=kt: e.dma_start(
                    natk[kt % 2][:, :], xn_s[kt * 128:(kt + 1) * 128, :]),
                    waits=w, inc=xp, k=16)
                for g in range(8):
                    for q in range(4):
                        dt = g * 4 + q
                        w = [(xp, ev_x)] if (g == 0 and q == 0) else []
                        if q == 0:
                            if g >= 2:
                                w.append((cS2, ev_ev[kt][g - 2]))
                            elif kt >= 1:
                                w.append((cS2, ev_kev[kt - 1][g]))
                        r = P.op("tensor", lambda e, kt=kt, dt=dt, g=g, q=q: e.matmul(
                            ps_t2[g % 2][:, q * 128:(q + 1) * 128],
                            natk[kt % 2][:, dt * 128:(dt + 1) * 128], ident[:, :],
                            start=True, stop=True),
                            waits=w, inc=cT if q == 3 else None)
                        if q == 3:
                            ev_tr[kt][g] = r
                    ev_ev[kt][g] = P.op("scalar", lambda e, kt=kt, g=g: e.copy(
                        xnT_t2[kt % 2][:, g * 512:(g + 1) * 512], ps_t2[g % 2][:, :]),
                        waits=[(cT, ev_tr[kt][g])], inc=cS2)
                r = None
                for dt in range(32):
                    for hf in range(2):
                        w = [(cS2, ev_ev[kt][dt // 4]), (cS, ev_wkc[dt])]
                        if dt == 0 and hf == 0 and kt >= 2:
                            w.append((cV, ev_rp[kt - 2]))
                        last = (dt == 31 and hf == 1)
                        r = P.op("tensor", lambda e, kt=kt, dt=dt, hf=hf: e.matmul(
                            ps_k[kt % 2][:, hf * 512:(hf + 1) * 512],
                            xnT_t2[kt % 2][:, dt * 128:(dt + 1) * 128],
                            wk_bf[:, dt * KVD + hf * 512: dt * KVD + (hf + 1) * 512],
                            start=(dt == 0), stop=(dt == 31)),
                            waits=w, inc=cT2 if last else None)
                ev_kmm[kt] = r
                qp = cQa if kt % 2 == 0 else cQb
                wtb = [(cV, ev_rp[kt - 2])] if kt >= 2 else []
                P.op("gpsimd", lambda e, kt=kt: e.dma_start(
                    cck_t[kt % 2][:, :], bass.AP(cck, kt * 128 * 128,
                                                 [[128, 128], [1, 128]])),
                    waits=wtb, inc=qp, k=16)
                ev_tbl = P.op("gpsimd", lambda e, kt=kt: e.dma_start(
                    ssk_t[kt % 2][:, :], bass.AP(ssk, kt * 128 * 128,
                                                 [[128, 128], [1, 128]])),
                    inc=qp, k=16)
                w = [(cT2, ev_kmm[kt]), (qp, ev_tbl)]
                if kt >= 2:
                    w.append((cV, ev_rp[kt - 2]))
                e1 = P.op("vector", lambda e, kt=kt: e.tensor_tensor(
                    kt1[kt % 2][:, :], ps_k[kt % 2][:, :],
                    bass.AP(cck_t[kt % 2], 0, [[128, 128], [0, 8], [1, 128]]),
                    op=OP.mult), waits=w, inc=cV)
                e2 = P.op("vector", lambda e, kt=kt: e.tensor_tensor(
                    kt2[kt % 2][:, :],
                    bass.AP(ps_k[kt % 2], 64, [[1024, 128], [128, 8], [-64, 2], [1, 64]]),
                    bass.AP(ssk_t[kt % 2], 0, [[128, 128], [0, 8], [1, 128]]),
                    op=OP.mult), inc=cV)
                w = [(cV, e2)]
                if kt >= 2:
                    w.append((cT, ev_ktr[kt - 2][1]))
                ev_rp[kt] = P.op("vector", lambda e, kt=kt: e.tensor_add(
                    kr_nat[kt % 2][:, :], kt1[kt % 2][:, :], kt2[kt % 2][:, :]),
                    waits=w, inc=cV)
                for g in range(2):
                    for q in range(4):
                        hh = g * 4 + q
                        w = [(cV, ev_rp[kt])] if (g == 0 and q == 0) else []
                        if q == 0:
                            w.append((cS2, ev_ev[kt][6 + g]))
                        r = P.op("tensor", lambda e, kt=kt, hh=hh, g=g, q=q: e.matmul(
                            ps_t2[g % 2][:, q * 128:(q + 1) * 128],
                            kr_nat[kt % 2][:, hh * 128:(hh + 1) * 128], ident[:, :],
                            start=True, stop=True),
                            waits=w, inc=cT if q == 3 else None)
                        if q == 3:
                            ev_ktr[kt][g] = r
                    ev_kev[kt][g] = P.op("scalar", lambda e, kt=kt, g=g: e.copy(
                        bass.AP(KT_res, (g * 4) * T + kt * 128,
                                [[NKV * T, 128], [T, 4], [1, 128]]),
                        ps_t2[g % 2][:, :]), waits=[(cT, ev_ktr[kt][g])], inc=cS2)
        if debug:
            evd = P.op("sync", lambda e: e.dma_start(kT_dbg[:, :], KT_res[:, :]),
                       waits=[(cS2, cS2.v)], inc=cY, k=16)
        barrier()

    if "P3" in phases:
        with ExitStack() as ph:
            sb = lambda n, s, d: ph.enter_context(nc.sbuf_tensor(n, s, d))
            ps = lambda n, s: ph.enter_context(nc.psum_tensor(n, s, F32))
            V_res = sb("V_res", [128, NT * KVD], BF16)
            mask_sb = sb("mask_sb", [128, 256], BF16)
            qbuf = [sb(f"qb{i}", [128, 8 * 512], BF16) for i in range(2)]
            probs_f = [sb(f"pf{i}", [128, 4096], F32) for i in range(2)]
            probs = [sb(f"pb{i}", [128, 4096], BF16) for i in range(2)]
            pTsb = [sb(f"pT{i}", [128, 512], BF16) for i in range(2)]
            cstage = [sb(f"cst{i}", [128, 512], BF16) for i in range(2)]
            lparts = [sb(f"lp{i}", [128, 4], F32) for i in range(2)]
            lsum = [sb(f"ls{i}", [128, 1], F32) for i in range(2)]
            rl = [sb(f"rl{i}", [128, 1], F32) for i in range(2)]
            negm = [sb(f"ng{i}", [128, 1], F32) for i in range(2)]
            ps_S = [ps(f"psS{i}", [128, 1024]) for i in range(2)]
            ps_T3 = [ps(f"psT3{i}", [128, 512]) for i in range(2)]
            ps_C = ps("psC", [128, 256])

            ev_msk = P.op("gpsimd", lambda e: e.dma_start(mask_sb[:, :], mask_in),
                          inc=cX2, k=16)
            ev_vres = 0
            for kt in range(NT):
                ev_vres = P.op("gpsimd", lambda e, kt=kt: e.dma_start(
                    V_res[:, kt * KVD:(kt + 1) * KVD], vn_s[kt * 128:(kt + 1) * 128, :]),
                    inc=cX2, k=16)
            ev_qld = {}
            cQ = [cQa, cQb]
            for wb in range(8):
                ev_qld[0] = P.op("gpsimd", lambda e, wb=wb: e.dma_start(
                    qbuf[0][:, wb * 512:(wb + 1) * 512], qT_s[0, wb, :, :]),
                    inc=cQ[0], k=16)

            NU = NQT * NH
            bank_user = [0, 0]
            ev_chunk = {}
            ev_max = {}
            ev_exp = {}
            ev_explast = {}
            ev_scale = {}
            ev_ttail = {}
            ev_pev = {}
            ev_cgrp = {}
            ev_ctx = {}
            ev_cev = {}
            ev_lsum = {}
            ev_cw = {}
            ev_sc_last_q = [0, 0]

            psT_free = [None, None]
            pT_free = [0, 0]

            def emit_tc(u):
                j, hh = u // NH, u % NH
                L, rounds = slot_rounds(j)
                nkt = L // 128
                hkv = hh // 4
                ectr = cS2 if u % 2 == 0 else cV2
                eeng = "scalar" if u % 2 == 0 else "vector"
                r = None
                for g in range((nkt + 3) // 4):
                    b = g % 2
                    qn = min(4, nkt - g * 4)
                    for q in range(qn):
                        kt = g * 4 + q
                        w = [(cV, ev_scale[u])] if kt == 0 else []
                        if q == 0 and psT_free[b] is not None:
                            w.append(psT_free[b])
                        r = P.op("tensor", lambda e, u=u, kt=kt, b=b: e.matmul(
                            ps_T3[b][:, (kt % 4) * 128:(kt % 4 + 1) * 128],
                            probs[u % 2][:, kt * 128:(kt + 1) * 128], ident[:, :],
                            start=True, stop=True),
                            waits=w, inc=cT2 if q == qn - 1 else None)
                    ev_ttail[u] = r
                    w = [(cT2, r)]
                    if pT_free[b]:
                        w.append((cT2, pT_free[b]))
                    def _evac(e, b=b, qn=qn):
                        if hasattr(e, "tensor_copy"):
                            return e.tensor_copy(pTsb[b][:, :qn * 128],
                                                 ps_T3[b][:, :qn * 128])
                        return e.copy(pTsb[b][:, :qn * 128], ps_T3[b][:, :qn * 128])
                    ev_pev[(u, g)] = (ectr, P.op(eeng, _evac, waits=w, inc=ectr))
                    psT_free[b] = ev_pev[(u, g)]
                    r2 = None
                    for q in range(qn):
                        kt = g * 4 + q
                        w = [ev_pev[(u, g)]]
                        if kt == 0:
                            w.append((cX2, ev_vres))
                            if u >= 2:
                                w.append((cV, ev_cev[u - 2]))
                        r2 = P.op("tensor", lambda e, u=u, kt=kt, hkv=hkv, b=b, nkt=nkt: e.matmul(
                            ps_C[:, (u % 2) * 128:(u % 2) * 128 + 128],
                            V_res[:, kt * KVD + hkv * 128: kt * KVD + hkv * 128 + 128],
                            pTsb[b][:, (kt % 4) * 128:(kt % 4 + 1) * 128],
                            start=(kt == 0), stop=(kt == nkt - 1)),
                            waits=w, inc=cT2 if q == qn - 1 else None)
                    ev_cgrp[(u, g)] = r2
                    pT_free[b] = r2
                ev_ctx[u] = r2
                grp = hh // 4
                w = [(cT2, ev_ctx[u])]
                pkey = (j, grp - 2) if grp >= 2 else (j - 1, 8 + grp - 2)
                if pkey in ev_cw:
                    w.append((cY if pkey[1] % 2 == 0 else cYb, ev_cw[pkey]))
                ev_cev[u] = P.op("vector", lambda e, u=u, hh=hh, grp=grp: e.tensor_copy(
                    cstage[grp % 2][:, (hh % 4) * 128:(hh % 4 + 1) * 128],
                    ps_C[:, (u % 2) * 128:(u % 2) * 128 + 128]),
                    waits=w, inc=cV)
                if hh % 4 == 3:
                    ev_cw[(j, grp)] = P.op("sync", lambda e, j=j, grp=grp: e.dma_start(
                        ctxT_s[j, grp, :, :], cstage[grp % 2][:, :]),
                        waits=[(cV, ev_cev[u])],
                        inc=(cY if grp % 2 == 0 else cYb), k=16)

            for u in range(NU):
                j, hh = u // NH, u % NH
                L, rounds = slot_rounds(j)
                nr = len(rounds)
                hkv = hh // 4
                if hh == 16 and j + 1 < NQT:
                    for wb in range(8):
                        w = [(cT, ev_sc_last_q[(j + 1) % 2])] if wb == 0 else []
                        ev_qld[j + 1] = P.op("gpsimd", lambda e, j=j, wb=wb: e.dma_start(
                            qbuf[(j + 1) % 2][:, wb * 512:(wb + 1) * 512],
                            qT_s[j + 1, wb, :, :]), waits=w, inc=cQ[(j + 1) % 2], k=16)
                def emit_round(r):
                    r_off, r_w, chunks = rounds[r]
                    bank = r % 2
                    last = None
                    for ci, (c_off, c_w) in enumerate(chunks):
                        w = []
                        if ci == 0:
                            if bank_user[bank]:
                                w.append((cS2, bank_user[bank]))
                            w.append((cQ[j % 2], ev_qld[j]))
                        has_mask = (r == nr - 1 and ci == len(chunks) - 1)
                        r1 = P.op("tensor", lambda e, u=u, r=r, c_off=c_off, c_w=c_w,
                                  bank=bank, r_off=r_off, hkv=hkv, hh=hh, j=j: e.matmul(
                            ps_S[bank][:, c_off - r_off: c_off - r_off + c_w],
                            qbuf[j % 2][:, (hh // 4) * 512 + (hh % 4) * 128:
                                        (hh // 4) * 512 + (hh % 4) * 128 + 128],
                            KT_res[:, hkv * T + c_off: hkv * T + c_off + c_w],
                            start=True, stop=not has_mask),
                            waits=w, inc=None if has_mask else cT)
                        if has_mask:
                            r1 = P.op("tensor", lambda e, u=u, bank=bank, L=L, r_off=r_off: e.matmul(
                                ps_S[bank][:, L - 256 - r_off: L - r_off],
                                ident[:, :], mask_sb[:, :],
                                start=False, stop=True),
                                waits=[(cX2, ev_vres)], inc=cT)
                        last = r1
                    ev_chunk[(u, r)] = last
                    return last

                for r in range(min(2, nr)):
                    emit_round(r)
                w = [(cT, ev_chunk[(u, 0)])]
                if u >= 2:
                    w.append((cS2, ev_explast[u - 2]))
                ev_max[u] = P.op("vector", lambda e, u=u, L=L: e.reduce_max(
                    negm[u % 2][:, :], ps_S[0][:, 0:min(L, 512)], axis=AX, negate=True),
                    waits=w, inc=cV)
                def emit_exp(r):
                    r_off, r_w, chunks = rounds[r]
                    bank = r % 2
                    w = [(cT, ev_chunk[(u, r)]), (cV, ev_max[u])]
                    if u >= 2 and r == 0:
                        w.append((cV, ev_scale.get(u - 2, 0)))
                        w.append((cV, ev_lsum.get(u - 2, 0)))
                    ev_exp[(u, r)] = P.op("scalar", lambda e, u=u, r=r, r_off=r_off,
                                          r_w=r_w, bank=bank: e.activation(
                        probs_f[u % 2][:, r_off:r_off + r_w], ps_S[bank][:, 0:r_w],
                        AF.Exp, bias=negm[u % 2][:, :1],
                        accum_out=lparts[u % 2][:, r:r + 1]),
                        waits=w, inc=cS2)
                    bank_user[bank] = ev_exp[(u, r)]

                for r in range(min(2, nr)):
                    emit_exp(r)
                for r in range(2, nr):
                    emit_round(r)
                    emit_exp(r)
                if u >= 1:
                    emit_tc(u - 1)
                ev_explast[u] = ev_exp[(u, nr - 1)]
                ev_sc_last_q[j % 2] = cT.v
                ev_lsum[u] = P.op("vector", lambda e, u=u, nr=nr: e.reduce_sum(
                    lsum[u % 2][:, :], lparts[u % 2][:, 0:nr], axis=AX),
                    waits=[(cS2, ev_explast[u])], inc=cV)
                ev_rl = P.op("vector", lambda e, u=u: e.reciprocal(
                    rl[u % 2][:, :], lsum[u % 2][:, :]),
                    waits=[(cV, ev_lsum[u])], inc=cV)
                w = [(cV, ev_rl)]
                if u >= 2:
                    w.append((cT2, ev_ttail.get(u - 2, 0)))
                ev_scale[u] = P.op("vector", lambda e, u=u, L=L: e.tensor_scalar(
                    probs[u % 2][:, 0:L], probs_f[u % 2][:, 0:L], rl[u % 2][:, :1], None,
                    op0=OP.mult), waits=w, inc=cV)
            emit_tc(NU - 1)
        barrier()
        if kt_open[0]:
            kt_scope.close()
            kt_open[0] = False

    if "P5" in phases:
        with ExitStack() as ph:
            sb = lambda n, s, d: ph.enter_context(nc.sbuf_tensor(n, s, d))
            ps = lambda n, s: ph.enter_context(nc.psum_tensor(n, s, F32))
            ctx_sb = sb("ctx_sb", [128, 8 * 8 * 512], BF16)
            wst45 = [sb(f"o4_{i}", [128, 512], F32) for i in range(2)]
            wdum5 = sb("wdum5", [128, 8], F32)
            wobf = [sb(f"wobf{i}", [128, 32 * 512], BF16) for i in range(2)]
            gbuf = [sb(f"gb{i}", [128, 512], F32) for i in range(2)]
            xbuf = [sb(f"xb{i}", [128, 512], F32) for i in range(2)]
            tbuf = [sb(f"tb{i}", [128, 512], F32) for i in range(2)]
            obuf = [sb(f"ob{i}", [128, 512], F32) for i in range(2)]
            ps_s = [ps(f"pss{i}", [128, 512]) for i in range(2)]

            for h in range(2):
                ev_ctxl = 0
                for jl in range(8):
                    for wb in range(8):
                        j = h * 8 + jl
                        ev_ctxl = P.op("gpsimd", lambda e, j=j, jl=jl, wb=wb: e.dma_start(
                            ctx_sb[:, (jl * 8 + wb) * 512:(jl * 8 + wb + 1) * 512],
                            ctxT_s[j, wb, :, :]), inc=cX2, k=16)
                ev_wc = {}
                ev_mm = {}
                ev_t1 = {}
                ev_out = {}
                ev_yw = {}
                ev_gx = {}
                for wb in range(8):
                    for dt in range(32):
                        i = wb * 32 + dt
                        wp = cW if i % 2 == 0 else cWb
                        w = [(cS, ev_wc[(i - 2) // 32, (i - 2) % 32])] if i >= 2 else []
                        evl = P.op("sync", lambda e, wb=wb, dt=dt, i=i: e.dma_start(
                            wst45[i % 2][:, :],
                            wo[dt * 128:(dt + 1) * 128, wb * 512:(wb + 1) * 512]),
                            waits=w, inc=wp, k=16)
                        w = [(wp, evl)]
                        if wb >= 2 and dt == 0:
                            w.append((cT, ev_mm[wb - 2, 7]))
                        ev_wc[wb, dt] = P.op("scalar", lambda e, wb=wb, dt=dt, i=i: e.copy(
                            wobf[wb % 2][:, dt * 512:(dt + 1) * 512], wst45[i % 2][:, :]),
                            waits=w, inc=cS)
                    ev_wc[wb, 32] = P.op("scalar", lambda e, wb=wb: e.copy(
                        wdum5[:, :], wst45[(wb * 32 + 31) % 2][:, 0:8]),
                        waits=[(cS, ev_wc[wb, 31])], inc=cS)
                    for jl in range(8):
                        j = h * 8 + jl
                        ui = wb * 8 + jl
                        w = []
                        if ui >= 2:
                            pw, pj = (ui - 2) // 8, (ui - 2) % 8
                            w.append((cV, ev_t1[pw, pj]))
                        ev_g = P.op("gpsimd", lambda e, j=j, wb=wb, jl=jl: e.dma_start(
                            gbuf[jl % 2][:, :],
                            g_s[j * 128:(j + 1) * 128, wb * 512:(wb + 1) * 512]),
                            waits=w, inc=(cX if jl % 2 == 0 else cXb), k=16)
                        w2 = []
                        if ui >= 2:
                            pw, pj = (ui - 2) // 8, (ui - 2) % 8
                            w2.append((cV, ev_out[pw, pj]))
                        ev_xl = P.op("gpsimd", lambda e, j=j, wb=wb, jl=jl: e.dma_start(
                            xbuf[jl % 2][:, :],
                            x_own[j * 128:(j + 1) * 128, wb * 512:(wb + 1) * 512]),
                            waits=w2, inc=(cQa if jl % 2 == 0 else cQb), k=16)
                        ev_gx[wb, jl] = (ev_g, ev_xl)
                        r = None
                        for dt in range(32):
                            w = [(cS, ev_wc[wb, 32])]
                            if dt == 0:
                                w.append((cX2, ev_ctxl))
                                if ui >= 2:
                                    pw, pj = (ui - 2) // 8, (ui - 2) % 8
                                    w.append((cV, ev_t1[pw, pj]))
                            r = P.op("tensor", lambda e, wb=wb, jl=jl, dt=dt: e.matmul(
                                ps_s[jl % 2][:, :],
                                ctx_sb[:, (jl * 8 + dt // 4) * 512 + (dt % 4) * 128:
                                       (jl * 8 + dt // 4) * 512 + (dt % 4) * 128 + 128],
                                wobf[wb % 2][:, dt * 512:(dt + 1) * 512],
                                start=(dt == 0), stop=(dt == 31)),
                                waits=w, inc=cT if dt == 31 else None)
                        ev_mm[wb, jl] = r
                        w = [(cT, r), (cX if jl % 2 == 0 else cXb, ev_g)]
                        if ui >= 2:
                            pw, pj = (ui - 2) // 8, (ui - 2) % 8
                            w.append((cV, ev_out[pw, pj]))
                        ev_t1[wb, jl] = P.op("vector", lambda e, jl=jl: e.tensor_tensor(
                            tbuf[jl % 2][:, :], ps_s[jl % 2][:, :], gbuf[jl % 2][:, :],
                            op=OP.mult), waits=w, inc=cV)
                        w = [(cV, ev_t1[wb, jl]),
                             (cQa if jl % 2 == 0 else cQb, ev_xl)]
                        if ui >= 2:
                            pw, pj = (ui - 2) // 8, (ui - 2) % 8
                            w.append((cY if pj % 2 == 0 else cYb, ev_yw[pw, pj]))
                        ev_out[wb, jl] = P.op("vector", lambda e, jl=jl: e.tensor_add(
                            obuf[jl % 2][:, :], tbuf[jl % 2][:, :], xbuf[jl % 2][:, :]),
                            waits=w, inc=cV)
                        ev_yw[wb, jl] = P.op("gpsimd", lambda e, j=j, wb=wb, jl=jl: e.dma_start(
                            y[j * 128:(j + 1) * 128, wb * 512:(wb + 1) * 512],
                            obuf[jl % 2][:, :]),
                            waits=[(cV, ev_out[wb, jl])],
                            inc=(cY if jl % 2 == 0 else cYb), k=16)
                barrier()

    if KT_res is not None and kt_open[0]:
        kt_scope.close()
        kt_open[0] = False

    P.op("sync", None, waits=[(cY, cY.v), (cYb, cYb.v), (cY2, cY2.v),
                              (cZb, cZb.v), (cW2, cW2.v)])
    barrier()

    nums = sorted(h.num for h in sem_handles)
    assert nums == list(range(nums[0], nums[0] + len(nums))), nums
    srange = range(nums[0], nums[-1] + 1)
    nc.gpsimd.dma_reset(srange)
    nc.gpsimd.sem_clear(srange)
    nc.all_engine_barrier()

    with nc.Block() as block:
        @block.tensor
        def _(eng):
            P.emit("tensor", eng)

        @block.vector
        def _(eng):
            P.emit("vector", eng)

        @block.scalar
        def _(eng):
            P.emit("scalar", eng)

        @block.sync
        def _(eng):
            P.emit("sync", eng)

        @block.gpsimd
        def _(eng):
            P.emit("gpsimd", eng)

    es.close()
    return nc


def _rope_tables():
    inv = 1.0 / (500000.0 ** (np.arange(0, HD, 2, dtype=np.float64) / HD))
    t = np.arange(T, dtype=np.float64)
    fr = np.outer(t, inv)
    cos = np.cos(fr).astype(np.float32)
    sin = np.sin(fr).astype(np.float32)
    cc = np.concatenate([cos, cos], axis=1)
    ss = np.concatenate([-sin, sin], axis=1)
    return cc, ss


def _mask_for_parity(p):
    m = np.zeros((128, 256), np.float32)
    qi, ci = np.meshgrid(np.arange(128), np.arange(256), indexing="ij")
    if p == 1:
        m[(ci >= 128) & (ci - 128 > qi)] = NEGBIG
    else:
        m[(ci < 128) & (ci > qi)] = NEGBIG
        m[ci >= 128] = NEGBIG
    return m.astype(BF16NP)


def prep_in_maps(L0_attn_x, L0_attn_Wq, L0_attn_Wk, L0_attn_Wv, L0_attn_Wo,
                 L0_ffn_gate, L0_ffn_kernel, L0_ffn_bias, L0_ffn_lora_A,
                 L0_ffn_lora_B, L0_ffn_lora_A_shared, L0_ffn_lora_B_shared):
    xin = np.ascontiguousarray(np.asarray(L0_attn_x, dtype=np.float32))
    wq = np.ascontiguousarray(np.asarray(L0_attn_Wq, dtype=np.float32))
    wk = np.ascontiguousarray(np.asarray(L0_attn_Wk, dtype=np.float32))
    wv = np.ascontiguousarray(np.asarray(L0_attn_Wv, dtype=np.float32))
    wo = np.ascontiguousarray(np.asarray(L0_attn_Wo, dtype=np.float32))
    gate = np.asarray(L0_ffn_gate, dtype=np.float32)
    wkern = np.ascontiguousarray(np.asarray(L0_ffn_kernel, dtype=np.float32))
    bias = np.asarray(L0_ffn_bias, dtype=np.float32).reshape(1, D)
    lora_A = np.asarray(L0_ffn_lora_A, dtype=np.float32)
    lora_B = np.asarray(L0_ffn_lora_B, dtype=np.float32)
    As = np.asarray(L0_ffn_lora_A_shared, dtype=np.float32)
    Bs = np.asarray(L0_ffn_lora_B_shared, dtype=np.float32)

    loraA_dev = np.ascontiguousarray(
        lora_A.reshape(NLORA, RANK, 32, 128).transpose(0, 3, 2, 1).reshape(NLORA, 128, 32 * RANK))
    As_dev = np.ascontiguousarray(As.reshape(RANK, 32, 128).transpose(2, 1, 0).reshape(128, 32 * RANK))
    loraB_dev = np.ascontiguousarray(
        lora_B.transpose(0, 2, 1).reshape(NLORA, 128, 32 * RANK))
    Bs_fold = np.ascontiguousarray(Bs.T.reshape(128, 32 * RANK))
    bias_f = np.ascontiguousarray(bias.reshape(128, 32))
    identity = np.eye(128, dtype=np.float32).astype(BF16NP)
    cc, ss = _rope_tables()
    scale = 1.0 / np.sqrt(HD)
    in_maps = []
    for c in range(NCORES):
        b, p = c // 2, c % 2
        tiles = [2 * j + p for j in range(NQT)]
        rows = np.concatenate([np.arange(t * 128, t * 128 + 128) for t in tiles])
        x_b = np.ascontiguousarray(xin[b])
        in_maps.append({
            "x": x_b,
            "x_own": np.ascontiguousarray(x_b[rows]),
            "wq": wq, "wk": wk, "wv": wv, "wo": wo, "wkern": wkern,
            "bias_fold": bias_f,
            "gate2": np.ascontiguousarray(gate[[2 * b, 2 * b + 1]].reshape(1, 2 * NLORA)),
            "loraA_dev": loraA_dev, "As_dev": As_dev,
            "loraB_dev": loraB_dev, "Bs_fold": Bs_fold,
            "ident_in": identity,
            "mask_in": _mask_for_parity(p),
            "ccq": np.ascontiguousarray(cc[rows] * scale),
            "ssq": np.ascontiguousarray(ss[rows] * scale),
            "cck": cc, "ssk": ss,
        })
    return in_maps


def assemble_out(results):
    out = np.empty((B, T, D), np.float32)
    for c in range(NCORES):
        b, p = c // 2, c % 2
        yc = np.asarray(results[c]["y"])
        for j in range(NQT):
            t = 2 * j + p
            out[b, t * 128:(t + 1) * 128, :] = yc[j * 128:(j + 1) * 128, :]
    return out


def kernel(**inputs):
    in_maps = prep_in_maps(**inputs)
    nc = build_program()
    res = run_bass_kernel_spmd(nc, in_maps, core_ids=list(range(NCORES)))
    return assemble_out(res.results)


# version 03 (final)
# speedup vs baseline: 1.1281x; 1.0725x over previous
import numpy as np
import ml_dtypes
from contextlib import ExitStack

import concourse.bass as bass
import concourse.mybir as mybir
from concourse.bass_utils import run_bass_kernel_spmd

F32 = mybir.dt.float32
BF16 = mybir.dt.bfloat16
AX = mybir.AxisListType.X
AF = mybir.ActivationFunctionType
OP = mybir.AluOpType
BF16NP = ml_dtypes.bfloat16

B, T, D = 4, 4096, 4096
NCORES = 8
NH, NKV, HD = 32, 8, 128
KVD = NKV * HD
NT = T // 128
NQT = 16
RANK = 32
NLORA = 16
EPS = 1e-5
NEGBIG = -30000.0
ENGINES = ("tensor", "vector", "scalar", "sync", "gpsimd")


class Ctr:
    def __init__(self, sem, name):
        self.sem = sem
        self.name = name
        self.v = 0


class Prog:

    def __init__(self):
        self.ops = []

    def op(self, engine, fn, waits=(), inc=None, k=1):
        w = [(c, v) for (c, v) in waits if c is not None and v and v > 0]
        self.ops.append((engine, fn, w, inc, k))
        if inc is not None:
            inc.v += k
            return inc.v
        return None

    def emit(self, engine_name, eng):
        last_wait = {}
        for e, fn, waits, inc, k in self.ops:
            if e != engine_name:
                continue
            for ctr, val in waits:
                if last_wait.get(id(ctr), 0) < val:
                    eng.wait_ge(ctr.sem, val)
                    last_wait[id(ctr)] = val
            if fn is None:
                continue
            ins = fn(eng)
            if inc is not None:
                ins.then_inc(inc.sem, k)


def slot_rounds(j):
    L = 256 * (j + 1)
    rounds = []
    off = 0
    while off < L:
        w = min(1024, L - off)
        chunks = []
        o = off
        while o < off + w:
            cw = min(512, off + w - o)
            chunks.append((o, cw))
            o += cw
        rounds.append((off, w, chunks))
        off += w
    return L, rounds


def build_program(phases=("P0", "P1", "P2", "P3", "P5"), debug=False):
    nc = bass.Bass("TRN2", target_bir_lowering=False)

    x = nc.dram_tensor("x", [T, D], F32, kind="ExternalInput").ap()
    x_own = nc.dram_tensor("x_own", [NQT * 128, D], F32, kind="ExternalInput").ap()
    wq = nc.dram_tensor("wq", [D, D], F32, kind="ExternalInput").ap()
    wk = nc.dram_tensor("wk", [D, KVD], F32, kind="ExternalInput").ap()
    wv = nc.dram_tensor("wv", [D, KVD], F32, kind="ExternalInput").ap()
    wo = nc.dram_tensor("wo", [D, D], F32, kind="ExternalInput").ap()
    wkern = nc.dram_tensor("wkern", [D, D], F32, kind="ExternalInput").ap()
    bias_fold = nc.dram_tensor("bias_fold", [128, 32], F32, kind="ExternalInput").ap()
    gate2 = nc.dram_tensor("gate2", [1, 2 * NLORA], F32, kind="ExternalInput").ap()
    loraA_dev = nc.dram_tensor("loraA_dev", [NLORA, 128, 32 * RANK], F32, kind="ExternalInput").ap()
    As_dev = nc.dram_tensor("As_dev", [128, 32 * RANK], F32, kind="ExternalInput").ap()
    loraB_dev = nc.dram_tensor("loraB_dev", [NLORA, 128, 32 * RANK], F32, kind="ExternalInput").ap()
    Bs_fold = nc.dram_tensor("Bs_fold", [128, 32 * RANK], F32, kind="ExternalInput").ap()
    ident_in = nc.dram_tensor("ident_in", [128, 128], BF16, kind="ExternalInput").ap()
    mask_in = nc.dram_tensor("mask_in", [128, 256], BF16, kind="ExternalInput").ap()
    ccq = nc.dram_tensor("ccq", [NQT * 128, 128], F32, kind="ExternalInput")
    ssq = nc.dram_tensor("ssq", [NQT * 128, 128], F32, kind="ExternalInput")
    cck = nc.dram_tensor("cck", [T, 128], F32, kind="ExternalInput")
    ssk = nc.dram_tensor("ssk", [T, 128], F32, kind="ExternalInput")

    sk = "ExternalOutput" if debug else "Internal"
    xn_s = nc.dram_tensor("xn_s", [T, D], BF16, kind=sk).ap()
    vn_s = nc.dram_tensor("vn_s", [T, KVD], BF16, kind=sk).ap()
    qT_s = nc.dram_tensor("qT_s", [NQT, 8, 128, 512], BF16, kind=sk).ap()
    g_s = nc.dram_tensor("g_s", [NQT * 128, D], F32, kind=sk).ap()
    ctxT_s = nc.dram_tensor("ctxT_s", [NQT, 8, 128, 512], BF16, kind=sk).ap()
    kT_dbg = nc.dram_tensor("kT_dbg", [128, NKV * T], BF16,
                            kind="ExternalOutput" if debug else "Internal").ap()
    astk_s = nc.dram_tensor("astk_s", [2, 128, 2048], BF16, kind=sk).ap()
    axt_dbg = nc.dram_tensor("axt_dbg", [2, 65, 1024], BF16,
                             kind="ExternalOutput" if debug else "Internal").ap()
    bsb_dbg = nc.dram_tensor("bsb_dbg", [2, 128, D], BF16,
                             kind="ExternalOutput" if debug else "Internal").ap()
    wbf_dbg = nc.dram_tensor("wbf_dbg", [2, 2, 128, 32 * 512], BF16,
                             kind="ExternalOutput" if debug else "Internal").ap()
    xnt_dbg = nc.dram_tensor("xnt_dbg", [2, 128, 32 * 1024], BF16,
                             kind="ExternalOutput" if debug else "Internal").ap()
    bstk_s_h = nc.dram_tensor("bstk_s", [2 * 65 * D], BF16, kind=sk)
    bstk_s = bstk_s_h.ap()
    y = nc.dram_tensor("y", [NQT * 128, D], F32, kind="ExternalOutput").ap()

    es = ExitStack()
    sem_handles = []

    def SEM(name):
        h = es.enter_context(nc.semaphore(name))
        sem_handles.append(h)
        return h

    P = Prog()
    cW = Ctr(SEM("sW"), "W")
    cWb = Ctr(SEM("sWb"), "Wb")
    cWc = Ctr(SEM("sWc"), "Wc")
    cWd = Ctr(SEM("sWd"), "Wd")
    cW2 = Ctr(SEM("sW2"), "W2")
    cX2 = Ctr(SEM("sX2"), "X2")
    cXb = Ctr(SEM("sXb"), "Xb")
    cQa = Ctr(SEM("sQa"), "Qa")
    cQb = Ctr(SEM("sQb"), "Qb")
    cY2 = Ctr(SEM("sY2"), "Y2")
    cYb = Ctr(SEM("sYb"), "Yb")
    cZb = Ctr(SEM("sZb"), "Zb")
    cX = Ctr(SEM("sX"), "X")
    cY = Ctr(SEM("sY"), "Y")
    cT = Ctr(SEM("sT"), "T")
    cT2 = Ctr(SEM("sT2"), "T2")
    cV = Ctr(SEM("sV"), "V")
    cV2 = Ctr(SEM("sV2"), "V2")
    cS = Ctr(SEM("sS"), "S")
    cS2 = Ctr(SEM("sS2"), "S2")
    cB = Ctr(SEM("sB"), "B")

    ident = es.enter_context(nc.sbuf_tensor("identity", [128, 128], BF16))
    ones_bf = es.enter_context(nc.sbuf_tensor("ones_bf", [1, 128], BF16))
    eps_sb = es.enter_context(nc.sbuf_tensor("eps_sb", [128, 1], F32))

    def barrier():
        allw = [(cW, cW.v), (cWb, cWb.v), (cWc, cWc.v), (cWd, cWd.v),
                (cW2, cW2.v), (cY, cY.v),
                (cYb, cYb.v), (cY2, cY2.v), (cZb, cZb.v), (cX, cX.v),
                (cXb, cXb.v), (cX2, cX2.v), (cQa, cQa.v), (cQb, cQb.v)]
        P.op("sync", None, waits=list(allw))
        P.op("gpsimd", None, waits=list(allw))
        for name in ENGINES:
            P.op(name, lambda eng: eng.nop(), inc=cB)
        tgt = cB.v
        for name in ENGINES:
            P.op(name, None, waits=[(cB, tgt)])

    ev_ident = P.op("sync", lambda e: e.dma_start(ident[:, :], ident_in), inc=cW2, k=16)
    P.op("vector", lambda e: e.memset(eps_sb[:, :], EPS), inc=cV)
    ev_ones = P.op("vector", lambda e: e.memset(ones_bf[:, :], 1.0), inc=cV)

    def make_ln_bufs(ph_sb, tag):
        return dict(
            stats=[ph_sb(f"{tag}st{i}", [128, 48], F32) for i in range(2)],
            mv=[ph_sb(f"{tag}mv{i}", [128, 2], F32) for i in range(2)],
            lnv=[ph_sb(f"{tag}ln{i}", [128, 1], F32) for i in range(2)],
            rstd=[ph_sb(f"{tag}rs{i}", [128, 1], F32) for i in range(2)],
            nms=[ph_sb(f"{tag}nm{i}", [128, 1], F32) for i in range(2)],
        )

    def emit_ln(lb, par, x_buf, xn_buf, w_first, w_norm):
        evb = None
        for r in range(8):
            w = list(w_first) if r == 0 else []
            evb = P.op("vector", lambda e, r=r: e.bn_stats(
                lb["stats"][par][:, r * 6:(r + 1) * 6],
                x_buf[:, r * 512:(r + 1) * 512]), waits=w, inc=cV)
        ev_ag = P.op("vector", lambda e: e.bn_aggr(
            lb["mv"][par][:, :], lb["stats"][par][:, :]), waits=[(cV, evb)], inc=cV)
        ev_ln = P.op("scalar", lambda e: e.activation(
            lb["lnv"][par][:, :], lb["mv"][par][:, 1:2], AF.Ln, bias=eps_sb[:, :1],
            scale=1.0), waits=[(cV, ev_ag)], inc=cS2)
        ev_rs = P.op("scalar", lambda e: e.activation(
            lb["rstd"][par][:, :], lb["lnv"][par][:, :], AF.Exp, scale=-0.5),
            waits=[(cS2, ev_ln)], inc=cS2)
        ev_nm = P.op("vector", lambda e: e.scalar_tensor_tensor(
            lb["nms"][par][:, :], lb["mv"][par][:, :1], -1.0, lb["rstd"][par][:, :1],
            op0=OP.mult, op1=OP.mult), waits=[(cS2, ev_rs)], inc=cV)
        return P.op("vector", lambda e: e.tensor_scalar(
            xn_buf[:, :], x_buf[:, :], lb["rstd"][par][:, :1], lb["nms"][par][:, :1],
            op0=OP.mult, op1=OP.add), waits=[(cV, ev_nm)] + list(w_norm), inc=cV)

    if "P0" in phases:
        with ExitStack() as ph:
            sb = lambda n, s, d: ph.enter_context(nc.sbuf_tensor(n, s, d))
            ps = lambda n, s: ph.enter_context(nc.psum_tensor(n, s, F32))
            wv_bf = sb("wv_bf", [128, 32 * KVD], BF16)
            wst = [sb(f"p0w{i}", [128, KVD], F32) for i in range(2)]
            x_f32 = [sb(f"p0x{i}", [128, D], F32) for i in range(2)]
            xn_nat = [sb(f"p0n{i}", [128, D], BF16) for i in range(2)]
            xnT_t = [sb(f"p0t{i}", [128, D], BF16) for i in range(2)]
            v_bf = [sb(f"p0v{i}", [128, KVD], BF16) for i in range(2)]
            lb = make_ln_bufs(sb, "p0")
            ps_t = [ps(f"p0pt{i}", [128, 512]) for i in range(2)]
            ps_v = [ps(f"p0pv{i}", [128, KVD]) for i in range(2)]

            ev_wvc = [0] * 32
            for wt in range(32):
                wp = cW if wt % 2 == 0 else cWb
                w = [(cS, ev_wvc[wt - 2])] if wt >= 2 else []
                ev = P.op("sync", lambda e, wt=wt: e.dma_start(
                    wst[wt % 2][:, :], wv[wt * 128:(wt + 1) * 128, :]),
                    waits=w, inc=wp, k=16)
                ev_wvc[wt] = P.op("scalar", lambda e, wt=wt: e.copy(
                    wv_bf[:, wt * KVD:(wt + 1) * KVD], wst[wt % 2][:, :]),
                    waits=[(wp, ev)], inc=cS)

            ev_norm = [0] * NT
            ev_xnw = [0] * NT
            ev_tr = [[0] * 8 for _ in range(NT)]
            ev_evT = [[0] * 8 for _ in range(NT)]
            ev_vmm = [0] * NT
            ev_evV = [0] * NT
            ev_vw = [0] * NT
            for t in range(NT):
                xp = cX if t % 2 == 0 else cXb
                yp = cY if t % 2 == 0 else cYb
                w = [(cV, ev_norm[t - 2])] if t >= 2 else []
                ev_x = P.op("gpsimd", lambda e, t=t: e.dma_start(
                    x_f32[t % 2][:, :], x[t * 128:(t + 1) * 128, :]),
                    waits=w, inc=xp, k=16)
                w_norm = []
                if t >= 2:
                    w_norm = [(yp, ev_xnw[t - 2]), (cT, ev_tr[t - 2][7])]
                ev_norm[t] = emit_ln(lb, t % 2, x_f32[t % 2], xn_nat[t % 2],
                                     [(xp, ev_x)], w_norm)
                ev_xnw[t] = P.op("sync", lambda e, t=t: e.dma_start(
                    xn_s[t * 128:(t + 1) * 128, :], xn_nat[t % 2][:, :]),
                    waits=[(cV, ev_norm[t])], inc=yp, k=16)
                for g in range(8):
                    for q in range(4):
                        dt = g * 4 + q
                        w = []
                        if g == 0 and q == 0:
                            w = [(cV, ev_norm[t]), (cW2, ev_ident)]
                        if q == 0 and t * 8 + g >= 2:
                            gi = t * 8 + g - 2
                            w.append((cS2, ev_evT[gi // 8][gi % 8]))
                        r = P.op("tensor", lambda e, t=t, dt=dt, g=g, q=q: e.matmul(
                            ps_t[g % 2][:, q * 128:(q + 1) * 128],
                            xn_nat[t % 2][:, dt * 128:(dt + 1) * 128], ident[:, :],
                            start=True, stop=True),
                            waits=w, inc=cT if q == 3 else None)
                        if q == 3:
                            ev_tr[t][g] = r
                    w = [(cT, ev_tr[t][g])]
                    if g == 0 and t >= 2:
                        w.append((cT2, ev_vmm[t - 2]))
                    ev_evT[t][g] = P.op("scalar", lambda e, t=t, g=g: e.copy(
                        xnT_t[t % 2][:, g * 512:(g + 1) * 512], ps_t[g % 2][:, :]),
                        waits=w, inc=cS2)
                r = None
                for dt in range(32):
                    for hf in range(2):
                        w = [(cS2, ev_evT[t][dt // 4]), (cS, ev_wvc[dt])]
                        if dt == 0 and hf == 0 and t >= 2:
                            w.append((cS2, ev_evV[t - 2]))
                        last = (dt == 31 and hf == 1)
                        r = P.op("tensor", lambda e, t=t, dt=dt, hf=hf: e.matmul(
                            ps_v[t % 2][:, hf * 512:(hf + 1) * 512],
                            xnT_t[t % 2][:, dt * 128:(dt + 1) * 128],
                            wv_bf[:, dt * KVD + hf * 512: dt * KVD + (hf + 1) * 512],
                            start=(dt == 0), stop=(dt == 31)),
                            waits=w, inc=cT2 if last else None)
                ev_vmm[t] = r
                zp = cY2 if t % 2 == 0 else cZb
                w = [(cT2, ev_vmm[t])]
                if t >= 2:
                    w.append((zp, ev_vw[t - 2]))
                ev_evV[t] = P.op("scalar", lambda e, t=t: e.copy(
                    v_bf[t % 2][:, :], ps_v[t % 2][:, :]), waits=w, inc=cS2)
                ev_vw[t] = P.op("sync", lambda e, t=t: e.dma_start(
                    vn_s[t * 128:(t + 1) * 128, :], v_bf[t % 2][:, :]),
                    waits=[(cS2, ev_evV[t])], inc=zp, k=16)
        barrier()

    if "P1" in phases:
        with ExitStack() as ph:
            sb = lambda n, s, d: ph.enter_context(nc.sbuf_tensor(n, s, d))
            ps = lambda n, s: ph.enter_context(nc.psum_tensor(n, s, F32))
            A_stack = sb("Astk", [128, 32 * 64], BF16)
            B_stack = sb("Bstk", [128, D], BF16)
            AxT = sb("AxT", [128, 1024], BF16)
            gate_f = sb("gate_f", [1, 2 * NLORA], F32)
            gate_bf = sb("gate_bf", [1, 2 * NLORA], BF16)
            gbc = sb("gbc", [128, NLORA], F32)
            ccq_sb = sb("ccq_sb", [128, 8 * 128], F32)
            ssq_sb = sb("ssq_sb", [128, 8 * 128], F32)
            lb1 = make_ln_bufs(sb, "p1")
            wst4 = [sb(f"w4_{i}", [128, 512], F32) for i in range(4)]
            t1b = sb("t1b", [128, 512], F32)
            t2b = sb("t2b", [128, 512], F32)
            qr_nat = [sb(f"qrn{i}", [128, 512], BF16) for i in range(2)]
            qstage = [sb(f"qst{i}", [128, 512], BF16) for i in range(2)]
            gstage = [sb(f"gst{i}", [128, 512], F32) for i in range(2)]
            sge = [sb(f"sge{i}", [128, 512], F32) for i in range(2)]
            wdum = sb("wdum", [128, 8], F32)
            ps_qg = [ps(f"psqg{i}", [128, 512]) for i in range(2)]
            ps_t1 = [ps(f"pst1{i}", [128, 512]) for i in range(2)]
            ps_ax = ps("ps_ax", [64, 512])
            ps_sm = ps("ps_sm", [128, NLORA])

            with ExitStack() as tmp:
                tsb = lambda n, s, d: tmp.enter_context(nc.sbuf_tensor(n, s, d))
                acc_a = [tsb(f"acca{i}", [128, 32 * RANK], F32) for i in range(2)]
                acc_b = [tsb(f"accb{i}", [128, 32 * RANK], F32) for i in range(2)]
                At_st = [tsb(f"Atst{i}", [128, 32 * RANK], F32) for i in range(2)]
                As_st = tsb("Asst", [128, 32 * RANK], F32)
                ab_bf = tsb("ab_bf", [128, 2048], BF16)
                fb_bf = tsb("fb_bf", [128, 32 * RANK], BF16)
                bi_st = tsb("bi_st", [128, 32], F32)
                bi_bf = tsb("bi_bf", [128, 32], BF16)

                evg = P.op("sync", lambda e: e.dma_start(gate_f[:, :], gate2), inc=cW2, k=16)
                evA = P.op("sync", lambda e: e.dma_start(As_st[:, :], As_dev), inc=cW2, k=16)
                evBs = P.op("sync", lambda e: e.dma_start(At_st[0][:, :], Bs_fold),
                            waits=[], inc=cW2, k=16)
                evbi0 = P.op("sync", lambda e: e.dma_start(bi_st[:, :], bias_fold),
                             inc=cW2, k=16)
                ev_gbf = P.op("scalar", lambda e: e.copy(gate_bf[:, :], gate_f[:, :]),
                              waits=[(cW2, evbi0)], inc=cS)
                ev_bsc = P.op("scalar", lambda e: e.copy(fb_bf[:, :], At_st[0][:, :]),
                              waits=[(cW2, evbi0)], inc=cS)
                for c in range(2):
                    P.op("sync", lambda e, c=c: e.dma_start(
                        bass.AP(bstk_s_h, c * 65 * D + 32 * D, [[1024, 128], [1, 1024]]),
                        fb_bf[:, :]), waits=[(cS, ev_bsc)], inc=cW2, k=16)
                ev_bic = P.op("scalar", lambda e: e.copy(bi_bf[:, :], bi_st[:, :]),
                              waits=[(cW2, evbi0)], inc=cS)
                for c in range(2):
                    P.op("sync", lambda e, c=c: e.dma_start(
                        bass.AP(bstk_s_h, c * 65 * D + 64 * D, [[32, 128], [1, 32]]),
                        bi_bf[:, :]), waits=[(cS, ev_bic)], inc=cW2, k=16)

                at_free = [(cS, ev_bsc), None]
                ev_fb_free = cW2.v
                ev_ab_free = 0
                ev_gbc_prev = 0
                ev_lastac = 0
                for c in range(2):
                    w = [(cS, ev_gbf), (cV, ev_ones), (cW, ev_ident)]
                    if ev_gbc_prev:
                        w.append((cV, ev_gbc_prev))
                    evm = P.op("tensor", lambda e, c=c: e.matmul(
                        ps_sm[:, :], ones_bf[:1, :], gate_bf[:1, c * NLORA:(c + 1) * NLORA],
                        start=True, stop=True), waits=w, inc=cT)
                    w = [(cT, evm)]
                    if ev_lastac:
                        w.append((cV, ev_lastac))
                    ev_gbc = P.op("vector", lambda e: e.tensor_copy(
                        gbc[:, :], ps_sm[:, :]), waits=w, inc=cV)
                    ev_gbc_prev = ev_gbc
                    for mat, dev, accs in (("A", loraA_dev, acc_a),
                                           ("B", loraB_dev, acc_b)):
                        ev_dma = [0] * NLORA
                        ev_ac = [0] * NLORA
                        for n in range(NLORA):
                            xp = cX if n % 2 == 0 else cXb
                            w = [at_free[n % 2]] if at_free[n % 2] else []
                            ev_dma[n] = P.op("gpsimd", lambda e, dev=dev, n=n: e.dma_start(
                                At_st[n % 2][:, :], dev[n, :, :]), waits=w, inc=xp, k=16)
                            if n == 0:
                                ev_ac[0] = P.op("vector", lambda e, accs=accs: e.tensor_scalar(
                                    accs[1][:, :], At_st[0][:, :], gbc[:, 0:1], None,
                                    op0=OP.mult),
                                    waits=[(xp, ev_dma[0]), (cV, ev_gbc)], inc=cV)
                            else:
                                ev_ac[n] = P.op("vector", lambda e, accs=accs, n=n:
                                                e.scalar_tensor_tensor(
                                    accs[(n + 1) % 2][:, :], At_st[n % 2][:, :],
                                    gbc[:, n:n + 1], accs[n % 2][:, :],
                                    op0=OP.mult, op1=OP.add),
                                    waits=[(xp, ev_dma[n]), (cV, ev_ac[n - 1])], inc=cV)
                            at_free[n % 2] = (cV, ev_ac[n])
                            ev_lastac = ev_ac[n]
                        if mat == "A":
                            w = [(cV, ev_ac[NLORA - 1])]
                            if ev_ab_free:
                                w.append((cW2, ev_ab_free))
                            e1 = P.op("vector", lambda e: e.tensor_copy(
                                bass.AP(ab_bf, 0, [[2048, 128], [64, 32], [1, 32]]),
                                bass.AP(acc_a[0], 0, [[1024, 128], [32, 32], [1, 32]])),
                                waits=w, inc=cV)
                            e2 = P.op("vector", lambda e: e.tensor_copy(
                                bass.AP(ab_bf, 32, [[2048, 128], [64, 32], [1, 32]]),
                                bass.AP(As_st, 0, [[1024, 128], [32, 32], [1, 32]])),
                                waits=[(cW2, evA), (cV, e1)], inc=cV)
                            ev_ab_free = P.op("sync", lambda e, c=c: e.dma_start(
                                astk_s[c, :, :], ab_bf[:, :]),
                                waits=[(cV, e2)], inc=cW2, k=16)
                        else:
                            e1 = P.op("vector", lambda e: e.tensor_copy(
                                fb_bf[:, :], acc_b[0][:, :]),
                                waits=[(cV, ev_ac[NLORA - 1]), (cW2, ev_fb_free)], inc=cV)
                            ev_fb_free = P.op("sync", lambda e, c=c: e.dma_start(
                                bass.AP(bstk_s_h, c * 65 * D, [[1024, 128], [1, 1024]]),
                                fb_bf[:, :]), waits=[(cV, e1)], inc=cW2, k=16)
                ev_stk_w = cW2.v

            xnT_sb = sb("xnT_sb", [128, 32 * 1024], BF16)

            for h in range(2):
                hscope = ExitStack()
                xo_f32 = [hscope.enter_context(nc.sbuf_tensor(f"p1x{i}_{h}", [128, D], F32))
                          for i in range(2)]
                natq = [hscope.enter_context(nc.sbuf_tensor(f"natq{i}_{h}", [128, D], BF16))
                        for i in range(2)]
                ev_Ald = P.op("gpsimd", lambda e, h=h: e.dma_start(
                    A_stack[:, :], astk_s[h, :, :]), waits=[(cW2, ev_stk_w)], inc=cX2, k=16)
                ev_bz = P.op("vector", lambda e: e.memset(B_stack[64:128, :], 0.0), inc=cV)
                ev_Bld = P.op("gpsimd", lambda e, h=h: e.dma_start(
                    B_stack[0:65, :], bass.AP(bstk_s_h, h * 65 * D, [[D, 65], [1, D]])),
                    waits=[(cW2, ev_stk_w), (cV, ev_bz)], inc=cX2, k=16)
                P.op("vector", lambda e: e.memset(AxT[64:128, :], 0.0), inc=cV)
                ev_ones_ax = P.op("vector", lambda e: e.memset(AxT[64:65, :], 1.0), inc=cV)
                ev_cc = P.op("gpsimd", lambda e, h=h: e.dma_start(
                    ccq_sb[:, :], bass.AP(ccq, h * 1024 * 128,
                                          [[128, 128], [128 * 128, 8], [1, 128]])),
                    inc=cX2, k=16)
                ev_ss = P.op("gpsimd", lambda e, h=h: e.dma_start(
                    ssq_sb[:, :], bass.AP(ssq, h * 1024 * 128,
                                          [[128, 128], [128 * 128, 8], [1, 128]])),
                    inc=cX2, k=16)
                ev_tr = [[0] * 8 for _ in range(8)]
                ev_ev = [[0] * 8 for _ in range(8)]
                ev_nrm = [0] * 8
                for jl in range(8):
                    j = h * 8 + jl
                    xp = cX if jl % 2 == 0 else cXb
                    w = [(cV, ev_nrm[jl - 2])] if jl >= 2 else []
                    ev_x = P.op("gpsimd", lambda e, j=j, jl=jl: e.dma_start(
                        xo_f32[jl % 2][:, :], x_own[j * 128:(j + 1) * 128, :]),
                        waits=w, inc=xp, k=16)
                    w_norm = [(cT, ev_tr[jl - 2][7])] if jl >= 2 else []
                    ev_nrm[jl] = emit_ln(lb1, jl % 2, xo_f32[jl % 2], natq[jl % 2],
                                         [(xp, ev_x)], w_norm)
                    for g in range(8):
                        for q in range(4):
                            dt = g * 4 + q
                            w = []
                            if g == 0 and q == 0:
                                w = [(cV, ev_nrm[jl])]
                            if q == 0 and jl * 8 + g >= 2:
                                gi = jl * 8 + g - 2
                                w.append((cS2, ev_ev[gi // 8][gi % 8]))
                            r = P.op("tensor", lambda e, jl=jl, dt=dt, g=g, q=q: e.matmul(
                                ps_t1[g % 2][:, q * 128:(q + 1) * 128],
                                natq[jl % 2][:, dt * 128:(dt + 1) * 128], ident[:, :],
                                start=True, stop=True),
                                waits=w, inc=cT if q == 3 else None)
                            if q == 3:
                                ev_tr[jl][g] = r
                        ev_ev[jl][g] = P.op("scalar", lambda e, jl=jl, g=g: e.copy(
                            bass.AP(xnT_sb, (g * 4) * 1024 + jl * 128,
                                    [[32768, 128], [1024, 4], [1, 128]]),
                            ps_t1[g % 2][:, :]), waits=[(cT, ev_tr[jl][g])], inc=cS2)
                ev_xnT = ev_ev[7][7]
                hscope.close()
                wscope = ExitStack()
                wbf = [wscope.enter_context(nc.sbuf_tensor(f"wbf{i}_{h}", [128, 32 * 512], BF16))
                       for i in range(2)]

                ev_axev = [0, 0]
                for tb in range(2):
                    r = None
                    for dt in range(32):
                        w = []
                        if dt == 0:
                            w = [(cS2, ev_xnT), (cX2, ev_ss)]
                            if tb == 1:
                                w = [(cS2, ev_axev[0])]
                        r = P.op("tensor", lambda e, tb=tb, dt=dt: e.matmul(
                            ps_ax[:, :], A_stack[:, dt * 64:(dt + 1) * 64],
                            xnT_sb[:, dt * 1024 + tb * 512: dt * 1024 + (tb + 1) * 512],
                            start=(dt == 0), stop=(dt == 31)),
                            waits=w, inc=cT if dt == 31 else None)
                    ev_axev[tb] = P.op("scalar", lambda e, tb=tb: e.copy(
                        AxT[0:64, tb * 512:(tb + 1) * 512], ps_ax[:, :]),
                        waits=[(cT, r)], inc=cS2)
                if debug:
                    P.op("sync", lambda e, h=h: e.dma_start(
                        axt_dbg[h, :, :], AxT[0:65, :]),
                        waits=[(cS2, ev_axev[1]), (cV, ev_ones_ax)], inc=cY, k=16)

                ev_wc = {}
                ev_mm = {}
                ev_rope2 = {}
                ev_rope3 = {}
                ev_qtr = {}
                ev_qev = {}
                ev_qw = {}
                ev_sil = {}
                ev_sg3 = {}
                ev_gw = {}
                ev_r3_last = 0
                for wb in range(16):
                    src = wq if wb < 8 else wkern
                    cb = wb % 8
                    for dt in range(32):
                        i = wb * 32 + dt
                        wp = (cW, cWb, cWc, cWd)[i % 4]
                        w = [(cS, ev_wc[(i - 4) // 32, (i - 4) % 32])] if i >= 4 else []
                        evl = P.op("sync", lambda e, src=src, cb=cb, dt=dt, i=i: e.dma_start(
                            wst4[i % 4][:, :],
                            src[dt * 128:(dt + 1) * 128, cb * 512:(cb + 1) * 512]),
                            waits=w, inc=wp, k=16)
                        w = [(wp, evl)]
                        if wb >= 2 and dt == 0:
                            w.append((cT, ev_mm[wb - 2, 7]))
                        ev_wc[wb, dt] = P.op("scalar", lambda e, wb=wb, dt=dt, i=i: e.copy(
                            wbf[wb % 2][:, dt * 512:(dt + 1) * 512], wst4[i % 4][:, :]),
                            waits=w, inc=cS)
                    ev_wc[wb, 32] = P.op("scalar", lambda e, wb=wb: e.copy(
                        wdum[:, :], wst4[(wb * 32 + 31) % 4][:, 0:8]),
                        waits=[(cS, ev_wc[wb, 31])], inc=cS)
                    for jl in range(8):
                        j = h * 8 + jl
                        ui = wb * 8 + jl
                        r = None
                        for dt in range(32):
                            w = [(cS, ev_wc[wb, 32])]
                            if dt == 0:
                                if ui >= 2:
                                    pw, pj = (ui - 2) // 8, (ui - 2) % 8
                                    if pw < 8:
                                        w.append((cV, ev_rope2[pw, pj]))
                                    else:
                                        w.append((cV, ev_sil[pw, pj]))
                                if wb == 8:
                                    w += [(cS2, ev_axev[1]), (cX2, ev_ss),
                                          (cV, ev_ones_ax)]
                            last = (wb < 8 and dt == 31)
                            r = P.op("tensor", lambda e, wb=wb, jl=jl, dt=dt, last=last: e.matmul(
                                ps_qg[jl % 2][:, :],
                                xnT_sb[:, dt * 1024 + jl * 128: dt * 1024 + jl * 128 + 128],
                                wbf[wb % 2][:, dt * 512:(dt + 1) * 512],
                                start=(dt == 0), stop=last),
                                waits=w, inc=cT if last else None)
                        if wb >= 8:
                            r = P.op("tensor", lambda e, jl=jl, cb=cb: e.matmul(
                                ps_qg[jl % 2][:, :], AxT[:, jl * 128:(jl + 1) * 128],
                                B_stack[:, cb * 512:(cb + 1) * 512],
                                start=False, stop=True), inc=cT)
                        ev_mm[wb, jl] = r
                        if wb < 8:
                            w = [(cT, ev_mm[wb, jl]), (cX2, ev_ss)]
                            if ev_r3_last:
                                w.append((cV, ev_r3_last))
                            e1 = P.op("vector", lambda e, wb=wb, jl=jl: e.tensor_tensor(
                                t1b[:, :], ps_qg[jl % 2][:, :],
                                bass.AP(ccq_sb, jl * 128, [[1024, 128], [0, 4], [1, 128]]),
                                op=OP.mult), waits=w, inc=cV)
                            ev_rope2[wb, jl] = P.op("vector", lambda e, wb=wb, jl=jl: e.tensor_tensor(
                                t2b[:, :],
                                bass.AP(ps_qg[jl % 2], 64,
                                        [[512, 128], [128, 4], [-64, 2], [1, 64]]),
                                bass.AP(ssq_sb, jl * 128, [[1024, 128], [0, 4], [1, 128]]),
                                op=OP.mult), waits=[(cX2, ev_ss)], inc=cV)
                            w = [(cV, ev_rope2[wb, jl])]
                            if ui >= 2:
                                pw, pj = (ui - 2) // 8, (ui - 2) % 8
                                w.append((cT, ev_qtr[pw, pj]))
                            ev_rope3[wb, jl] = P.op("vector", lambda e, jl=jl: e.tensor_add(
                                qr_nat[jl % 2][:, :], t1b[:, :], t2b[:, :]),
                                waits=w, inc=cV)
                            ev_r3_last = ev_rope3[wb, jl]
                            r = None
                            for hh in range(4):
                                w = [(cV, ev_rope3[wb, jl])] if hh == 0 else []
                                if hh == 0 and ui >= 2:
                                    pw, pj = (ui - 2) // 8, (ui - 2) % 8
                                    w.append((cS2, ev_qev[pw, pj]))
                                elif hh == 0:
                                    w.append((cS2, ev_ev[7][6 + jl]))
                                r = P.op("tensor", lambda e, jl=jl, hh=hh: e.matmul(
                                    ps_t1[jl % 2][:, hh * 128:(hh + 1) * 128],
                                    qr_nat[jl % 2][:, hh * 128:(hh + 1) * 128], ident[:, :],
                                    start=True, stop=True),
                                    waits=w, inc=cT if hh == 3 else None)
                            ev_qtr[wb, jl] = r
                            w = [(cT, ev_qtr[wb, jl])]
                            if ui >= 2:
                                pw, pj = (ui - 2) // 8, (ui - 2) % 8
                                w.append((cY if pj % 2 == 0 else cYb, ev_qw[pw, pj]))
                            ev_qev[wb, jl] = P.op("scalar", lambda e, jl=jl: e.copy(
                                qstage[jl % 2][:, :], ps_t1[jl % 2][:, :]), waits=w, inc=cS2)
                            ev_qw[wb, jl] = P.op("gpsimd", lambda e, j=j, wb=wb, jl=jl: e.dma_start(
                                qT_s[j, wb, :, :], qstage[jl % 2][:, :]),
                                waits=[(cS2, ev_qev[wb, jl])],
                                inc=(cY if jl % 2 == 0 else cYb), k=16)
                        else:
                            w = [(cT, ev_mm[wb, jl])]
                            if ui >= 2:
                                pw, pj = (ui - 2) // 8, (ui - 2) % 8
                                if pw >= 8:
                                    w.append((cV, ev_sg3[pw, pj]))
                            ev_se = P.op("scalar", lambda e, jl=jl: e.activation(
                                sge[jl % 2][:, :], ps_qg[jl % 2][:, :], AF.Exp,
                                scale=-1.0), waits=w, inc=cS2)
                            e1 = P.op("vector", lambda e, jl=jl: e.tensor_scalar(
                                sge[jl % 2][:, :], sge[jl % 2][:, :], 1.0, None,
                                op0=OP.add), waits=[(cS2, ev_se)], inc=cV)
                            e2 = P.op("vector", lambda e, jl=jl: e.reciprocal(
                                sge[jl % 2][:, :], sge[jl % 2][:, :]),
                                waits=[(cV, e1)], inc=cV)
                            w = [(cV, e2)]
                            if ui >= 2:
                                pw, pj = (ui - 2) // 8, (ui - 2) % 8
                                if pw >= 8:
                                    w.append((cY2 if pj % 2 == 0 else cZb, ev_gw[pw, pj]))
                            ev_sg3[wb, jl] = P.op("vector", lambda e, jl=jl: e.tensor_tensor(
                                gstage[jl % 2][:, :], ps_qg[jl % 2][:, :],
                                sge[jl % 2][:, :], op=OP.mult),
                                waits=w, inc=cV)
                            ev_sil[wb, jl] = ev_sg3[wb, jl]
                            ev_gw[wb, jl] = P.op("gpsimd", lambda e, j=j, cb=cb, jl=jl: e.dma_start(
                                g_s[j * 128:(j + 1) * 128, cb * 512:(cb + 1) * 512],
                                gstage[jl % 2][:, :]),
                                waits=[(cV, ev_sil[wb, jl])],
                                inc=(cY2 if jl % 2 == 0 else cZb), k=16)
                if debug:
                    P.op("sync", lambda e, h=h: e.dma_start(
                        xnt_dbg[h, :, :], xnT_sb[:, :]),
                        waits=[(cT, ev_mm[15, 7])], inc=cY, k=16)
                    P.op("sync", lambda e, h=h: e.dma_start(
                        bsb_dbg[h, :, :], B_stack[:, :]),
                        waits=[(cT, ev_mm[15, 7])], inc=cY, k=16)
                    for par in range(2):
                        P.op("sync", lambda e, h=h, par=par: e.dma_start(
                            wbf_dbg[h, par, :, :], wbf[par][:, :]),
                            waits=[(cT, ev_mm[15, 7])], inc=cY, k=16)
                wscope.close()
                barrier()

    KT_res = None
    kt_open = [False]
    if "P2" in phases:
        kt_scope = ExitStack()
        kt_open[0] = True
        KT_res = kt_scope.enter_context(nc.sbuf_tensor("KT_res", [128, NKV * T], BF16))
        with ExitStack() as ph:
            sb = lambda n, s, d: ph.enter_context(nc.sbuf_tensor(n, s, d))
            ps = lambda n, s: ph.enter_context(nc.psum_tensor(n, s, F32))
            wk_bf = sb("wk_bf", [128, 32 * KVD], BF16)
            wst2 = [sb(f"p2w{i}", [128, KVD], F32) for i in range(2)]
            natk = [sb(f"p2n{i}", [128, D], BF16) for i in range(2)]
            xnT_t2 = [sb(f"p2t{i}", [128, D], BF16) for i in range(2)]
            cck_t = [sb(f"cck{i}", [128, 128], F32) for i in range(2)]
            ssk_t = [sb(f"ssk{i}", [128, 128], F32) for i in range(2)]
            kr_nat = [sb(f"p2k{i}", [128, KVD], BF16) for i in range(2)]
            kt1 = [sb(f"p2r{i}", [128, KVD], F32) for i in range(2)]
            kt2 = [sb(f"p2s{i}", [128, KVD], F32) for i in range(2)]
            ps_t2 = [ps(f"p2pt{i}", [128, 512]) for i in range(2)]
            ps_k = [ps(f"p2pk{i}", [128, KVD]) for i in range(2)]

            ev_wkc = [0] * 32
            for wt in range(32):
                wp = cW if wt % 2 == 0 else cWb
                w = [(cS, ev_wkc[wt - 2])] if wt >= 2 else []
                ev = P.op("sync", lambda e, wt=wt: e.dma_start(
                    wst2[wt % 2][:, :], wk[wt * 128:(wt + 1) * 128, :]),
                    waits=w, inc=wp, k=16)
                ev_wkc[wt] = P.op("scalar", lambda e, wt=wt: e.copy(
                    wk_bf[:, wt * KVD:(wt + 1) * KVD], wst2[wt % 2][:, :]),
                    waits=[(wp, ev)], inc=cS)

            ev_tr = [[0] * 8 for _ in range(NT)]
            ev_ev = [[0] * 8 for _ in range(NT)]
            ev_kmm = [0] * NT
            ev_rp = [0] * NT
            ev_ktr = [[0] * 2 for _ in range(NT)]
            ev_kev = [[0] * 2 for _ in range(NT)]
            for kt in range(NT):
                xp = cX if kt % 2 == 0 else cXb
                w = [(cT, ev_tr[kt - 2][7])] if kt >= 2 else []
                ev_x = P.op("gpsimd", lambda e, kt=kt: e.dma_start(
                    natk[kt % 2][:, :], xn_s[kt * 128:(kt + 1) * 128, :]),
                    waits=w, inc=xp, k=16)
                for g in range(8):
                    for q in range(4):
                        dt = g * 4 + q
                        w = [(xp, ev_x)] if (g == 0 and q == 0) else []
                        if q == 0:
                            if g >= 2:
                                w.append((cS2, ev_ev[kt][g - 2]))
                            elif kt >= 1:
                                w.append((cS2, ev_kev[kt - 1][g]))
                        r = P.op("tensor", lambda e, kt=kt, dt=dt, g=g, q=q: e.matmul(
                            ps_t2[g % 2][:, q * 128:(q + 1) * 128],
                            natk[kt % 2][:, dt * 128:(dt + 1) * 128], ident[:, :],
                            start=True, stop=True),
                            waits=w, inc=cT if q == 3 else None)
                        if q == 3:
                            ev_tr[kt][g] = r
                    ev_ev[kt][g] = P.op("scalar", lambda e, kt=kt, g=g: e.copy(
                        xnT_t2[kt % 2][:, g * 512:(g + 1) * 512], ps_t2[g % 2][:, :]),
                        waits=[(cT, ev_tr[kt][g])], inc=cS2)
                r = None
                for dt in range(32):
                    for hf in range(2):
                        w = [(cS2, ev_ev[kt][dt // 4]), (cS, ev_wkc[dt])]
                        if dt == 0 and hf == 0 and kt >= 2:
                            w.append((cV, ev_rp[kt - 2]))
                        last = (dt == 31 and hf == 1)
                        r = P.op("tensor", lambda e, kt=kt, dt=dt, hf=hf: e.matmul(
                            ps_k[kt % 2][:, hf * 512:(hf + 1) * 512],
                            xnT_t2[kt % 2][:, dt * 128:(dt + 1) * 128],
                            wk_bf[:, dt * KVD + hf * 512: dt * KVD + (hf + 1) * 512],
                            start=(dt == 0), stop=(dt == 31)),
                            waits=w, inc=cT2 if last else None)
                ev_kmm[kt] = r
                qp = cQa if kt % 2 == 0 else cQb
                wtb = [(cV, ev_rp[kt - 2])] if kt >= 2 else []
                P.op("gpsimd", lambda e, kt=kt: e.dma_start(
                    cck_t[kt % 2][:, :], bass.AP(cck, kt * 128 * 128,
                                                 [[128, 128], [1, 128]])),
                    waits=wtb, inc=qp, k=16)
                ev_tbl = P.op("gpsimd", lambda e, kt=kt: e.dma_start(
                    ssk_t[kt % 2][:, :], bass.AP(ssk, kt * 128 * 128,
                                                 [[128, 128], [1, 128]])),
                    inc=qp, k=16)
                w = [(cT2, ev_kmm[kt]), (qp, ev_tbl)]
                if kt >= 2:
                    w.append((cV, ev_rp[kt - 2]))
                e1 = P.op("vector", lambda e, kt=kt: e.tensor_tensor(
                    kt1[kt % 2][:, :], ps_k[kt % 2][:, :],
                    bass.AP(cck_t[kt % 2], 0, [[128, 128], [0, 8], [1, 128]]),
                    op=OP.mult), waits=w, inc=cV)
                e2 = P.op("vector", lambda e, kt=kt: e.tensor_tensor(
                    kt2[kt % 2][:, :],
                    bass.AP(ps_k[kt % 2], 64, [[1024, 128], [128, 8], [-64, 2], [1, 64]]),
                    bass.AP(ssk_t[kt % 2], 0, [[128, 128], [0, 8], [1, 128]]),
                    op=OP.mult), inc=cV)
                w = [(cV, e2)]
                if kt >= 2:
                    w.append((cT, ev_ktr[kt - 2][1]))
                ev_rp[kt] = P.op("vector", lambda e, kt=kt: e.tensor_add(
                    kr_nat[kt % 2][:, :], kt1[kt % 2][:, :], kt2[kt % 2][:, :]),
                    waits=w, inc=cV)
                for g in range(2):
                    for q in range(4):
                        hh = g * 4 + q
                        w = [(cV, ev_rp[kt])] if (g == 0 and q == 0) else []
                        if q == 0:
                            w.append((cS2, ev_ev[kt][6 + g]))
                        r = P.op("tensor", lambda e, kt=kt, hh=hh, g=g, q=q: e.matmul(
                            ps_t2[g % 2][:, q * 128:(q + 1) * 128],
                            kr_nat[kt % 2][:, hh * 128:(hh + 1) * 128], ident[:, :],
                            start=True, stop=True),
                            waits=w, inc=cT if q == 3 else None)
                        if q == 3:
                            ev_ktr[kt][g] = r
                    ev_kev[kt][g] = P.op("scalar", lambda e, kt=kt, g=g: e.copy(
                        bass.AP(KT_res, (g * 4) * T + kt * 128,
                                [[NKV * T, 128], [T, 4], [1, 128]]),
                        ps_t2[g % 2][:, :]), waits=[(cT, ev_ktr[kt][g])], inc=cS2)
        if debug:
            evd = P.op("sync", lambda e: e.dma_start(kT_dbg[:, :], KT_res[:, :]),
                       waits=[(cS2, cS2.v)], inc=cY, k=16)
        barrier()

    if "P3" in phases:
        with ExitStack() as ph:
            sb = lambda n, s, d: ph.enter_context(nc.sbuf_tensor(n, s, d))
            ps = lambda n, s: ph.enter_context(nc.psum_tensor(n, s, F32))
            V_res = sb("V_res", [128, NT * KVD], BF16)
            mask_sb = sb("mask_sb", [128, 256], BF16)
            qbuf = [sb(f"qb{i}", [128, 8 * 512], BF16) for i in range(2)]
            probs_f = [sb(f"pf{i}", [128, 4096], F32) for i in range(2)]
            probs = [sb(f"pb{i}", [128, 4096], BF16) for i in range(2)]
            pTsb = [sb(f"pT{i}", [128, 512], BF16) for i in range(2)]
            cstage = [sb(f"cst{i}", [128, 512], BF16) for i in range(2)]
            lparts = [sb(f"lp{i}", [128, 4], F32) for i in range(2)]
            lsum = [sb(f"ls{i}", [128, 1], F32) for i in range(2)]
            rl = [sb(f"rl{i}", [128, 1], F32) for i in range(2)]
            negm = [sb(f"ng{i}", [128, 1], F32) for i in range(2)]
            ps_S = [ps(f"psS{i}", [128, 1024]) for i in range(2)]
            ps_T3 = [ps(f"psT3{i}", [128, 512]) for i in range(2)]
            ps_C = ps("psC", [128, 256])

            ev_msk = P.op("gpsimd", lambda e: e.dma_start(mask_sb[:, :], mask_in),
                          inc=cX2, k=16)
            ev_vres = 0
            for kt in range(NT):
                ev_vres = P.op("gpsimd", lambda e, kt=kt: e.dma_start(
                    V_res[:, kt * KVD:(kt + 1) * KVD], vn_s[kt * 128:(kt + 1) * 128, :]),
                    inc=cX2, k=16)
            ev_qld = {}
            cQ = [cQa, cQb]
            for wb in range(8):
                ev_qld[0] = P.op("gpsimd", lambda e, wb=wb: e.dma_start(
                    qbuf[0][:, wb * 512:(wb + 1) * 512], qT_s[0, wb, :, :]),
                    inc=cQ[0], k=16)

            NU = NQT * NH
            bank_user = [0, 0]
            ev_chunk = {}
            ev_max = {}
            ev_exp = {}
            ev_explast = {}
            ev_scale = {}
            ev_ttail = {}
            ev_pev = {}
            ev_cgrp = {}
            ev_ctx = {}
            ev_cev = {}
            ev_lsum = {}
            ev_cw = {}
            ev_sc_last_q = [0, 0]

            psT_free = [None, None]
            pT_free = [0, 0]

            def emit_tc(u):
                j, hh = u // NH, u % NH
                L, rounds = slot_rounds(j)
                nkt = L // 128
                hkv = hh // 4
                ectr = cS2 if u % 2 == 0 else cV2
                eeng = "scalar" if u % 2 == 0 else "vector"
                r = None
                for g in range((nkt + 3) // 4):
                    b = g % 2
                    qn = min(4, nkt - g * 4)
                    for q in range(qn):
                        kt = g * 4 + q
                        w = [(cV, ev_scale[u])] if kt == 0 else []
                        if q == 0 and psT_free[b] is not None:
                            w.append(psT_free[b])
                        r = P.op("tensor", lambda e, u=u, kt=kt, b=b: e.matmul(
                            ps_T3[b][:, (kt % 4) * 128:(kt % 4 + 1) * 128],
                            probs[u % 2][:, kt * 128:(kt + 1) * 128], ident[:, :],
                            start=True, stop=True),
                            waits=w, inc=cT2 if q == qn - 1 else None)
                    ev_ttail[u] = r
                    w = [(cT2, r)]
                    if pT_free[b]:
                        w.append((cT2, pT_free[b]))
                    def _evac(e, b=b, qn=qn):
                        if hasattr(e, "tensor_copy"):
                            return e.tensor_copy(pTsb[b][:, :qn * 128],
                                                 ps_T3[b][:, :qn * 128])
                        return e.copy(pTsb[b][:, :qn * 128], ps_T3[b][:, :qn * 128])
                    ev_pev[(u, g)] = (ectr, P.op(eeng, _evac, waits=w, inc=ectr))
                    psT_free[b] = ev_pev[(u, g)]
                    r2 = None
                    for q in range(qn):
                        kt = g * 4 + q
                        w = [ev_pev[(u, g)]]
                        if kt == 0:
                            w.append((cX2, ev_vres))
                            if u >= 2:
                                w.append((cV, ev_cev[u - 2]))
                        r2 = P.op("tensor", lambda e, u=u, kt=kt, hkv=hkv, b=b, nkt=nkt: e.matmul(
                            ps_C[:, (u % 2) * 128:(u % 2) * 128 + 128],
                            V_res[:, kt * KVD + hkv * 128: kt * KVD + hkv * 128 + 128],
                            pTsb[b][:, (kt % 4) * 128:(kt % 4 + 1) * 128],
                            start=(kt == 0), stop=(kt == nkt - 1)),
                            waits=w, inc=cT2 if q == qn - 1 else None)
                    ev_cgrp[(u, g)] = r2
                    pT_free[b] = r2
                ev_ctx[u] = r2
                grp = hh // 4
                w = [(cT2, ev_ctx[u])]
                pkey = (j, grp - 2) if grp >= 2 else (j - 1, 8 + grp - 2)
                if pkey in ev_cw:
                    w.append((cY if pkey[1] % 2 == 0 else cYb, ev_cw[pkey]))
                ev_cev[u] = P.op("vector", lambda e, u=u, hh=hh, grp=grp: e.tensor_copy(
                    cstage[grp % 2][:, (hh % 4) * 128:(hh % 4 + 1) * 128],
                    ps_C[:, (u % 2) * 128:(u % 2) * 128 + 128]),
                    waits=w, inc=cV)
                if hh % 4 == 3:
                    ev_cw[(j, grp)] = P.op("sync", lambda e, j=j, grp=grp: e.dma_start(
                        ctxT_s[j, grp, :, :], cstage[grp % 2][:, :]),
                        waits=[(cV, ev_cev[u])],
                        inc=(cY if grp % 2 == 0 else cYb), k=16)

            for u in range(NU):
                j, hh = u // NH, u % NH
                L, rounds = slot_rounds(j)
                nr = len(rounds)
                hkv = hh // 4
                if hh == 16 and j + 1 < NQT:
                    for wb in range(8):
                        w = [(cT, ev_sc_last_q[(j + 1) % 2])] if wb == 0 else []
                        ev_qld[j + 1] = P.op("gpsimd", lambda e, j=j, wb=wb: e.dma_start(
                            qbuf[(j + 1) % 2][:, wb * 512:(wb + 1) * 512],
                            qT_s[j + 1, wb, :, :]), waits=w, inc=cQ[(j + 1) % 2], k=16)
                def emit_round(r):
                    r_off, r_w, chunks = rounds[r]
                    bank = r % 2
                    last = None
                    for ci, (c_off, c_w) in enumerate(chunks):
                        w = []
                        if ci == 0:
                            if bank_user[bank]:
                                w.append((cS2, bank_user[bank]))
                            w.append((cQ[j % 2], ev_qld[j]))
                        has_mask = (r == nr - 1 and ci == len(chunks) - 1)
                        r1 = P.op("tensor", lambda e, u=u, r=r, c_off=c_off, c_w=c_w,
                                  bank=bank, r_off=r_off, hkv=hkv, hh=hh, j=j: e.matmul(
                            ps_S[bank][:, c_off - r_off: c_off - r_off + c_w],
                            qbuf[j % 2][:, (hh // 4) * 512 + (hh % 4) * 128:
                                        (hh // 4) * 512 + (hh % 4) * 128 + 128],
                            KT_res[:, hkv * T + c_off: hkv * T + c_off + c_w],
                            start=True, stop=not has_mask),
                            waits=w, inc=None if has_mask else cT)
                        if has_mask:
                            r1 = P.op("tensor", lambda e, u=u, bank=bank, L=L, r_off=r_off: e.matmul(
                                ps_S[bank][:, L - 256 - r_off: L - r_off],
                                ident[:, :], mask_sb[:, :],
                                start=False, stop=True),
                                waits=[(cX2, ev_vres)], inc=cT)
                        last = r1
                    ev_chunk[(u, r)] = last
                    return last

                for r in range(min(2, nr)):
                    emit_round(r)
                w = [(cT, ev_chunk[(u, 0)])]
                if u >= 2:
                    w.append((cS2, ev_explast[u - 2]))
                ev_max[u] = P.op("vector", lambda e, u=u, L=L: e.reduce_max(
                    negm[u % 2][:, :], ps_S[0][:, 0:min(L, 512)], axis=AX, negate=True),
                    waits=w, inc=cV)
                def emit_exp(r):
                    r_off, r_w, chunks = rounds[r]
                    bank = r % 2
                    w = [(cT, ev_chunk[(u, r)]), (cV, ev_max[u])]
                    if u >= 2 and r == 0:
                        w.append((cV, ev_scale.get(u - 2, 0)))
                        w.append((cV, ev_lsum.get(u - 2, 0)))
                    ev_exp[(u, r)] = P.op("scalar", lambda e, u=u, r=r, r_off=r_off,
                                          r_w=r_w, bank=bank: e.activation(
                        probs_f[u % 2][:, r_off:r_off + r_w], ps_S[bank][:, 0:r_w],
                        AF.Exp, bias=negm[u % 2][:, :1],
                        accum_out=lparts[u % 2][:, r:r + 1]),
                        waits=w, inc=cS2)
                    bank_user[bank] = ev_exp[(u, r)]

                for r in range(min(2, nr)):
                    emit_exp(r)
                for r in range(2, nr):
                    emit_round(r)
                    emit_exp(r)
                if u >= 1:
                    emit_tc(u - 1)
                ev_explast[u] = ev_exp[(u, nr - 1)]
                ev_sc_last_q[j % 2] = cT.v
                ev_lsum[u] = P.op("vector", lambda e, u=u, nr=nr: e.reduce_sum(
                    lsum[u % 2][:, :], lparts[u % 2][:, 0:nr], axis=AX),
                    waits=[(cS2, ev_explast[u])], inc=cV)
                ev_rl = P.op("vector", lambda e, u=u: e.reciprocal(
                    rl[u % 2][:, :], lsum[u % 2][:, :]),
                    waits=[(cV, ev_lsum[u])], inc=cV)
                w = [(cV, ev_rl)]
                if u >= 2:
                    w.append((cT2, ev_ttail.get(u - 2, 0)))
                ev_scale[u] = P.op("vector", lambda e, u=u, L=L: e.tensor_scalar(
                    probs[u % 2][:, 0:L], probs_f[u % 2][:, 0:L], rl[u % 2][:, :1], None,
                    op0=OP.mult), waits=w, inc=cV)
            emit_tc(NU - 1)
        barrier()
        if kt_open[0]:
            kt_scope.close()
            kt_open[0] = False

    if "P5" in phases:
        with ExitStack() as ph:
            sb = lambda n, s, d: ph.enter_context(nc.sbuf_tensor(n, s, d))
            ps = lambda n, s: ph.enter_context(nc.psum_tensor(n, s, F32))
            ctx_sb = sb("ctx_sb", [128, 8 * 8 * 512], BF16)
            wst45 = [sb(f"o4_{i}", [128, 512], F32) for i in range(4)]
            wdum5 = sb("wdum5", [128, 8], F32)
            wobf = [sb(f"wobf{i}", [128, 32 * 512], BF16) for i in range(2)]
            gbuf = [sb(f"gb{i}", [128, 512], F32) for i in range(2)]
            xbuf = [sb(f"xb{i}", [128, 512], F32) for i in range(2)]
            tbuf = [sb(f"tb{i}", [128, 512], F32) for i in range(2)]
            obuf = [sb(f"ob{i}", [128, 512], F32) for i in range(2)]
            ps_s = [ps(f"pss{i}", [128, 512]) for i in range(2)]

            for h in range(2):
                ev_ctxl = 0
                for jl in range(8):
                    for wb in range(8):
                        j = h * 8 + jl
                        ev_ctxl = P.op("gpsimd", lambda e, j=j, jl=jl, wb=wb: e.dma_start(
                            ctx_sb[:, (jl * 8 + wb) * 512:(jl * 8 + wb + 1) * 512],
                            ctxT_s[j, wb, :, :]), inc=cX2, k=16)
                ev_wc = {}
                ev_mm = {}
                ev_t1 = {}
                ev_out = {}
                ev_yw = {}
                ev_gx = {}
                for wb in range(8):
                    for dt in range(32):
                        i = wb * 32 + dt
                        wp = (cW, cWb, cWc, cWd)[i % 4]
                        w = [(cS, ev_wc[(i - 4) // 32, (i - 4) % 32])] if i >= 4 else []
                        evl = P.op("sync", lambda e, wb=wb, dt=dt, i=i: e.dma_start(
                            wst45[i % 4][:, :],
                            wo[dt * 128:(dt + 1) * 128, wb * 512:(wb + 1) * 512]),
                            waits=w, inc=wp, k=16)
                        w = [(wp, evl)]
                        if wb >= 2 and dt == 0:
                            w.append((cT, ev_mm[wb - 2, 7]))
                        ev_wc[wb, dt] = P.op("scalar", lambda e, wb=wb, dt=dt, i=i: e.copy(
                            wobf[wb % 2][:, dt * 512:(dt + 1) * 512], wst45[i % 4][:, :]),
                            waits=w, inc=cS)
                    ev_wc[wb, 32] = P.op("scalar", lambda e, wb=wb: e.copy(
                        wdum5[:, :], wst45[(wb * 32 + 31) % 4][:, 0:8]),
                        waits=[(cS, ev_wc[wb, 31])], inc=cS)
                    for jl in range(8):
                        j = h * 8 + jl
                        ui = wb * 8 + jl
                        w = []
                        if ui >= 2:
                            pw, pj = (ui - 2) // 8, (ui - 2) % 8
                            w.append((cV, ev_t1[pw, pj]))
                        ev_g = P.op("gpsimd", lambda e, j=j, wb=wb, jl=jl: e.dma_start(
                            gbuf[jl % 2][:, :],
                            g_s[j * 128:(j + 1) * 128, wb * 512:(wb + 1) * 512]),
                            waits=w, inc=(cX if jl % 2 == 0 else cXb), k=16)
                        w2 = []
                        if ui >= 2:
                            pw, pj = (ui - 2) // 8, (ui - 2) % 8
                            w2.append((cV, ev_out[pw, pj]))
                        ev_xl = P.op("gpsimd", lambda e, j=j, wb=wb, jl=jl: e.dma_start(
                            xbuf[jl % 2][:, :],
                            x_own[j * 128:(j + 1) * 128, wb * 512:(wb + 1) * 512]),
                            waits=w2, inc=(cQa if jl % 2 == 0 else cQb), k=16)
                        ev_gx[wb, jl] = (ev_g, ev_xl)
                        r = None
                        for dt in range(32):
                            w = [(cS, ev_wc[wb, 32])]
                            if dt == 0:
                                w.append((cX2, ev_ctxl))
                                if ui >= 2:
                                    pw, pj = (ui - 2) // 8, (ui - 2) % 8
                                    w.append((cV, ev_t1[pw, pj]))
                            r = P.op("tensor", lambda e, wb=wb, jl=jl, dt=dt: e.matmul(
                                ps_s[jl % 2][:, :],
                                ctx_sb[:, (jl * 8 + dt // 4) * 512 + (dt % 4) * 128:
                                       (jl * 8 + dt // 4) * 512 + (dt % 4) * 128 + 128],
                                wobf[wb % 2][:, dt * 512:(dt + 1) * 512],
                                start=(dt == 0), stop=(dt == 31)),
                                waits=w, inc=cT if dt == 31 else None)
                        ev_mm[wb, jl] = r
                        w = [(cT, r), (cX if jl % 2 == 0 else cXb, ev_g)]
                        if ui >= 2:
                            pw, pj = (ui - 2) // 8, (ui - 2) % 8
                            w.append((cV, ev_out[pw, pj]))
                        ev_t1[wb, jl] = P.op("vector", lambda e, jl=jl: e.tensor_tensor(
                            tbuf[jl % 2][:, :], ps_s[jl % 2][:, :], gbuf[jl % 2][:, :],
                            op=OP.mult), waits=w, inc=cV)
                        w = [(cV, ev_t1[wb, jl]),
                             (cQa if jl % 2 == 0 else cQb, ev_xl)]
                        if ui >= 2:
                            pw, pj = (ui - 2) // 8, (ui - 2) % 8
                            w.append((cY if pj % 2 == 0 else cYb, ev_yw[pw, pj]))
                        ev_out[wb, jl] = P.op("vector", lambda e, jl=jl: e.tensor_add(
                            obuf[jl % 2][:, :], tbuf[jl % 2][:, :], xbuf[jl % 2][:, :]),
                            waits=w, inc=cV)
                        ev_yw[wb, jl] = P.op("gpsimd", lambda e, j=j, wb=wb, jl=jl: e.dma_start(
                            y[j * 128:(j + 1) * 128, wb * 512:(wb + 1) * 512],
                            obuf[jl % 2][:, :]),
                            waits=[(cV, ev_out[wb, jl])],
                            inc=(cY if jl % 2 == 0 else cYb), k=16)
                barrier()

    if KT_res is not None and kt_open[0]:
        kt_scope.close()
        kt_open[0] = False

    P.op("sync", None, waits=[(cY, cY.v), (cYb, cYb.v), (cY2, cY2.v),
                              (cZb, cZb.v), (cW2, cW2.v)])
    barrier()

    nums = sorted(h.num for h in sem_handles)
    assert nums == list(range(nums[0], nums[0] + len(nums))), nums
    srange = range(nums[0], nums[-1] + 1)
    nc.gpsimd.dma_reset(srange)
    nc.gpsimd.sem_clear(srange)
    nc.all_engine_barrier()

    with nc.Block() as block:
        @block.tensor
        def _(eng):
            P.emit("tensor", eng)

        @block.vector
        def _(eng):
            P.emit("vector", eng)

        @block.scalar
        def _(eng):
            P.emit("scalar", eng)

        @block.sync
        def _(eng):
            P.emit("sync", eng)

        @block.gpsimd
        def _(eng):
            P.emit("gpsimd", eng)

    es.close()
    return nc


def _rope_tables():
    inv = 1.0 / (500000.0 ** (np.arange(0, HD, 2, dtype=np.float64) / HD))
    t = np.arange(T, dtype=np.float64)
    fr = np.outer(t, inv)
    cos = np.cos(fr).astype(np.float32)
    sin = np.sin(fr).astype(np.float32)
    cc = np.concatenate([cos, cos], axis=1)
    ss = np.concatenate([-sin, sin], axis=1)
    return cc, ss


def _mask_for_parity(p):
    m = np.zeros((128, 256), np.float32)
    qi, ci = np.meshgrid(np.arange(128), np.arange(256), indexing="ij")
    if p == 1:
        m[(ci >= 128) & (ci - 128 > qi)] = NEGBIG
    else:
        m[(ci < 128) & (ci > qi)] = NEGBIG
        m[ci >= 128] = NEGBIG
    return m.astype(BF16NP)


def prep_in_maps(L0_attn_x, L0_attn_Wq, L0_attn_Wk, L0_attn_Wv, L0_attn_Wo,
                 L0_ffn_gate, L0_ffn_kernel, L0_ffn_bias, L0_ffn_lora_A,
                 L0_ffn_lora_B, L0_ffn_lora_A_shared, L0_ffn_lora_B_shared):
    xin = np.ascontiguousarray(np.asarray(L0_attn_x, dtype=np.float32))
    wq = np.ascontiguousarray(np.asarray(L0_attn_Wq, dtype=np.float32))
    wk = np.ascontiguousarray(np.asarray(L0_attn_Wk, dtype=np.float32))
    wv = np.ascontiguousarray(np.asarray(L0_attn_Wv, dtype=np.float32))
    wo = np.ascontiguousarray(np.asarray(L0_attn_Wo, dtype=np.float32))
    gate = np.asarray(L0_ffn_gate, dtype=np.float32)
    wkern = np.ascontiguousarray(np.asarray(L0_ffn_kernel, dtype=np.float32))
    bias = np.asarray(L0_ffn_bias, dtype=np.float32).reshape(1, D)
    lora_A = np.asarray(L0_ffn_lora_A, dtype=np.float32)
    lora_B = np.asarray(L0_ffn_lora_B, dtype=np.float32)
    As = np.asarray(L0_ffn_lora_A_shared, dtype=np.float32)
    Bs = np.asarray(L0_ffn_lora_B_shared, dtype=np.float32)

    loraA_dev = np.ascontiguousarray(
        lora_A.reshape(NLORA, RANK, 32, 128).transpose(0, 3, 2, 1).reshape(NLORA, 128, 32 * RANK))
    As_dev = np.ascontiguousarray(As.reshape(RANK, 32, 128).transpose(2, 1, 0).reshape(128, 32 * RANK))
    loraB_dev = np.ascontiguousarray(
        lora_B.transpose(0, 2, 1).reshape(NLORA, 128, 32 * RANK))
    Bs_fold = np.ascontiguousarray(Bs.T.reshape(128, 32 * RANK))
    bias_f = np.ascontiguousarray(bias.reshape(128, 32))
    identity = np.eye(128, dtype=np.float32).astype(BF16NP)
    cc, ss = _rope_tables()
    scale = 1.0 / np.sqrt(HD)
    in_maps = []
    for c in range(NCORES):
        b, p = c // 2, c % 2
        tiles = [2 * j + p for j in range(NQT)]
        rows = np.concatenate([np.arange(t * 128, t * 128 + 128) for t in tiles])
        x_b = np.ascontiguousarray(xin[b])
        in_maps.append({
            "x": x_b,
            "x_own": np.ascontiguousarray(x_b[rows]),
            "wq": wq, "wk": wk, "wv": wv, "wo": wo, "wkern": wkern,
            "bias_fold": bias_f,
            "gate2": np.ascontiguousarray(gate[[2 * b, 2 * b + 1]].reshape(1, 2 * NLORA)),
            "loraA_dev": loraA_dev, "As_dev": As_dev,
            "loraB_dev": loraB_dev, "Bs_fold": Bs_fold,
            "ident_in": identity,
            "mask_in": _mask_for_parity(p),
            "ccq": np.ascontiguousarray(cc[rows] * scale),
            "ssq": np.ascontiguousarray(ss[rows] * scale),
            "cck": cc, "ssk": ss,
        })
    return in_maps


def assemble_out(results):
    out = np.empty((B, T, D), np.float32)
    for c in range(NCORES):
        b, p = c // 2, c % 2
        yc = np.asarray(results[c]["y"])
        for j in range(NQT):
            t = 2 * j + p
            out[b, t * 128:(t + 1) * 128, :] = yc[j * 128:(j + 1) * 128, :]
    return out


def kernel(**inputs):
    in_maps = prep_in_maps(**inputs)
    nc = build_program()
    res = run_bass_kernel_spmd(nc, in_maps, core_ids=list(range(NCORES)))
    return assemble_out(res.results)
